# version 04
# baseline (speedup 1.0000x reference)
import math
import contextlib
import numpy as np
import ml_dtypes
import concourse.bass as bass
import concourse.mybir as mybir
from concourse.bass_utils import run_bass_kernel_spmd

F32 = mybir.dt.float32
BF16 = mybir.dt.bfloat16
AF = mybir.ActivationFunctionType
ALU = mybir.AluOpType
AX = mybir.AxisListType

BDIM = 4
T = 4096
D = 2048
TILE = 512
NTILE = T // TILE
CH = 128
EPS = 1e-5
CONSTS = [1.0, EPS, -10.0, 0.5, -1.0e30, 0.0, 1.0e6, -1.0]
BIGNEG = -1e30
ENGS = ("tensor", "vector", "scalar", "gpsimd", "sync")
NKC = D // 128


class Sem:
    __slots__ = ("h", "n", "name")

    def __init__(self, h, name):
        self.h = h
        self.n = 0
        self.name = name


class Ctx:
    def __init__(self, nc, es):
        self.nc = nc
        self.es = es
        self.es_outer = es
        self.streams = {e: [] for e in ENGS}
        self.waited = {e: {} for e in ENGS}
        self._semi = 0
        self.prefix = ""
        self.all_sems = []

    def sem(self, name=None):
        name = f"{name or 'sem'}_{self._semi}"
        self._semi += 1
        h = self.es_outer.enter_context(self.nc.semaphore(name))
        s = Sem(h, name)
        self.all_sems.append(s)
        return s

    @contextlib.contextmanager
    def scope(self, prefix):
        old_pfx, old_es = self.prefix, self.es
        self.prefix = prefix
        es2 = contextlib.ExitStack()
        self.es = es2
        try:
            with es2:
                yield
        finally:
            self.es = old_es
            self.prefix = old_pfx

    def barrier_all(self):
        waits = [(s, s.n) for s in self.all_sems if s.n > 0]
        for eng in ENGS:
            self.wait(eng, waits)

    def sbuf(self, name, shape, dtype):
        return self.es.enter_context(
            self.nc.sbuf_tensor(self.prefix + name, list(shape), dtype))

    def psum(self, name, shape, dtype=F32):
        return self.es.enter_context(self.nc.psum_tensor(name, list(shape), dtype))

    def emit(self, eng, fn, waits=(), inc=None, inc_n=1):
        wl = []
        wc = self.waited[eng]
        for s, v in waits:
            if v > 0 and wc.get(s.name, -1) < v:
                wl.append((s.h, v))
                wc[s.name] = v
        if inc is not None:
            inc.n += inc_n
            hn = (inc.h, inc_n)
        else:
            hn = None
        wl = tuple(wl)

        def run(e, _wl=wl, _fn=fn, _hn=hn):
            for (h, v) in _wl:
                e.wait_ge(h, v)
            ins = _fn(e)
            if _hn is not None:
                ins.then_inc(_hn[0], _hn[1])
        self.streams[eng].append(run)
        return inc.n if inc is not None else None

    def wait(self, eng, waits):
        wl = []
        wc = self.waited[eng]
        for s, v in waits:
            if v > 0 and wc.get(s.name, -1) < v:
                wl.append((s.h, v))
                wc[s.name] = v
        if not wl:
            return

        def run(e, _wl=tuple(wl)):
            for (h, v) in _wl:
                e.wait_ge(h, v)
        self.streams[eng].append(run)


def finish_build(nc, ctx):
    with nc.Block() as block:
        for eng in ENGS:
            fns = ctx.streams[eng]
            if not fns:
                continue
            dec = getattr(block, eng)

            def body(e, _fns=fns):
                for f in _fns:
                    f(e)
            dec(body)
    return nc


def make_cpool_host():
    a = np.zeros((128, max(4, len(CONSTS))), np.float32)
    for i, v in enumerate(CONSTS):
        a[:, i] = v
    return a


def ogrid(O, maxg=5):
    ocs = []
    o = 0
    while o < O:
        w = min(128, O - o)
        ocs.append((o, w))
        o += w
    return [ocs[i:i + maxg] for i in range(0, len(ocs), maxg)]


def tile_w(W, grps):
    W = np.asarray(W, np.float32)
    F, O = W.shape
    nkf = F // 128
    Wr = W.reshape(nkf, 128, O)
    blocks = []
    for grp in grps:
        o0 = grp[0][0]
        ow = sum(w for _, w in grp)
        blocks.append(Wr[:, :, o0:o0 + ow].transpose(1, 0, 2).reshape(128, nkf * ow))
    return np.ascontiguousarray(np.concatenate(blocks, axis=1))


def host_consts():
    c_idf = np.eye(128, dtype=np.float32)
    c_oh = np.zeros((32, 32 * 128), np.float32)
    for h in range(32):
        c_oh[h, h * 128:(h + 1) * 128] = 1.0
    c_m1 = np.zeros((128, 128), np.float32)
    for p in range(128):
        c_m1[p, p:] = 1.0
    c_m2s = np.zeros((128, 128), np.float32)
    for i in range(127):
        c_m2s[i + 1, i] = BIGNEG
    return c_idf, c_oh, c_m1, c_m2s


class Common:

    def __init__(self, ctx, cpool_dram):
        sbf = ctx.sbuf
        self.s_cpool = sbf("s_cpool", [128, max(4, len(CONSTS))], F32)
        for i, v in enumerate(CONSTS):
            ctx.nc.const_aps.register(F32, float(v), self.s_cpool[:, i:i + 1])
        self.s_ones_f = sbf("s_ones_f", [1, 128], F32)
        self.s_ones_b = sbf("s_ones_b", [1, 128], BF16)
        self.s_onescol_f = sbf("s_onescol_f", [128, 1], F32)
        self.s_onescol_b = sbf("s_onescol_b", [128, 1], BF16)
        self.ds = ctx.sem("ds")
        self.dg = ctx.sem("dg")
        self.dx = ctx.sem("dx")
        self.dw = [ctx.sem("dw0"), ctx.sem("dw1")]
        self.pe = ctx.sem("pe")
        self.ve = ctx.sem("ve")
        self.se = ctx.sem("se")
        self.so = ctx.sem("so")
        E = ctx.emit
        E("sync", lambda e: e.dma_start(out=self.s_cpool[:], in_=cpool_dram),
          inc=self.ds, inc_n=16)
        E("vector", lambda e: e.memset(self.s_ones_f[:], 1.0), inc=self.ve)
        E("vector", lambda e: e.memset(self.s_ones_b[:], 1.0), inc=self.ve)
        E("vector", lambda e: e.memset(self.s_onescol_f[:], 1.0), inc=self.ve)
        E("vector", lambda e: e.memset(self.s_onescol_b[:], 1.0), inc=self.ve)
        self.ve_init = self.ve.n
        self.ds_cpool = self.ds.n
        ctx.wait("scalar", [(self.ds, self.ds_cpool)])


def emit_ln(ctx, cm, x_tiles, nkc, ln_dst, pa, pb, s_sq2, s_rows, s_abc, s_bbc,
            x_ready, bank_free):
    E = ctx.emit
    pe, ve, se = cm.pe, cm.ve, cm.se
    ntok = TILE
    guard = [(pe, pe.n), (ve, ve.n)]
    se_sq = [0, 0]
    pe_mm = [0, 0]
    for kc in range(nkc):
        sq = s_sq2[kc % 2]
        se_sq[kc % 2] = E("scalar", lambda e, _x=x_tiles(kc), _q=sq: e.activation(
            _q[:], _x, AF.Square),
            waits=[x_ready, (pe, pe_mm[kc % 2])] + (guard if kc < 2 else []),
            inc=se)
        E("tensor", lambda e, _x=x_tiles(kc), _f=(kc == 0), _l=(kc == nkc - 1): e.matmul(
            pa[0:1, :], cm.s_onescol_b[:], _x, start=_f, stop=_l),
            waits=[x_ready, (ve, cm.ve_init)] + list(bank_free), inc=pe)
        pe_mm[kc % 2] = E("tensor", lambda e, _q=sq, _f=(kc == 0), _l=(kc == nkc - 1): e.matmul(
            pb[0:1, :], cm.s_onescol_f[:], _q[:], start=_f, stop=_l),
            waits=[(se, se_sq[kc % 2])], inc=pe)
    pe_stats = pe.n
    r_mean = s_rows[0:1, 0:ntok]
    r_msq = s_rows[0:1, ntok:2 * ntok]
    r_var = s_rows[0:1, 2 * ntok:3 * ntok]
    r_a = s_rows[0:1, 3 * ntok:4 * ntok]
    r_b = s_rows[0:1, 4 * ntok:5 * ntok]
    inv_n = 1.0 / (nkc * 128)
    se_m = E("scalar", lambda e: e.activation(r_mean, pa[0:1, :], AF.Copy, scale=inv_n),
             waits=[(pe, pe_stats)] + guard, inc=se)
    se_mq = E("scalar", lambda e: e.activation(r_msq, r_mean, AF.Square), inc=se)
    ve_v = E("vector", lambda e: e.scalar_tensor_tensor(
        r_var, pb[0:1, :], inv_n, r_msq, ALU.mult, ALU.subtract),
        waits=[(pe, pe_stats), (se, se_mq)], inc=ve)
    E("scalar", lambda e: e.activation(r_a, r_var, AF.Ln, bias=EPS),
      waits=[(ve, ve_v)], inc=se)
    se_a = E("scalar", lambda e: e.activation(r_a, r_a, AF.Exp, scale=-0.5), inc=se)
    E("vector", lambda e: e.tensor_tensor(r_b, r_mean, r_a, ALU.mult),
      waits=[(se, se_a)], inc=ve)
    ve_b = E("vector", lambda e: e.tensor_scalar_mul(r_b, r_b, -1.0), inc=ve)
    pe_ba = E("tensor", lambda e: e.matmul(pa[:, :], cm.s_ones_f[:], r_a,
                                           start=True, stop=True),
              waits=[(se, se_a), (ve, ve_v)], inc=pe)
    pe_bb = E("tensor", lambda e: e.matmul(pb[:, :], cm.s_ones_f[:], r_b,
                                           start=True, stop=True),
              waits=[(ve, ve_b)], inc=pe)
    se_ca = E("scalar", lambda e: e.activation(s_abc[:], pa[:, :], AF.Copy),
              waits=[(pe, pe_ba)], inc=se)
    se_cb = E("scalar", lambda e: e.activation(s_bbc[:], pb[:, :], AF.Copy),
              waits=[(pe, pe_bb)], inc=se)
    ve_last = 0
    for kc in range(nkc):
        sq = s_sq2[kc % 2]
        E("vector", lambda e, _x=x_tiles(kc), _q=sq: e.tensor_tensor(
            _q[:], _x, s_abc[:], ALU.mult),
            waits=[(se, se_ca)], inc=ve)
        ve_last = E("vector", lambda e, _d=ln_dst(kc), _q=sq: e.tensor_tensor(
            _d, _q[:], s_bbc[:], ALU.add),
            waits=[(se, se_cb)], inc=ve)
    return ve_last, pe.n


def emit_proj(ctx, cm, w_dram, Ocols, F, rhs_tiles, rhs_ready, evac, s_wbufs,
              banks, wbuf_state, maxg=None, init_free=()):
    E = ctx.emit
    pe = cm.pe
    nkf = F // 128
    grps = ogrid(Ocols, maxg or len(banks))
    bank_free = {i: list(init_free) for i in range(len(banks))}
    for gi, grp in enumerate(grps):
        par = gi % 2
        buf = s_wbufs[par]
        o0 = grp[0][0]
        ow = sum(w for _, w in grp)
        win = w_dram[:, nkf * o0:nkf * (o0 + ow)]
        dwsem = cm.dw[par]
        dg_w = E("gpsimd", lambda e, _w=win, _b=buf, _ow=ow, _nk=nkf: e.dma_start(
            out=_b[:, 0:_nk * _ow], in_=_w),
            waits=[wbuf_state[par]], inc=dwsem, inc_n=16)
        for kc in range(nkf):
            for j, (oc, w) in enumerate(grp):
                joff = oc - o0
                E("tensor", lambda e, _b=buf, _kc=kc, _j=j, _joff=joff, _w=w, _ow=ow,
                  _r=rhs_tiles(kc), _f=(kc == 0), _l=(kc == nkf - 1): e.matmul(
                    banks[_j][0:_w, :], _b[:, _kc * _ow + _joff:_kc * _ow + _joff + _w],
                    _r, start=_f, stop=_l),
                    waits=[(dwsem, dg_w), rhs_ready] + (bank_free[j] if kc == 0 else []),
                    inc=(pe if (kc == nkf - 1) else None))
        pe_grp = pe.n
        for j, (oc, w) in enumerate(grp):
            free_mark = evac(j, oc, w, banks[j], pe_grp)
            bank_free[j] = [free_mark]
        wbuf_state[par] = (pe, pe_grp)
    return


def emit_projmt(ctx, cm, w_dram, F, grps, rhs, rhs_ready, evac_grp, s_wbufs,
                wbuf_state, bank_of, ntt, init_free=(), after_load=None,
                stream=None):
    E = ctx.emit
    pe = cm.pe
    nkf = F // 128
    bank_free = {}
    pe_entry = pe.n
    for gi, grp in enumerate(grps):
        par = gi % 2
        buf = s_wbufs[par]
        o0 = grp[0][0]
        ow = sum(w for _, w in grp)
        win = w_dram[:, nkf * o0:nkf * (o0 + ow)]
        dwsem = cm.dw[par]
        dg_w = E("gpsimd", lambda e, _w=win, _b=buf, _ow=ow, _nk=nkf: e.dma_start(
            out=_b[:, 0:_nk * _ow], in_=_w),
            waits=[wbuf_state[par]], inc=dwsem, inc_n=16)
        wready = (dwsem, dg_w)
        if after_load is not None:
            wready = after_load(buf, grp, gi, ow, nkf, wready)
        pe_kc = {}
        if stream is not None:
            stream.reset(war_floor=[(pe, max(wbuf_state[0][1], wbuf_state[1][1],
                                             pe_entry))])
        for kc in range(nkf):
            if stream is not None:
                rready = stream.load(kc, [(pe, pe_kc.get(kc - 2, 0))])
            for j, (oc, w) in enumerate(grp):
                joff = oc - o0
                for tt in range(ntt):
                    if stream is not None:
                        rap, rw = stream.get(kc, tt), rready
                    else:
                        rap, rw = rhs(kc, tt), rhs_ready
                    bk = bank_of(j, tt)
                    last = (tt == ntt - 1 and j == len(grp) - 1)
                    fw = []
                    if kc == 0:
                        fw = [bank_free[(j, tt)]] if (j, tt) in bank_free \
                            else list(init_free)
                    E("tensor", lambda e, _b=buf, _kc=kc, _joff=joff, _w=w,
                      _ow=ow, _r=rap, _bk=bk, _f=(kc == 0), _l=(kc == nkf - 1):
                      e.matmul(_bk[0:_w, :],
                               _b[:, _kc * _ow + _joff:_kc * _ow + _joff + _w],
                               _r, start=_f, stop=_l),
                      waits=[wready, rw] + fw,
                      inc=(pe if last else None))
            pe_kc[kc] = pe.n
        pe_grp = pe.n
        marks = evac_grp(grp, gi, pe_grp)
        bank_free.update(marks)
        wbuf_state[par] = (pe, pe_grp)
    return


class RhsStream:

    def __init__(self, ctx, cm, name, src_kc, bufs, sems, cast):
        self.ctx, self.cm = ctx, cm
        self.src_kc = src_kc
        self.bufs = bufs
        self.sems = sems
        self.cast = cast
        self.loaded = {}
        self.war_floor = []

    def load(self, kc, war_waits):
        if kc in self.loaded:
            return self.loaded[kc]
        par = kc % 2
        buf = self.bufs[par]
        sem = self.sems[par]
        eng = "gpsimd" if self.cast else "sync"
        v = self.ctx.emit(eng, lambda e, _b=buf, _s=self.src_kc(kc): e.dma_start(
            out=_b[:], in_=_s), waits=list(war_waits) + list(self.war_floor),
            inc=sem, inc_n=16)
        self.loaded[kc] = (sem, v)
        return (sem, v)

    def reset(self, war_floor=()):
        self.loaded = {}
        self.war_floor = list(war_floor)

    def get(self, kc, tt):
        return self.bufs[kc % 2][:, tt * 512:(tt + 1) * 512]



M_INPUTS = ["xr", "w_proj", "w_z", "convw", "convb", "dtb", "alog", "dvec",
            "c_idf", "c_oh", "c_m1", "c_m2s"]


def dram_io(nc, names, shapes, pfx=""):
    io = {}
    for n in names:
        shape, dt, kind = shapes[n]
        io[n] = nc.dram_tensor(pfx + n, list(shape), dt, kind=kind).ap()
    return io


M_SHAPES = {
    "xr": ([D, T], F32, "ExternalInput"),
    "w_proj": ([128, 16 * 3104], F32, "ExternalInput"),
    "w_z": ([128, 16 * 2048], F32, "ExternalInput"),
    "convw": ([128, 24 * 4], F32, "ExternalInput"),
    "convb": ([128, 24], F32, "ExternalInput"),
    "dtb": ([32, 1], F32, "ExternalInput"),
    "alog": ([32, 1], F32, "ExternalInput"),
    "dvec": ([1, 32], F32, "ExternalInput"),
    "c_idf": ([128, 128], F32, "ExternalInput"),
    "c_oh": ([32, 32 * 128], F32, "ExternalInput"),
    "c_m1": ([128, 128], F32, "ExternalInput"),
    "c_m2s": ([128, 128], F32, "ExternalInput"),
    "g_out": ([D, T], BF16, "ExternalOutput"),
}


def build_phase_m(ntiles=NTILE, stage="full"):
    nc = bass.Bass("TRN2", target_bir_lowering=False, detect_race_conditions=False)
    es = contextlib.ExitStack()
    ctx = Ctx(nc, es)
    with es:
        io = dram_io(nc, M_INPUTS + ["g_out"], M_SHAPES)
        cpool = nc.dram_tensor("cpool", [128, max(4, len(CONSTS))], F32,
                               kind="ExternalInput").ap()
        cm = Common(ctx, cpool)
        ps = [ctx.psum(f"ps{i}", [128, 512], F32) for i in range(7)]
        psb = ctx.psum("psb", [128, 1024], BF16)
        _emit_phase_m(ctx, cm, ps, psb, io, ntiles, stage)
        finish_build(nc, ctx)
    return nc


def _emit_phase_m(ctx, cm, ps, psb, io, ntiles=NTILE, stage="full",
                  g_store=None):
    nc = ctx.nc
    xr = io["xr"]
    w_proj = io["w_proj"]
    w_z = io["w_z"]
    convw = io["convw"]
    convb = io["convb"]
    dtb = io["dtb"]
    alog = io["alog"]
    dvec = io["dvec"]
    c_idf = io["c_idf"]
    c_oh = io["c_oh"]
    c_m1 = io["c_m1"]
    c_m2s = io["c_m2s"]
    g_out = io.get("g_out")
    sbf = ctx.sbuf
    s_idf = sbf("s_idf", [128, 128], F32)
    s_idb = sbf("s_idb", [128, 128], BF16)
    s_ohf = sbf("s_ohf", [32, 32 * 128], F32)
    s_ohb = sbf("s_ohb", [32, 32 * 128], BF16)
    s_m1 = sbf("s_m1", [128, 128], BF16)
    s_m2s = sbf("s_m2s", [128, 128], BF16)
    s_convw = sbf("s_convw", [128, 24 * 4], F32)
    s_convb = sbf("s_convb", [128, 24], F32)
    s_dtb = sbf("s_dtb", [32, 1], F32)
    s_acol = sbf("s_acol", [32, 1], F32)
    s_dvec = sbf("s_dvec", [1, 32], F32)
    s_dbc = sbf("s_dbc", [128, 32], F32)
    s_xt = sbf("s_xt", [128, NKC * TILE], BF16)
    s_ln = sbf("s_ln", [128, NKC * TILE], BF16)
    s_xbcp = sbf("s_xbcp", [128, 24 * (TILE + 3)], BF16)
    s_xbco = sbf("s_xbco", [128, 24 * TILE], BF16)
    s_g = sbf("s_g", [128, 16 * TILE], BF16)
    s_state = sbf("s_state", [128, 32 * 64], BF16)
    s_w = [sbf(f"s_w{i}", [128, 16 * 640], BF16) for i in range(2)]
    s_sq2 = [sbf(f"s_sq{i}", [128, TILE], F32) for i in range(2)]
    s_rows = sbf("s_rows", [1, TILE * 5], F32)
    s_abc = sbf("s_abc", [128, TILE], F32)
    s_bbc = sbf("s_bbc", [128, TILE], F32)
    s_dtraw = sbf("s_dtraw", [32, TILE], F32)
    s_dt = sbf("s_dt", [32, TILE], F32)
    s_dtA = sbf("s_dtA", [32, TILE], F32)
    s_sp1 = sbf("s_sp1", [32, TILE], F32)
    s_scan1 = sbf("s_scan1", [32, 128], F32)
    s_L = sbf("s_L", [32, 128], F32)
    s_erow = sbf("s_erow", [32, 128], BF16)
    s_LtN = sbf("s_LtN", [128, 32], F32)
    s_dttm = sbf("s_dttm", [128, 32], F32)
    s_xtm = sbf("s_xtm", [128, 16 * 128], BF16)
    s_bt = sbf("s_bt", [128, 4 * 128], BF16)
    s_T1 = [sbf(f"s_T1_{i}", [128, 128], F32) for i in range(4)]
    s_mt = [sbf(f"s_mt_{i}", [128, 128], BF16) for i in range(4)]
    s_m2 = [sbf(f"s_m2_{i}", [128, 128], BF16) for i in range(4)]
    s_btsc = [sbf(f"s_btsc_{i}", [128, 128], BF16) for i in range(4)]
    s_eLQ = sbf("s_eLQ", [128, 4], F32)

    dg, ds, pe, ve, se, so = cm.dg, cm.ds, cm.pe, cm.ve, cm.se, cm.so
    dx = cm.dx
    E = ctx.emit

    for dst, src in ((s_idf, c_idf), (s_ohf, c_oh), (s_convw, convw),
                     (s_convb, convb), (s_dtb, dtb), (s_acol, alog),
                     (s_dvec, dvec)):
        E("sync", lambda e, _d=dst, _s=src: e.dma_start(out=_d[:], in_=_s),
          inc=ds, inc_n=16)
    for dst, src in ((s_idb, c_idf), (s_ohb, c_oh), (s_m1, c_m1), (s_m2s, c_m2s)):
        E("gpsimd", lambda e, _d=dst, _s=src: e.dma_start(out=_d[:], in_=_s),
          inc=dg, inc_n=16)
    ds0 = ds.n
    dg0 = dg.n
    E("vector", lambda e: e.memset(s_scan1[:], 1.0), inc=ve)
    E("vector", lambda e: e.memset(s_state[:], 0.0), inc=ve)
    ve_state0 = ve.n
    E("vector", lambda e: e.memset(s_xbcp[:], 0.0), inc=ve)
    ve_init2 = ve.n
    E("scalar", lambda e: e.activation(s_acol[:], s_acol[:], AF.Exp),
      waits=[(ds, ds0)], inc=se)
    ve_acol = E("vector", lambda e: e.tensor_scalar_mul(s_acol[:], s_acol[:], -1.0),
                waits=[(se, se.n)], inc=ve)
    pe_d = E("tensor", lambda e: e.matmul(ps[6][:, 0:32], cm.s_ones_f[:],
                                          s_dvec[0:1, :], start=True, stop=True),
             waits=[(ds, ds0), (ve, cm.ve_init)], inc=pe)
    se_dbc = E("scalar", lambda e: e.activation(s_dbc[:], ps[6][:, 0:32], AF.Copy),
               waits=[(pe, pe_d)], inc=se)

    wbuf_state = {0: (pe, 0), 1: (pe, 0)}
    so_g = 0
    ve_tail_done = 0
    ve_state_prev = ve_state0
    ve_ln_prev = 0
    se_evac = {}

    for t in range(ntiles):
        t0 = t * TILE
        xin = xr.rearrange("(kc p) T -> p kc T", p=128)[:, :, t0:t0 + TILE]
        dg_xt = E("gpsimd", lambda e, _x=xin: e.dma_start(
            out=s_xt[:].rearrange("p (kc T) -> p kc T", kc=NKC), in_=_x),
            waits=[(ve, ve_ln_prev)], inc=cm.dx, inc_n=16)

        ve_ln, pe_lnb = emit_ln(
            ctx, cm, lambda kc: s_xt[:, kc * TILE:(kc + 1) * TILE], NKC,
            lambda kc: s_ln[:, kc * TILE:(kc + 1) * TILE],
            ps[5], ps[6], s_sq2, s_rows, s_abc, s_bbc,
            (cm.dx, dg_xt), [(se, se.n), (ve, ve.n), (pe, pe.n)])
        ve_ln_prev = ve_ln

        if stage == "ln":
            vv = 0
            for kc in range(16):
                vv = E("vector", lambda e, _kc=kc: e.tensor_copy(
                    s_g[:, _kc * TILE:(_kc + 1) * TILE],
                    s_ln[:, _kc * TILE:(_kc + 1) * TILE]),
                    waits=[(ve, ve_ln), (so, so_g)], inc=ve)
            so_g = E("sync", lambda e, _t0=t0: e.dma_start(
                out=g_out.rearrange("(kc p) T -> p kc T", p=128)[:, :, _t0:_t0 + TILE],
                in_=s_g[:].rearrange("p (kc T) -> p kc T", kc=16)),
                waits=[(ve, vv)], inc=so, inc_n=16)
            continue

        def evac_m(j, oc, w, bank, pe_val, _t=t):
            if oc < 24 * 128:
                cch = oc // 128
                dst = s_xbcp[:, cch * (TILE + 3) + 3: cch * (TILE + 3) + 3 + TILE]
                sev = E("scalar", lambda e, _d=dst, _b=bank: e.activation(
                    _d, _b[0:128, :], AF.Copy),
                    waits=[(pe, pe_val), (ve, ve_tail_done)], inc=se)
                se_evac[cch] = sev
            else:
                sev = E("scalar", lambda e, _b=bank: e.activation(
                    s_dtraw[:], _b[0:32, :], AF.Copy),
                    waits=[(pe, pe_val)], inc=se)
                se_evac['dt'] = sev
            return (se, sev)

        emit_proj(ctx, cm, w_proj, 3104, D,
                  lambda kc: s_ln[:, kc * TILE:(kc + 1) * TILE],
                  (ve, ve_ln), evac_m, s_w, [ps[0], ps[1], ps[2], ps[3], ps[4]],
                  wbuf_state, init_free=[(se, se.n), (ve, ve.n)])

        se_conv = {}
        ve_tail = 0
        for cch in range(24):
            base = cch * (TILE + 3)
            acc = s_sq2[cch % 2]
            E("vector", lambda e, _b=base, _a=acc, _c=cch: e.tensor_scalar(
                _a[:], s_xbcp[:, _b:_b + TILE], s_convw[:, _c * 4:_c * 4 + 1],
                s_convb[:, _c:_c + 1], ALU.mult, ALU.add),
                waits=[(se, se_evac[cch]), (se, se_conv.get(cch - 2, 0)),
                       (ds, ds0)], inc=ve)
            for k in (1, 2, 3):
                E("vector", lambda e, _b=base, _a=acc, _c=cch, _k=k: e.scalar_tensor_tensor(
                    _a[:], s_xbcp[:, _b + _k:_b + _k + TILE],
                    s_convw[:, _c * 4 + _k:_c * 4 + _k + 1], _a[:],
                    ALU.mult, ALU.add), inc=ve)
            ve_acc = ve.n
            se_conv[cch] = E("scalar", lambda e, _a=acc, _c=cch: e.activation(
                s_xbco[:, _c * TILE:(_c + 1) * TILE], _a[:], AF.Silu),
                waits=[(ve, ve_acc)], inc=se)
            ve_tail = E("vector", lambda e, _b=base: e.tensor_copy(
                s_xbcp[:, _b:_b + 3], s_xbcp[:, _b + TILE:_b + TILE + 3]), inc=ve)
        ve_tail_done = ve_tail

        E("scalar", lambda e: e.activation(s_sp1[:], s_dtraw[:], AF.Abs,
                                           bias=s_dtb[:]),
          waits=[(se, se_evac['dt'])], inc=se)
        E("scalar", lambda e: e.activation(s_sp1[:], s_sp1[:], AF.Exp, scale=-1.0),
          inc=se)
        se_sp = E("scalar", lambda e: e.activation(s_sp1[:], s_sp1[:], AF.Ln,
                                                   bias=1.0), inc=se)
        E("vector", lambda e: e.tensor_scalar(s_dt[:], s_dtraw[:], s_dtb[:], 0.0,
                                              ALU.add, ALU.max),
          waits=[(se, se_evac['dt'])], inc=ve)
        E("vector", lambda e: e.tensor_tensor(s_dt[:], s_dt[:], s_sp1[:], ALU.add),
          waits=[(se, se_sp)], inc=ve)
        ve_dtA = E("vector", lambda e: e.tensor_scalar(s_dtA[:], s_dt[:], s_acol[:],
                                                       None, ALU.mult),
                   waits=[(ve, ve_acol)], inc=ve)

        if stage == "conv":
            vv = 0
            for kc in range(16):
                vv = E("vector", lambda e, _kc=kc: e.tensor_copy(
                    s_g[:, _kc * TILE:(_kc + 1) * TILE],
                    s_xbco[:, _kc * TILE:(_kc + 1) * TILE]),
                    waits=[(se, se_conv[kc]), (so, so_g)], inc=ve)
            so_g = E("sync", lambda e, _t0=t0: e.dma_start(
                out=g_out.rearrange("(kc p) T -> p kc T", p=128)[:, :, _t0:_t0 + TILE],
                in_=s_g[:].rearrange("p (kc T) -> p kc T", kc=16)),
                waits=[(ve, vv)], inc=so, inc_n=16)
            continue

        ve_state_this = ve_state_prev
        for c in range(4):
            ve_state_this = _scan_chunk(
                ctx, cm, c, s_xbco, s_dt, s_dtA, s_scan1, s_L, s_erow, s_LtN,
                s_dttm, s_xtm, s_bt, s_T1, s_mt, s_m2, s_btsc, s_eLQ, s_state,
                s_idf, s_idb, s_ohf, s_ohb, s_m1, s_m2s, s_dbc, s_g,
                ps, psb, ve_dtA, se_conv, ve_state_this, se_dbc,
                (so, so_g), (ds, ds0), (dg, dg0))
        ve_state_prev = ve_state_this

        if stage == "scan":
            so_g = E("sync", lambda e, _t0=t0: e.dma_start(
                out=g_out.rearrange("(kc p) T -> p kc T", p=128)[:, :, _t0:_t0 + TILE],
                in_=s_g[:].rearrange("p (kc T) -> p kc T", kc=16)),
                waits=[(ve, ve_state_this)], inc=so, inc_n=16)
            continue

        ve_y_done = ve_state_this
        ve_gate = {}

        def evac_gate(j, oc, w, bank, pe_val, _t=t):
            cch = oc // 128
            sil = s_sq2[cch % 2]
            sev = E("scalar", lambda e, _s=sil, _b=bank: e.activation(
                _s[:], _b[0:128, :], AF.Silu),
                waits=[(pe, pe_val), (ve, ve_gate.get(cch - 2, ve_y_done))], inc=se)
            vev = E("vector", lambda e, _s=sil, _c=cch: e.tensor_tensor(
                s_g[:, _c * TILE:(_c + 1) * TILE],
                s_g[:, _c * TILE:(_c + 1) * TILE], _s[:], ALU.mult),
                waits=[(se, sev), (ve, ve_y_done)], inc=ve)
            ve_gate[cch] = vev
            return (ve, vev)

        emit_proj(ctx, cm, w_z, 2048, D,
                  lambda kc: s_ln[:, kc * TILE:(kc + 1) * TILE],
                  (ve, ve_ln), evac_gate, s_w, [ps[0], ps[1], ps[2], ps[3]],
                  wbuf_state, maxg=4, init_free=[(se, se.n), (ve, ve.n)])
        ve_g_done = max(ve_gate.values())

        gdst = (g_store(t) if g_store is not None else
                g_out.rearrange("(kc p) T -> p kc T", p=128)[:, :, t0:t0 + TILE])
        so_g = E("sync", lambda e, _gd=gdst: e.dma_start(
            out=_gd, in_=s_g[:].rearrange("p (kc T) -> p kc T", kc=16)),
            waits=[(ve, ve_g_done)], inc=so, inc_n=16)

    ctx.wait("sync", [(so, so.n)])


def _scan_chunk(ctx, cm, c, s_xbco, s_dt, s_dtA, s_scan1, s_L, s_erow, s_LtN,
                s_dttm, s_xtm, s_bt, s_T1, s_mt, s_m2, s_btsc, s_eLQ, s_state,
                s_idf, s_idb, s_ohf, s_ohb, s_m1, s_m2s, s_dbc, s_g,
                ps, psb, ve_dtA, se_conv, ve_state_prev, se_dbc,
                g_store_done, ds0w, dg0w):
    E = ctx.emit
    pe, ve, se = cm.pe, cm.ve, cm.se
    c0 = c * CH
    fence_se = se.n
    fence_ve = ve.n

    ve_L = E("vector", lambda e: e.tensor_tensor_scan(
        s_L[:], s_scan1[:], s_dtA[:, c0:c0 + CH], 0.0, ALU.mult, ALU.add),
        waits=[(ve, ve_dtA)], inc=ve)
    se_er = E("scalar", lambda e: e.activation(s_erow[:], s_L[:], AF.Exp),
              waits=[(ve, ve_L)], inc=se)

    E("tensor", lambda e: e.transpose(ps[6][:, 0:32], s_L[:], s_idf[0:32, 0:32]),
      waits=[(ve, ve_L), (se, fence_se)] + [ds0w], inc=pe)
    pe_t = E("tensor", lambda e: e.transpose(ps[6][:, 32:64],
                                             s_dt[:, c0:c0 + CH], s_idf[0:32, 0:32]),
             inc=pe)
    E("scalar", lambda e: e.activation(s_LtN[:], ps[6][:, 0:32], AF.Copy,
                                       scale=-1.0),
      waits=[(pe, pe_t)], inc=se)
    se_lt = E("scalar", lambda e: e.activation(s_dttm[:], ps[6][:, 32:64], AF.Copy),
              inc=se)

    items = []
    for i in range(16):
        items.append((s_xbco[:, i * TILE + c0:i * TILE + c0 + CH],
                      s_xtm[:, i * CH:(i + 1) * CH], se_conv[i]))
    for grp in range(4):
        items.append((s_xbco[:, (16 + grp) * TILE + c0:(16 + grp) * TILE + c0 + CH],
                      s_bt[:, grp * CH:(grp + 1) * CH], se_conv[16 + grp]))
    ve_round = fence_ve
    ve_xtm = fence_ve
    for r0 in range(0, 20, 8):
        batch = items[r0:r0 + 8]
        pe_tr = 0
        for sl, (src, dst, sev) in enumerate(batch):
            pe_tr = E("tensor", lambda e, _s=src, _sl=sl: e.transpose(
                psb[:, _sl * CH:(_sl + 1) * CH], _s, s_idb[:]),
                waits=[(cm.se, sev), (ve, ve_round)] + [dg0w], inc=pe)
        for sl, (src, dst, sev) in enumerate(batch):
            ve_xtm = E("vector", lambda e, _d=dst, _sl=sl: e.tensor_copy(
                _d, psb[:, _sl * CH:(_sl + 1) * CH]),
                waits=[(pe, pe_tr)], inc=ve)
        ve_round = ve_xtm

    ve_state_last = ve_state_prev
    ve_upd_prev = fence_ve
    pe_G = 0
    ve_G_users = fence_ve
    seb_done = {0: fence_se, 1: fence_se}
    ve_e_done = fence_ve
    se_q_prev = fence_se
    ve_q_users = fence_ve
    ve_t1_users = [fence_ve] * 4
    for hh in range(8):
        grp = hh // 2
        bbank = ps[1] if hh % 2 == 0 else ps[6]
        if hh % 2 == 0:
            pe_G = E("tensor", lambda e, _g=grp, _c0=c0: e.matmul(
                ps[0][:, 0:128],
                s_xbco[:, (16 + _g) * TILE + _c0:(16 + _g) * TILE + _c0 + CH],
                s_xbco[:, (20 + _g) * TILE + _c0:(20 + _g) * TILE + _c0 + CH],
                start=True, stop=True),
                waits=[(cm.se, se_conv[16 + grp]), (cm.se, se_conv[20 + grp]),
                       (ve, ve_G_users)], inc=pe)
        bwait = seb_done[hh % 2] if hh % 2 == 0 else max(seb_done[1], se_lt)
        pe_b = 0
        for hl in range(4):
            h = hh * 4 + hl
            E("tensor", lambda e, _h=h, _hl=hl, _bk=bbank: e.matmul(
                _bk[:, _hl * 128:(_hl + 1) * 128], s_ohf[:, _h * 128:(_h + 1) * 128],
                s_L[:], start=True, stop=False),
                waits=[(ve, ve_L), (se, bwait)], inc=None)
            pe_b = E("tensor", lambda e, _hl=hl, _bk=bbank: e.matmul(
                _bk[:, _hl * 128:(_hl + 1) * 128], s_m1[:], s_m2s[:],
                start=False, stop=True), inc=pe)
        pe_e = 0
        for hl in range(4):
            h = hh * 4 + hl
            pe_e = E("tensor", lambda e, _h=h, _hl=hl: e.matmul(
                ps[2][:, _hl * 128:(_hl + 1) * 128],
                s_ohb[:, _h * 128:(_h + 1) * 128], s_erow[:],
                start=True, stop=True),
                waits=[(se, se_er), (se, se_q_prev), (ve, ve_e_done)], inc=pe)
        se_q = E("scalar", lambda e: e.activation(
            s_eLQ[:, 0:4].rearrange("p (h i) -> p h i", i=1),
            ps[2][:].rearrange("p (h i) -> p h i", h=4)[:, :, 127:128],
            AF.Copy), waits=[(pe, pe_e), (ve, ve_q_users)], inc=se)
        se_q_prev = se_q
        se_T1 = 0
        for hl in range(4):
            h = hh * 4 + hl
            se_T1 = E("scalar", lambda e, _hl=hl, _h=h, _bk=bbank: e.activation(
                s_T1[_hl][:], _bk[:, _hl * 128:(_hl + 1) * 128], AF.Exp,
                bias=s_LtN[:, _h:_h + 1]),
                waits=[(pe, pe_b), (se, se_lt), (ve, ve_t1_users[hl])], inc=se)
        seb_done[hh % 2] = se_T1
        ve_prep = 0
        for hl in range(4):
            h = hh * 4 + hl
            E("vector", lambda e, _hl=hl, _h=h: e.scalar_tensor_tensor(
                s_mt[_hl][:], s_T1[_hl][:], s_dttm[:, _h:_h + 1], ps[0][:, 0:128],
                ALU.mult, ALU.mult),
                waits=[(se, se_T1), (pe, pe_G), (ve, ve_xtm)], inc=ve)
            E("vector", lambda e, _hl=hl, _g=grp, _c0=c0: e.tensor_tensor(
                s_m2[_hl][:],
                s_xbco[:, (20 + _g) * TILE + _c0:(20 + _g) * TILE + _c0 + CH],
                ps[2][:, _hl * 128:(_hl + 1) * 128], ALU.mult),
                waits=[(pe, pe_e)], inc=ve)
            ve_prep = E("vector", lambda e, _hl=hl, _g=grp, _h=h: e.tensor_scalar(
                s_btsc[_hl][:], s_bt[:, _g * 128:(_g + 1) * 128],
                s_T1[_hl][:, 127:128], s_dttm[:, _h:_h + 1], ALU.mult, ALU.mult),
                inc=ve)
        ve_G_users = ve_prep
        ve_e_done = ve_prep
        ybank = ps[3 + (hh % 2)]
        pe_y = 0
        for hl in range(4):
            h = hh * 4 + hl
            ybase = (h % 2) * 64
            tp = (0, ybase)
            E("tensor", lambda e, _bk=ybank, _yb=ybase, _hl=hl,
              _x=(h // 2) * 128 + (h % 2) * 64, _tp=tp: e.matmul(
                _bk[_yb:_yb + 64, _hl * 128:(_hl + 1) * 128],
                s_xtm[:, _x:_x + 64], s_mt[_hl][:], start=True, stop=False,
                tile_position=_tp),
                waits=[(ve, ve_prep), (ve, ve_state_prev)], inc=None)
            pe_y = E("tensor", lambda e, _bk=ybank, _yb=ybase, _hl=hl, _h=h,
                     _tp=tp: e.matmul(
                _bk[_yb:_yb + 64, _hl * 128:(_hl + 1) * 128],
                s_state[:, _h * 64:(_h + 1) * 64], s_m2[_hl][:],
                start=False, stop=True, tile_position=_tp), inc=pe)
        for hl in range(4):
            h = hh * 4 + hl
            ybase = (h % 2) * 64
            gdst = s_g[ybase:ybase + 64,
                       (h // 2) * TILE + c0:(h // 2) * TILE + c0 + CH]
            xfm = s_xbco[ybase:ybase + 64,
                         (h // 2) * TILE + c0:(h // 2) * TILE + c0 + CH]
            E("vector", lambda e, _gd=gdst, _xf=xfm, _h=h, _bk=ybank, _yb=ybase,
              _hl=hl: e.scalar_tensor_tensor(
                _gd, _xf, s_dbc[_yb:_yb + 64, _h:_h + 1],
                _bk[_yb:_yb + 64, _hl * 128:(_hl + 1) * 128], ALU.mult, ALU.add),
                waits=[(pe, pe_y), (se, se_dbc)] + [g_store_done], inc=ve)
        pe_S = 0
        for hl in range(4):
            pe_S = E("tensor", lambda e, _hl=hl,
                     _x=((hh * 4 + hl) // 2) * 128 + ((hh * 4 + hl) % 2) * 64: e.matmul(
                ps[5][:, _hl * 64:(_hl + 1) * 64], s_btsc[_hl][:],
                s_xtm[:, _x:_x + 64], start=True, stop=True),
                waits=[(ve, ve_prep), (ve, ve_upd_prev)], inc=pe)
        for hl in range(4):
            h = hh * 4 + hl
            ve_state_last = E("vector", lambda e, _h=h, _hl=hl: e.scalar_tensor_tensor(
                s_state[:, _h * 64:(_h + 1) * 64], s_state[:, _h * 64:(_h + 1) * 64],
                s_eLQ[:, _hl:_hl + 1], ps[5][:, _hl * 64:(_hl + 1) * 64],
                ALU.mult, ALU.add),
                waits=[(pe, pe_S), (pe, pe_y), (se, se_q)], inc=ve)
        ve_upd_prev = ve_state_last
        ve_q_users = ve_state_last
        for hl in range(4):
            ve_t1_users[hl] = ve_prep
        ve_t1_users = [ve_prep] * 4
        seb_pe_guard = pe_y
    return ve_state_last



F0_INPUTS = ["g_in", "xr", "nw", "w_out", "gw0T", "w13", "w2T", "wsgu",
             "wsdT", "sgwT", "c_idf", "c_oh16"]
F0_SHAPES = {
    "g_in": ([4096, 2048], BF16, "ExternalInput"),
    "xr": ([D, 2048], F32, "ExternalInput"),
    "nw": ([128, 32], F32, "ExternalInput"),
    "w_out": ([128, 32 * 2048], F32, "ExternalInput"),
    "gw0T": ([128, 16 * 16], F32, "ExternalInput"),
    "w13": ([16, 128, 16 * 1024], F32, "ExternalInput"),
    "w2T": ([16, 128, 4 * 2048], F32, "ExternalInput"),
    "wsgu": ([128, 16 * 4096], F32, "ExternalInput"),
    "wsdT": ([128, 16 * 2048], F32, "ExternalInput"),
    "sgwT": ([128, 16], F32, "ExternalInput"),
    "c_idf": ([128, 128], F32, "ExternalInput"),
    "c_oh16": ([16, 16 * 128], F32, "ExternalInput"),
    "r_out": ([D, 2048], BF16, "ExternalOutput"),
}


def build_phase_m2f0(stage="full"):
    nc = bass.Bass("TRN2", target_bir_lowering=False, detect_race_conditions=False)
    es = contextlib.ExitStack()
    ctx = Ctx(nc, es)
    with es:
        io = dram_io(nc, F0_INPUTS + ["r_out"], F0_SHAPES)
        cpool = nc.dram_tensor("cpool", [128, max(4, len(CONSTS))], F32,
                               kind="ExternalInput").ap()
        cm = Common(ctx, cpool)
        ps = [ctx.psum(f"ps{i}", [128, 512], F32) for i in range(7)]
        _emit_phase_m2f0(ctx, cm, ps, io, stage)
        finish_build(nc, ctx)
    return nc


def _emit_phase_m2f0(ctx, cm, ps, io, stage="full", xr_bf16=False,
                     rs_mode=False):
    nc = ctx.nc
    NT = 2048
    g_in = io.get("g_in")
    xr = io["xr"]
    nw = io["nw"]
    w_out = io.get("w_out")
    gw0T = io["gw0T"]
    w13 = io["w13"]
    w2T = io["w2T"]
    wsgu = io["wsgu"]
    wsdT = io["wsdT"]
    sgwT = io["sgwT"]
    c_idf = io["c_idf"]
    c_oh16 = io["c_oh16"]
    r_out = io["r_out"]
    sbf = ctx.sbuf
    s_rt = sbf("s_rt", [128, 16 * NT], BF16)
    s_lnf = sbf("s_lnf", [128, 16 * NT], BF16)
    s_hmid = sbf("s_hmid", [128, 4 * NT], BF16)
    s_sh = s_hmid
    s_w = [sbf(f"s_w{i}", [128, 32 * 128], BF16) for i in range(2)]
    s_gb = [sbf(f"s_gb{i}", [128, NT], BF16) for i in range(2)]
    s_idf = sbf("s_idf", [128, 128], F32)
    s_oh16 = sbf("s_oh16", [16, 16 * 128], BF16)
    s_gw0 = sbf("s_gw0", [128, 16 * 16], BF16)
    s_sgw = sbf("s_sgw", [128, 16], BF16)
    s_nw = sbf("s_nw", [128, 32], F32)
    s_rows = sbf("s_rows", [1, TILE * 5], F32)
    s_sq2 = [sbf(f"s_sq{i}", [128, TILE], F32) for i in range(2)]
    s_abc = sbf("s_abc", [128, TILE], F32)
    s_bbc = sbf("s_bbc", [128, TILE], F32)
    s_rstd = sbf("s_rstd", [128, 4 * TILE], BF16)
    s_ev = [sbf(f"s_ev{i}", [128, TILE], F32) for i in range(2)]
    s_sm = sbf("s_sm", [128, 8], F32)
    s_e16 = sbf("s_e16", [128, 16], F32)
    s_e16b = sbf("s_e16b", [128, 16], F32)
    s_cmb = sbf("s_cmb", [128, 16], F32)
    s_CT = sbf("s_CT", [16, NT], BF16)
    s_cbc = s_gb[0]
    s_sgbc = s_gb[1]

    dsr = [ctx.sem("dr0"), ctx.sem("dr1")]
    dg, ds, pe, ve, se, so = cm.dg, cm.ds, cm.pe, cm.ve, cm.se, cm.so
    E = ctx.emit

    for dst, src in ((s_idf, c_idf), (s_nw, nw)):
        E("sync", lambda e, _d=dst, _s=src: e.dma_start(out=_d[:], in_=_s),
          inc=ds, inc_n=16)
    for dst, src in ((s_gw0, gw0T), (s_sgw, sgwT), (s_oh16, c_oh16)):
        E("gpsimd", lambda e, _d=dst, _s=src: e.dma_start(out=_d[:], in_=_s),
          inc=dg, inc_n=16)
    ds0, dg0 = ds.n, dg.n
    _rteng = "sync" if xr_bf16 else "gpsimd"
    dg_rt = E(_rteng, lambda e: e.dma_start(
        out=s_rt[:].rearrange("p (kc T) -> p kc T", kc=16),
        in_=xr.rearrange("(kc p) T -> p kc T", p=128)), inc=cm.dx, inc_n=16)

    wbuf_state = {0: (pe, 0), 1: (pe, 0)}
    if rs_mode:
        ssq_rs = io["ssq_rs"]
        wop_rs = io["wop_rs"]
        drs = ctx.sem("drs")
        d1 = E("sync", lambda e: e.dma_start(out=s_rows[0:1, 0:2048],
                                             in_=ssq_rs), inc=drs, inc_n=16)
        se_r2 = 0
        for tt in range(4):
            r = s_rows[0:1, tt * TILE:(tt + 1) * TILE]
            E("scalar", lambda e, _r=r: e.activation(
                _r, _r, AF.Ln, scale=1.0 / 4096.0, bias=EPS),
                waits=[(drs, d1)], inc=se)
            se_r = E("scalar", lambda e, _r=r: e.activation(
                _r, _r, AF.Exp, scale=-0.5), inc=se)
            pe_bc = E("tensor", lambda e, _r=r: e.matmul(
                ps[4][:, :], cm.s_ones_f[:], _r, start=True, stop=True),
                waits=[(se, se_r), (se, se_r2), (ve, cm.ve_init)], inc=pe)
            se_r2 = E("scalar", lambda e, _tt=tt: e.activation(
                s_rstd[:, _tt * TILE:(_tt + 1) * TILE], ps[4][:, :], AF.Copy),
                waits=[(pe, pe_bc)], inc=se)
        se_rstd = se_r2
        wsrc = wop_rs.rearrange("(kc p) T -> p kc T", p=128)
        ve_wout = 0
        dwo = {0: 0, 1: 0}
        for kc in range(16):
            par = kc % 2
            dmw = E("sync", lambda e, _kc=kc, _p=par: e.dma_start(
                out=s_gb[_p][:], in_=wsrc[:, _kc, :]),
                waits=[(ve, dwo[par])], inc=dsr[par], inc_n=16)
            E("vector", lambda e, _p=par: e.tensor_tensor(
                s_gb[_p][:], s_gb[_p][:], s_rstd[:, :], ALU.mult),
                waits=[(dsr[par], dmw), (se, se_rstd)], inc=ve)
            ve_wout = E("vector", lambda e, _kc=kc, _p=par: e.tensor_tensor(
                s_rt[:, _kc * NT:(_kc + 1) * NT], s_rt[:, _kc * NT:(_kc + 1) * NT],
                s_gb[_p][:], ALU.add),
                waits=[(cm.dx, dg_rt)], inc=ve)
            dwo[par] = ve_wout
    else:
        gsrc = g_in.rearrange("(kc p) T -> p kc T", p=128)
        gstream = RhsStream(ctx, cm, "g", lambda kc: gsrc[:, kc, :], s_gb, dsr, False)

        se_sq = {0: 0, 1: 0}
        pe_mm = {0: 0, 1: 0}
        pe_sqmm = {0: 0, 1: 0}
        for kc in range(32):
            par = kc % 2
            rr = gstream.load(kc, [(pe, pe_mm[par])])
            for tt in range(4):
                sq = s_sq2[tt % 2]
                ssq = E("scalar", lambda e, _p=par, _tt=tt, _q=sq: e.activation(
                    _q[:], s_gb[_p][:, _tt * TILE:(_tt + 1) * TILE], AF.Square),
                    waits=[rr, (pe, pe_sqmm[tt % 2])], inc=se)
                pe_mm[par] = E("tensor", lambda e, _q=sq, _tt=tt, _f=(kc == 0),
                               _l=(kc == 31): e.matmul(
                    ps[_tt][0:1, :], cm.s_onescol_f[:], _q[:], start=_f, stop=_l),
                    waits=[(se, ssq), (ve, cm.ve_init)], inc=pe)
                pe_sqmm[tt % 2] = pe_mm[par]
        pe_ss = pe.n
        se_r = 0
        for tt in range(4):
            r = s_rows[0:1, tt * TILE:(tt + 1) * TILE]
            E("scalar", lambda e, _r=r, _tt=tt: e.activation(
                _r, ps[_tt][0:1, :], AF.Ln, scale=1.0 / 4096.0, bias=EPS),
                waits=[(pe, pe_ss)], inc=se)
            se_r = E("scalar", lambda e, _r=r: e.activation(_r, _r, AF.Exp, scale=-0.5),
                     inc=se)
        pe_bc = 0
        se_r2 = 0
        for tt in range(4):
            r = s_rows[0:1, tt * TILE:(tt + 1) * TILE]
            pe_bc = E("tensor", lambda e, _r=r: e.matmul(
                ps[4][:, :], cm.s_ones_f[:], _r, start=True, stop=True),
                waits=[(se, se_r), (se, se_r2)], inc=pe)
            se_r2 = E("scalar", lambda e, _tt=tt: e.activation(
                s_rstd[:, _tt * TILE:(_tt + 1) * TILE], ps[4][:, :], AF.Copy),
                waits=[(pe, pe_bc)], inc=se)
        se_rstd = se_r2
        gstream.reset()

        def wout_after_load(buf, grp, gi, ow, nkf, wready):
            vv = 0
            for kc in range(nkf):
                vv = E("vector", lambda e, _b=buf, _kc=kc, _ow=ow: e.tensor_scalar(
                    _b[:, _kc * _ow:(_kc + 1) * _ow], _b[:, _kc * _ow:(_kc + 1) * _ow],
                    s_nw[:, _kc:_kc + 1], None, ALU.mult),
                    waits=[wready, (ds, ds0)], inc=ve)
            return (ve, vv)

        ve_wout = 0

        def wout_evac(grp, gi, pe_val):
            nonlocal ve_wout
            marks = {}
            oc = grp[0][0]
            for tt in range(4):
                E("vector", lambda e, _tt=tt: e.tensor_tensor(
                    s_ev[_tt % 2][:], ps[_tt][:, :],
                    s_rstd[:, _tt * TILE:(_tt + 1) * TILE], ALU.mult),
                    waits=[(pe, pe_val), (se, se_rstd)], inc=ve)
                dst = s_rt[:, (oc // 128) * NT + tt * TILE:
                           (oc // 128) * NT + (tt + 1) * TILE]
                ve_wout = E("vector", lambda e, _d=dst, _tt=tt: e.tensor_tensor(
                    _d, _d, s_ev[_tt % 2][:], ALU.add),
                    waits=[(cm.dx, dg_rt)], inc=ve)
                marks[(0, tt)] = (ve, ve_wout)
            return marks

        emit_projmt(ctx, cm, w_out, 4096, [[(o, 128)] for o in range(0, 2048, 128)],
                    None, None, wout_evac, s_w, wbuf_state,
                    lambda j, tt: ps[tt], 4, init_free=[(se, se_rstd)],
                    after_load=wout_after_load, stream=gstream)

        if stage == "wout":
            so_v = E("sync", lambda e: e.dma_start(
                out=r_out.rearrange("(kc p) T -> p kc T", p=128),
                in_=s_rt[:].rearrange("p (kc T) -> p kc T", kc=16)),
                waits=[(ve, ve_wout)], inc=so, inc_n=16)
            ctx.wait("sync", [(so, so.n)])
            return

    ve_ln = 0
    for tt in range(4):
        ve_ln, _ = emit_ln(
            ctx, cm, lambda kc, _tt=tt: s_rt[:, kc * NT + _tt * TILE:
                                             kc * NT + (_tt + 1) * TILE], 16,
            lambda kc, _tt=tt: s_lnf[:, kc * NT + _tt * TILE:
                                     kc * NT + (_tt + 1) * TILE],
            ps[5], ps[6], s_sq2, s_rows, s_abc, s_bbc,
            (ve, ve_wout), [(se, se.n), (ve, ve.n), (pe, pe.n)])

    if stage == "ln":
        so_v = E("sync", lambda e: e.dma_start(
            out=r_out.rearrange("(kc p) T -> p kc T", p=128),
            in_=s_lnf[:].rearrange("p (kc T) -> p kc T", kc=16)),
            waits=[(ve, ve_ln)], inc=so, inc_n=16)
        ctx.wait("sync", [(so, so.n)])
        return

    se_ct = 0
    ve_p4_users = 0
    se_p4_users = 0
    for tt in range(4):
        for tsub in range(4):
            pe_rl = 0
            for kc in range(16):
                off = kc * NT + tt * TILE + tsub * 128
                pe_rl = E("tensor", lambda e, _o=off, _kc=kc, _f=(kc == 0),
                          _l=(kc == 15): e.matmul(
                    ps[4][:, 0:16], s_lnf[:, _o:_o + 128],
                    s_gw0[:, _kc * 16:(_kc + 1) * 16], start=_f, stop=_l),
                    waits=[(ve, ve_ln), (dg, dg0),
                           (ve, ve_p4_users), (se, se_p4_users)], inc=pe)
            def VB(fn, waits=()):
                vv = E("vector", fn, waits=waits, inc=ve)
                ctx.wait("vector", [(ve, vv)])
                return vv
            VB(lambda e: e.reduce_max(s_sm[:, 0:1], ps[4][:, 0:16], axis=AX.X),
               waits=[(pe, pe_rl)])
            ve_neg = VB(lambda e: e.tensor_scalar_mul(s_sm[:, 1:2], s_sm[:, 0:1],
                                                      -1.0))
            se_e = E("scalar", lambda e: e.activation(
                s_e16[:], ps[4][:, 0:16], AF.Exp, bias=s_sm[:, 1:2]),
                waits=[(ve, ve_neg)], inc=se)
            VB(lambda e: e.tensor_scalar(s_cmb[:], ps[4][:, 0:16], s_sm[:, 0:1],
                                         None, ALU.subtract))
            VB(lambda e: e.tensor_scalar(s_cmb[:], s_cmb[:],
                                         cm.s_cpool[:, 4:5], cm.s_cpool[:, 0:1],
                                         ALU.mult, ALU.min))
            VB(lambda e: e.tensor_scalar(s_cmb[:], s_cmb[:], cm.s_cpool[:, 7:8],
                                         None, ALU.add))
            VB(lambda e: e.scalar_tensor_tensor(s_e16b[:], s_cmb[:],
                                                cm.s_cpool[:, 6:7], ps[4][:, 0:16],
                                                ALU.mult, ALU.add))
            VB(lambda e: e.reduce_max(s_sm[:, 2:3], s_e16b[:], axis=AX.X))
            VB(lambda e: e.tensor_scalar(s_cmb[:], ps[4][:, 0:16], s_sm[:, 2:3],
                                         None, ALU.subtract))
            VB(lambda e: e.tensor_scalar(s_cmb[:], s_cmb[:],
                                         cm.s_cpool[:, 4:5], cm.s_cpool[:, 5:6],
                                         ALU.mult, ALU.max))
            VB(lambda e: e.tensor_scalar(s_cmb[:], s_cmb[:],
                                         cm.s_cpool[:, 0:1], cm.s_cpool[:, 7:8],
                                         ALU.min, ALU.add))
            VB(lambda e: e.tensor_scalar(s_cmb[:], s_cmb[:], cm.s_cpool[:, 7:8],
                                         None, ALU.mult))
            VB(lambda e: e.tensor_tensor(s_cmb[:], s_cmb[:], s_e16[:], ALU.mult),
               waits=[(se, se_e)])
            VB(lambda e: e.tensor_reduce(s_sm[:, 3:4], s_cmb[:], axis=AX.X,
                                         op=ALU.add))
            VB(lambda e: e.reciprocal(s_sm[:, 4:5], s_sm[:, 3:4]))
            ve_c = VB(lambda e: e.tensor_scalar(s_cmb[:], s_cmb[:], s_sm[:, 4:5],
                                                None, ALU.mult))
            ve_p4_users = ve_c
            se_p4_users = se_e
            pe_tc = E("tensor", lambda e: e.transpose(
                ps[5][0:16, 0:128], s_cmb[:], s_idf[:]),
                waits=[(ve, ve_c), (se, se_ct), (ds, ds0)], inc=pe)
            se_ct = E("scalar", lambda e, _o=tt * TILE + tsub * 128: e.activation(
                s_CT[0:16, _o:_o + 128], ps[5][0:16, 0:128], AF.Copy),
                waits=[(pe, pe_tc)], inc=se)

    if stage == "router":
        so_v = E("sync", lambda e: e.dma_start(
            out=r_out[0:16, :], in_=s_CT[0:16, :]),
            waits=[(se, se_ct)], inc=so, inc_n=16)
        ctx.wait("sync", [(so, so.n)])
        return

    ve_hmid_last = 0
    se_cbc_users = 0
    for ei in range(16):
        se_cbc = se_cbc_users
        pe_cb = 0
        for tt in range(4):
            pe_cb = E("tensor", lambda e, _e=ei, _tt=tt: e.matmul(
                ps[6][:, :], s_oh16[:, _e * 128:(_e + 1) * 128],
                s_CT[0:16, _tt * TILE:(_tt + 1) * TILE], start=True, stop=True),
                waits=[(se, se_ct), (ve, ve_hmid_last), (se, se_cbc)],
                inc=pe)
            se_cbc = E("scalar", lambda e, _tt=tt: e.activation(
                s_cbc[:, _tt * TILE:(_tt + 1) * TILE], ps[6][:, :], AF.Copy),
                waits=[(pe, pe_cb)], inc=se)
        se_cbc_users = se_cbc

        w13g = [[(j * 256, 128), (j * 256 + 128, 128)] for j in range(4)]
        for tth in range(2):
            def ev13(grp, gi, pe_val, _tth=tth, _e=ei):
                nonlocal ve_hmid_last
                marks = {}
                j = grp[0][0] // 256
                for tl in range(2):
                    tt = _tth * 2 + tl
                    sil = s_ev[tl]
                    sev = E("scalar", lambda e, _s=sil, _tl=tl: e.activation(
                        _s[:], ps[_tl * 2][:, :], AF.Silu),
                        waits=[(pe, pe_val)], inc=se)
                    E("vector", lambda e, _s=sil, _tl=tl: e.tensor_tensor(
                        _s[:], _s[:], ps[_tl * 2 + 1][:, :], ALU.mult),
                        waits=[(se, sev)], inc=ve)
                    dst = s_hmid[:, j * NT + tt * TILE: j * NT + (tt + 1) * TILE]
                    vh = E("vector", lambda e, _d=dst, _s=sil, _tt=tt: e.tensor_tensor(
                        _d, _s[:], s_cbc[:, _tt * TILE:(_tt + 1) * TILE], ALU.mult),
                        waits=[(se, se_cbc)], inc=ve)
                    marks[(0, tl)] = (ve, vh)
                    marks[(1, tl)] = (ve, vh)
                    ve_hmid_last = vh
                return marks

            emit_projmt(ctx, cm, w13[ei], D, w13g,
                        lambda kc, tl, _tth=tth: s_lnf[:, kc * NT + (_tth * 2 + tl)
                                                       * TILE: kc * NT +
                                                       (_tth * 2 + tl + 1) * TILE],
                        (ve, ve_ln), ev13, s_w, wbuf_state,
                        lambda j, tl: ps[tl * 2 + j], 2,
                        init_free=[(ve, ve.n), (se, se.n)])

        def ev2(grp, gi, pe_val, _e=ei):
            marks = {}
            oc = grp[0][0]
            for tt in range(4):
                dst = s_rt[:, (oc // 128) * NT + tt * TILE:
                           (oc // 128) * NT + (tt + 1) * TILE]
                vv = E("vector", lambda e, _d=dst, _tt=tt: e.tensor_tensor(
                    _d, _d, ps[_tt][:, :], ALU.add),
                    waits=[(pe, pe_val), (ve, ve_ln)], inc=ve)
                marks[(0, tt)] = (ve, vv)
            return marks

        emit_projmt(ctx, cm, w2T[ei], 512, [[(o, 128)] for o in range(0, 2048, 128)],
                    lambda kc, tt: s_hmid[:, kc * NT + tt * TILE:
                                          kc * NT + (tt + 1) * TILE],
                    (ve, ve_hmid_last), ev2, s_w, wbuf_state,
                    lambda j, tt: ps[tt], 4, init_free=[(ve, ve.n), (se, se.n)])

    if stage == "moe":
        so_v = E("sync", lambda e: e.dma_start(
            out=r_out.rearrange("(kc p) T -> p kc T", p=128),
            in_=s_rt[:].rearrange("p (kc T) -> p kc T", kc=16)),
            waits=[(ve, ve.n)], inc=so, inc_n=16)
        ctx.wait("sync", [(so, so.n)])
        return

    se_sg = se.n
    ve_sg_guard = ve.n
    for tt in range(4):
        pe_sg = 0
        for kc in range(16):
            pe_sg = E("tensor", lambda e, _kc=kc, _tt=tt, _f=(kc == 0),
                      _l=(kc == 15): e.matmul(
                ps[4][0:1, :], s_sgw[:, _kc:_kc + 1],
                s_lnf[:, _kc * NT + _tt * TILE:_kc * NT + (_tt + 1) * TILE],
                start=_f, stop=_l),
                waits=[(ve, ve_ln), (dg, dg0), (se, se_sg), (ve, ve_sg_guard)],
                inc=pe)
        r = s_rows[0:1, 0:TILE]
        E("scalar", lambda e, _r=r: e.activation(_r, ps[4][0:1, :], AF.Sigmoid),
          waits=[(pe, pe_sg)], inc=se)
        pe_sgb = E("tensor", lambda e, _r=r: e.matmul(
            ps[5][:, :], cm.s_ones_f[:], _r, start=True, stop=True),
            waits=[(se, se.n)], inc=pe)
        se_sg = E("scalar", lambda e, _tt=tt: e.activation(
            s_sgbc[:, _tt * TILE:(_tt + 1) * TILE], ps[5][:, :], AF.Copy),
            waits=[(pe, pe_sgb)], inc=se)

    for blk in range(8):
        sgug = [[(blk * 512 + j * 256, 128), (blk * 512 + j * 256 + 128, 128)]
                for j in range(2)]
        for tth in range(2):
            def evsh(grp, gi, pe_val, _tth=tth, _blk=blk):
                marks = {}
                j = (grp[0][0] - _blk * 512) // 256
                for tl in range(2):
                    tt = _tth * 2 + tl
                    sil = s_ev[tl]
                    sev = E("scalar", lambda e, _s=sil, _tl=tl: e.activation(
                        _s[:], ps[_tl * 2][:, :], AF.Silu),
                        waits=[(pe, pe_val)], inc=se)
                    dst = s_sh[:, j * (4 * TILE) + tt * TILE:
                               j * (4 * TILE) + (tt + 1) * TILE]
                    vh = E("vector", lambda e, _d=dst, _s=sil, _tl=tl: e.tensor_tensor(
                        _d, _s[:], ps[_tl * 2 + 1][:, :], ALU.mult),
                        waits=[(se, sev)], inc=ve)
                    marks[(0, tl)] = (ve, vh)
                    marks[(1, tl)] = (ve, vh)
                return marks

            emit_projmt(ctx, cm, wsgu, D, sgug,
                        lambda kc, tl, _tth=tth: s_lnf[:, kc * NT + (_tth * 2 + tl)
                                                       * TILE: kc * NT +
                                                       (_tth * 2 + tl + 1) * TILE],
                        (ve, ve_ln), evsh, s_w, wbuf_state,
                        lambda j, tl: ps[tl * 2 + j], 2,
                        init_free=[(ve, ve.n), (se, se.n)])

        ve_sh = ve.n

        def evsd(grp, gi, pe_val, _blk=blk):
            marks = {}
            oc = grp[0][0]
            for tt in range(4):
                E("vector", lambda e, _tt=tt: e.tensor_tensor(
                    s_ev[_tt % 2][:], ps[_tt][:, :],
                    s_sgbc[:, _tt * TILE:(_tt + 1) * TILE], ALU.mult),
                    waits=[(pe, pe_val), (se, se_sg)], inc=ve)
                dst = s_rt[:, (oc // 128) * NT + tt * TILE:
                           (oc // 128) * NT + (tt + 1) * TILE]
                vv = E("vector", lambda e, _d=dst, _tt=tt: e.tensor_tensor(
                    _d, _d, s_ev[_tt % 2][:], ALU.add), inc=ve)
                marks[(0, tt)] = (ve, vv)
            return marks

        emit_projmt(ctx, cm, wsdT[:, blk * 4096:(blk + 1) * 4096], 256,
                    [[(o, 128)] for o in range(0, 2048, 128)],
                    lambda kc, tt, _blk=blk: s_sh[:, kc * (4 * TILE) + tt * TILE:
                                                  kc * (4 * TILE) + (tt + 1) * TILE],
                    (ve, ve_sh), evsd, s_w, wbuf_state,
                    lambda j, tt: ps[tt], 4, init_free=[(ve, ve.n), (se, se.n)])

    so_v = E("sync", lambda e: e.dma_start(
        out=r_out.rearrange("(kc p) T -> p kc T", p=128),
        in_=s_rt[:].rearrange("p (kc T) -> p kc T", kc=16)),
        waits=[(ve, ve.n)], inc=so, inc_n=16)
    ctx.wait("sync", [(so, so.n)])


def host_prep_m2f0(inputs, core, g_full):
    b, hf = core // 2, core % 2
    c_idf, _, _, _ = host_consts()
    c_oh16 = np.zeros((16, 16 * 128), np.float32)
    for h in range(16):
        c_oh16[h, h * 128:(h + 1) * 128] = 1.0
    own_tok = np.concatenate([np.arange((2 * s + hf) * TILE, (2 * s + hf + 1) * TILE)
                              for s in range(4)])
    nw_ = np.ascontiguousarray(
        np.asarray(inputs["L0_attn_norm_w"], np.float32).reshape(32, 128).T)
    w1 = np.asarray(inputs["L0_ffn_w1"], np.float32)
    w3 = np.asarray(inputs["L0_ffn_w3"], np.float32)
    w2 = np.asarray(inputs["L0_ffn_w2"], np.float32)
    w13 = np.empty((16, 2048, 1024), np.float32)
    for j in range(4):
        w13[:, :, j * 256:j * 256 + 128] = w1[:, j * 128:(j + 1) * 128, :].transpose(0, 2, 1)
        w13[:, :, j * 256 + 128:j * 256 + 256] = w3[:, j * 128:(j + 1) * 128, :].transpose(0, 2, 1)
    oc16 = [[(o, 128)] for o in range(0, 2048, 128)]
    w13g = [[(j * 256, 128), (j * 256 + 128, 128)] for j in range(4)]
    w13 = np.stack([tile_w(w13[e], w13g) for e in range(16)])
    w2T = np.stack([tile_w(w2.transpose(0, 2, 1)[e], oc16) for e in range(16)])
    wsg = np.asarray(inputs["L0_ffn_ws_gate"], np.float32)
    wsu = np.asarray(inputs["L0_ffn_ws_up"], np.float32)
    wsgu = np.empty((2048, 4096), np.float32)
    for j in range(16):
        wsgu[:, j * 256:j * 256 + 128] = wsg[j * 128:(j + 1) * 128, :].T
        wsgu[:, j * 256 + 128:j * 256 + 256] = wsu[j * 128:(j + 1) * 128, :].T
    sgug = [[(blk * 512 + j * 256, 128), (blk * 512 + j * 256 + 128, 128)]
            for blk in range(8) for j in range(2)]
    wsgu = tile_w(wsgu, sgug)
    wsd = np.asarray(inputs["L0_ffn_ws_down"], np.float32).T
    wsdT = np.concatenate([tile_w(wsd[blk * 256:(blk + 1) * 256], oc16)
                           for blk in range(8)], axis=1)
    gw0 = np.asarray(inputs["L0_ffn_gate_w"], np.float32)
    gw0T = np.ascontiguousarray(
        gw0.T.reshape(16, 128, 16).transpose(1, 0, 2).reshape(128, 256))
    sgw = np.asarray(inputs["L0_ffn_shared_gate_w"], np.float32)
    sgwT = np.ascontiguousarray(sgw.reshape(16, 128).T)
    ret = {
        "nw": nw_,
        "gw0T": gw0T, "w13": w13, "w2T": w2T,
        "wsgu": wsgu, "wsdT": wsdT, "sgwT": sgwT,
        "c_idf": c_idf, "c_oh16": c_oh16, "cpool": make_cpool_host(),
    }
    if g_full is not None:
        x = np.asarray(inputs["L0_attn_hidden_states"][b])
        ret["xr"] = np.ascontiguousarray(x[own_tok].T, dtype=np.float32)
        ret["g_in"] = np.ascontiguousarray(g_full[b][:, own_tok])
        ret["w_out"] = tile_w(np.asarray(inputs["L0_attn_Wout"], np.float32),
                              oc16)
    return ret



P0_SHAPES = {
    "g_in": ([D, T], BF16, "ExternalInput"),
    "w_out": ([128, 16 * 2048], F32, "ExternalInput"),
    "nw": ([128, 16], F32, "ExternalInput"),
    "ssq_part": ([2, 2048], F32, "ExternalOutput"),
    "wop": ([4096, 2048], BF16, "ExternalOutput"),
}


def build_phase_m2p():
    nc = bass.Bass("TRN2", target_bir_lowering=False, detect_race_conditions=False)
    es = contextlib.ExitStack()
    ctx = Ctx(nc, es)
    with es:
        io = dram_io(nc, ["g_in", "w_out", "nw", "ssq_part", "wop"], P0_SHAPES)
        cpool = nc.dram_tensor("cpool", [128, max(4, len(CONSTS))], F32,
                               kind="ExternalInput").ap()
        cm = Common(ctx, cpool)
        ps = [ctx.psum(f"ps{i}", [128, 512], F32) for i in range(7)]
        _emit_m2_partials(ctx, cm, ps, io)
        finish_build(nc, ctx)
    return nc


def _emit_m2_partials(ctx, cm, ps, io, g_tile=None):
    g_in = io.get("g_in")
    w_out = io["w_out"]
    nw = io["nw"]
    ssq_part = io["ssq_part"]
    wop = io["wop"]
    sbf = ctx.sbuf
    s_wout = sbf("s_wout", [128, 16 * 2048], BF16)
    s_gt = sbf("s_gt", [128, 16 * TILE], BF16)
    s_wsb = sbf("s_wsb", [128, 16 * TILE], BF16)
    s_sq = [sbf(f"s_sq{i}", [128, TILE], F32) for i in range(2)]
    s_row = sbf("s_row", [1, TILE], F32)
    s_nw = sbf("s_nw", [128, 16], F32)
    dg, ds, pe, ve, se, so = cm.dg, cm.ds, cm.pe, cm.ve, cm.se, cm.so
    E = ctx.emit

    ds_nw = E("sync", lambda e: e.dma_start(out=s_nw[:], in_=nw),
              inc=ds, inc_n=16)
    dg_w = 0
    for half in range(2):
        dg_w = E("gpsimd", lambda e, _h=half: e.dma_start(
            out=s_wout[:, _h * 8 * 2048:(_h + 1) * 8 * 2048],
            in_=w_out[:, _h * 8 * 2048:(_h + 1) * 8 * 2048]), inc=dg, inc_n=16)
    ve_wf = 0
    for kc in range(16):
        ve_wf = E("vector", lambda e, _kc=kc: e.tensor_scalar(
            s_wout[:, _kc * 2048:(_kc + 1) * 2048],
            s_wout[:, _kc * 2048:(_kc + 1) * 2048],
            s_nw[:, _kc:_kc + 1], None, ALU.mult),
            waits=[(dg, dg_w), (ds, ds_nw)], inc=ve)

    gsrc = None if g_tile is not None else \
        g_in.rearrange("(kc p) T -> p kc T", p=128)
    dgt = ctx.sem("dgt")
    pe_gdone = 0
    se_ev_prev = 0
    se_r_prev = 0
    so_prev = 0
    for h in range(2):
        for tt in range(4):
            tg0 = h * 2048 + tt * TILE
            gin = (g_tile(h * 4 + tt) if g_tile is not None else
                   gsrc[:, :, tg0:tg0 + TILE])
            dm = E("sync", lambda e, _g=gin, _fl=(g_tile is not None): e.dma_start(
                out=(s_gt[:] if _fl else
                     s_gt[:].rearrange("p (kc T) -> p kc T", kc=16)),
                in_=_g), waits=[(pe, pe_gdone)], inc=dgt, inc_n=16)
            se_sq = {0: 0, 1: 0}
            pe_ss = 0
            for kc in range(16):
                par = kc % 2
                ssq = E("scalar", lambda e, _kc=kc, _p=par: e.activation(
                    s_sq[_p][:], s_gt[:, _kc * TILE:(_kc + 1) * TILE], AF.Square),
                    waits=[(dgt, dm), (pe, se_sq[par])], inc=se)
                pe_ss = E("tensor", lambda e, _p=par, _f=(kc == 0),
                          _l=(kc == 15): e.matmul(
                    ps[6][0:1, :], cm.s_onescol_f[:], s_sq[_p][:],
                    start=_f, stop=_l),
                    waits=[(se, ssq), (ve, cm.ve_init)] +
                          ([(se, se_r_prev)] if kc == 0 else []), inc=pe)
                se_sq[par] = pe_ss
            se_r = E("scalar", lambda e: e.activation(
                s_row[0:1, :], ps[6][0:1, :], AF.Copy),
                waits=[(pe, pe_ss), (so, so_prev)], inc=se)
            se_r_prev = se_r
            E("sync", lambda e, _h=h, _tt=tt: e.dma_start(
                out=ssq_part[_h:_h + 1, _tt * TILE:(_tt + 1) * TILE],
                in_=s_row[0:1, :]), waits=[(se, se_r)], inc=so, inc_n=16)
            se_ev = {0: se_ev_prev, 1: se_ev_prev}
            for oc in range(16):
                ob = oc % 2
                pe_o = 0
                for kc in range(16):
                    pe_o = E("tensor", lambda e, _kc=kc, _oc=oc, _ob=ob,
                             _f=(kc == 0), _l=(kc == 15): e.matmul(
                        ps[_ob][:, :],
                        s_wout[:, _kc * 2048 + _oc * 128:
                               _kc * 2048 + _oc * 128 + 128],
                        s_gt[:, _kc * TILE:(_kc + 1) * TILE],
                        start=_f, stop=_l),
                        waits=[(ve, ve_wf), (dgt, dm)] +
                              ([(se, se_ev[ob])] if kc == 0 else []),
                        inc=(pe if kc == 15 else None))
                se_ev[ob] = E("scalar", lambda e, _oc=oc, _ob=ob: e.activation(
                    s_wsb[:, _oc * TILE:(_oc + 1) * TILE], ps[_ob][:, :], AF.Copy),
                    waits=[(pe, pe_o), (so, so_prev)], inc=se)
            se_ev_prev = max(se_ev.values())
            pe_gdone = pe.n
            E("sync", lambda e, _h=h, _tt=tt: e.dma_start(
                out=wop.rearrange("(s o p) T -> p s o T", p=128, o=16)[
                    :, _h, :, _tt * TILE:(_tt + 1) * TILE],
                in_=s_wsb[:].rearrange("p (o T) -> p o T", o=16)),
                waits=[(se, se_ev_prev)], inc=so, inc_n=16)
            so_prev = so.n
    ctx.wait("sync", [(so, so.n)])


def host_prep_m2p(inputs, core):
    b, hf = core // 2, core % 2
    w_out = np.asarray(inputs["L0_attn_Wout"], np.float32)
    nw_ = np.asarray(inputs["L0_attn_norm_w"], np.float32).reshape(32, 128).T
    return {
        "w_out": tile_w(w_out[hf * 2048:(hf + 1) * 2048], [[(0, 2048)]]),
        "nw": np.ascontiguousarray(nw_[:, hf * 16:(hf + 1) * 16]),
    }



A1_INPUTS = ["wkvT", "wqT", "c_cc", "c_ss", "c_pr", "c_pr64", "c_idf",
             "c_t1c", "c_t2c"]
A1_SHAPES = {
    "xr": ([D, T], BF16, "ExternalInput"),
    "wkvT": ([128, 16 * 512], F32, "ExternalInput"),
    "wqT": ([128, 16 * 1024], F32, "ExternalInput"),
    "woT": ([128, 8 * 2048], F32, "ExternalInput"),
    "c_cc": ([128, T], F32, "ExternalInput"),
    "c_ss": ([128, T], F32, "ExternalInput"),
    "c_pr": ([128, 128], F32, "ExternalInput"),
    "c_pr64": ([128, 128], F32, "ExternalInput"),
    "c_idf": ([128, 128], F32, "ExternalInput"),
    "c_t1c": ([128, 128], F32, "ExternalInput"),
    "c_t2c": ([128, 4 * TILE], F32, "ExternalInput"),
    "y_out": ([1024, T], BF16, "ExternalOutput"),
}


def build_phase_a1(stage="full"):
    nc = bass.Bass("TRN2", target_bir_lowering=False, detect_race_conditions=False)
    es = contextlib.ExitStack()
    ctx = Ctx(nc, es)
    with es:
        io = dram_io(nc, A1_INPUTS + ["xr", "y_out"], A1_SHAPES)
        cpool = nc.dram_tensor("cpool", [128, max(4, len(CONSTS))], F32,
                               kind="ExternalInput").ap()
        cm = Common(ctx, cpool)
        ps = [ctx.psum(f"ps{i}", [128, 512], F32) for i in range(7)]
        psb = ctx.psum("psb", [128, 1024], BF16)
        io["lnsc"] = nc.dram_tensor("lnsc", [D, T], BF16, kind="Internal").ap()
        _emit_phase_a1(ctx, cm, ps, psb, io, stage)
        finish_build(nc, ctx)
    return nc


def _emit_phase_a1(ctx, cm, ps, psb, io, stage="full", x_tile=None,
                   wo_part=False):
    nc = ctx.nc
    xr = io.get("xr")
    wkvT = io["wkvT"]
    wqT = io["wqT"]
    c_cc = io["c_cc"]
    c_ss = io["c_ss"]
    c_pr = io["c_pr"]
    c_pr64 = io["c_pr64"]
    c_idf = io["c_idf"]
    c_t1c = io["c_t1c"]
    c_t2c = io["c_t2c"]
    y_out = io.get("y_out")
    lnsc = io["lnsc"]
    sbf = ctx.sbuf
    s_xt = sbf("s_xt", [128, 16 * TILE], BF16)
    s_ln = sbf("s_ln", [128, 16 * TILE], BF16)
    s_k = sbf("s_k", [128, 2 * T], BF16)
    s_ksw = sbf("s_ksw", [128, 2 * T], BF16)
    s_vt = sbf("s_vt", [128, 32 * 260], BF16)
    s_q = sbf("s_q", [128, 8 * TILE], BF16)
    s_qr = sbf("s_qr", [128, 8 * TILE], BF16)
    s_cc = sbf("s_cc", [128, T], BF16)
    s_ss = sbf("s_ss", [128, T], BF16)
    s_pr = sbf("s_pr", [128, 128], BF16)
    s_pr64 = sbf("s_pr64", [128, 128], BF16)
    s_idb = sbf("s_idb", [128, 128], BF16)
    s_t1c = sbf("s_t1c", [128, 128], BF16)
    s_t2c = sbf("s_t2c", [128, 4 * TILE], BF16)
    s_pt = [sbf(f"s_pt{i}", [128, TILE], BF16) for i in range(4)]
    s_y = sbf("s_y", [64, 16 * TILE], BF16)
    s_w = [sbf(f"s_w{i}", [128, 16 * 512], BF16) for i in range(2)]
    s_sq2 = [sbf(f"s_sq{i}", [128, TILE], F32) for i in range(2)]
    s_rows = sbf("s_rows", [1, TILE * 5], F32)
    s_abc = sbf("s_abc", [128, TILE], F32)
    s_bbc = sbf("s_bbc", [128, TILE], F32)
    s_rr = sbf("s_rr", [1, TILE], F32)
    s_rb = sbf("s_rb", [64, TILE], F32)
    s_rop = [sbf(f"s_rop{i}", [128, TILE], F32) for i in range(2)]

    dg, ds, pe, ve, se, so = cm.dg, cm.ds, cm.pe, cm.ve, cm.se, cm.so
    E = ctx.emit

    for dst, srcd in ((s_cc, c_cc), (s_ss, c_ss), (s_pr, c_pr), (s_t1c, c_t1c),
                      (s_t2c, c_t2c), (s_idb, c_idf), (s_pr64, c_pr64)):
        E("gpsimd", lambda e, _d=dst, _s=srcd: e.dma_start(out=_d[:], in_=_s),
          inc=dg, inc_n=16)
    dg0 = dg.n
    ve_vt1 = 0
    for kc in range(32):
        for h in range(4):
            ve_vt1 = E("vector", lambda e, _o=kc * 260 + h * 65 + 64: e.tensor_copy(
                s_vt[:, _o:_o + 1], cm.s_cpool[:, 0:1]),
                waits=[(ds, cm.ds_cpool)], inc=ve)
    wbuf_state = {0: (pe, 0), 1: (pe, 0)}
    ve_ln_prev = 0

    def rope(nch, t0, src_ready):
        ve_last = 0
        g = max(ve.n, 0)
        war = {0: ve.n, 1: ve.n}
        ctx.wait("tensor", [(ve, ve.n), (se, se.n)])
        for c in range(nch):
            bk = ps[4 + (c % 2)]
            pe_s = E("tensor", lambda e, _c=c, _bk=bk: e.matmul(
                _bk[:, :], s_pr[:], s_q[:, _c * TILE:(_c + 1) * TILE],
                start=True, stop=True),
                waits=[src_ready, (dg, dg0), (ve, war[c % 2])], inc=pe)
            E("vector", lambda e, _bk=bk, _p=s_rop[c % 2], _t0=t0: e.tensor_tensor(
                _p[:], _bk[:, :], s_ss[:, _t0:_t0 + TILE], ALU.mult),
                waits=[(pe, pe_s)], inc=ve)
            E("vector", lambda e, _c=c, _p=s_sq2[c % 2], _t0=t0: e.tensor_tensor(
                _p[:], s_q[:, _c * TILE:(_c + 1) * TILE],
                s_cc[:, _t0:_t0 + TILE], ALU.mult), inc=ve)
            ve_last = E("vector", lambda e, _c=c, _p=s_rop[c % 2],
                        _q=s_sq2[c % 2]: e.tensor_tensor(
                s_qr[:, _c * TILE:(_c + 1) * TILE], _p[:], _q[:], ALU.add),
                inc=ve)
            war[c % 2] = ve_last
        return ve_last

    for t in range(NTILE):
        t0 = t * TILE
        xin = (x_tile(t) if x_tile is not None else
               xr.rearrange("(kc p) T -> p kc T", p=128)[:, :, t0:t0 + TILE])
        dg_xt = E("sync", lambda e, _x=xin: e.dma_start(
            out=s_xt[:].rearrange("p (kc T) -> p kc T", kc=16), in_=_x),
            waits=[(ve, ve_ln_prev)], inc=cm.dx, inc_n=16)
        ve_ln, _ = emit_ln(
            ctx, cm, lambda kc: s_xt[:, kc * TILE:(kc + 1) * TILE], 16,
            lambda kc: s_ln[:, kc * TILE:(kc + 1) * TILE],
            ps[5], ps[6], s_sq2, s_rows, s_abc, s_bbc,
            (cm.dx, dg_xt), [(se, se.n), (ve, ve.n), (pe, pe.n)])
        ve_ln_prev = ve_ln
        E("sync", lambda e, _t0=t0: e.dma_start(
            out=lnsc.rearrange("(kc p) T -> p kc T", p=128)[:, :, _t0:_t0 + TILE],
            in_=s_ln[:].rearrange("p (kc T) -> p kc T", kc=16)),
            waits=[(ve, ve_ln)], inc=so, inc_n=16)

        kvev = {}

        def evac_kv(grp, gi, pe_val, _t=t):
            marks = {}
            for j, (oc, w) in enumerate(grp):
                sev = E("scalar", lambda e, _j=j: e.activation(
                    s_q[:, _j * TILE:(_j + 1) * TILE], ps[_j][:, :], AF.Copy),
                    waits=[(pe, pe_val), (ve, ve.n)], inc=se)
                marks[(j, 0)] = (se, sev)
                kvev[j] = sev
            return marks

        emit_projmt(ctx, cm, wkvT, D,
                    [[(0, 128), (128, 128), (256, 128), (384, 128)]],
                    lambda kc, tt: s_ln[:, kc * TILE:(kc + 1) * TILE],
                    (ve, ve_ln), evac_kv, s_w, wbuf_state,
                    lambda j, tt: ps[j], 1, init_free=[(se, se.n), (ve, ve.n)])

        ve_rk = rope(2, t0, (se, max(kvev[0], kvev[1])))
        for c in range(2):
            E("vector", lambda e, _c=c, _t0=t0: e.tensor_copy(
                s_k[:, _c * T + _t0:_c * T + _t0 + TILE],
                s_qr[:, _c * TILE:(_c + 1) * TILE]),
                waits=[(ve, ve_rk)], inc=ve)
            pe_sw = E("tensor", lambda e, _c=c: e.matmul(
                ps[4 + _c][:, :], s_pr64[:], s_qr[:, _c * TILE:(_c + 1) * TILE],
                start=True, stop=True), waits=[(ve, ve_rk)], inc=pe)
            E("vector", lambda e, _c=c, _t0=t0: e.tensor_copy(
                s_ksw[:, _c * T + _t0:_c * T + _t0 + TILE], ps[4 + _c][:, :]),
                waits=[(pe, pe_sw)], inc=ve)
        ve_k_done = ve.n
        ve_tv = 0
        for c in range(2):
            for sub in range(4):
                kc = t * 4 + sub
                slot = (c * 4 + sub) % 8
                pe_tv = E("tensor", lambda e, _c=c, _sub=sub, _sl=slot: e.transpose(
                    psb[:, _sl * 128:(_sl + 1) * 128],
                    s_q[:, (2 + _c) * TILE + _sub * 128:
                        (2 + _c) * TILE + (_sub + 1) * 128],
                    s_idb[:]),
                    waits=[(se, kvev[2 + c]), (ve, ve_tv), (dg, dg0)], inc=pe)
                for hh in range(2):
                    h = 2 * c + hh
                    ve_tv = E("vector", lambda e, _kc=kc, _h=h, _sl=slot,
                              _hh=hh: e.tensor_copy(
                        s_vt[:, _kc * 260 + _h * 65:_kc * 260 + _h * 65 + 64],
                        psb[:, _sl * 128 + _hh * 64:_sl * 128 + _hh * 64 + 64]),
                        waits=[(pe, pe_tv)], inc=ve)
    ve_kv_done = ve.n
    so_p1 = so.n

    if stage == "kv":
        E("sync", lambda e: e.dma_start(
            out=y_out.rearrange("(kc p) T -> p kc T", p=128)[:, 0:2, :],
            in_=s_k[:].rearrange("p (kc T) -> p kc T", kc=2)),
            waits=[(ve, ve_kv_done)], inc=so, inc_n=16)
        ctx.wait("sync", [(so, so.n)])
        return

    if wo_part:
        woT = io["woT"]
        wopart = io["wopart"]
        s_yfm = s_q
        s_wosb = s_xt
    dlq = ctx.sem("dlq")
    pe_q_users = pe.n
    pe_yfm_prev = 0
    so_wo_prev = 0
    se_rb_prev = 0
    ve_rb_war = 0
    for t in range(NTILE):
        t0 = t * TILE
        dg_ln = E("sync", lambda e, _t0=t0: e.dma_start(
            out=s_ln[:].rearrange("p (kc T) -> p kc T", kc=16),
            in_=lnsc.rearrange("(kc p) T -> p kc T", p=128)[:, :, _t0:_t0 + TILE]),
            waits=[(so, so_p1), (pe, pe_q_users)], inc=dlq, inc_n=16)

        ve_q_free = ve.n

        def evac_q(grp, gi, pe_val, _t=t):
            marks = {}
            for j, (oc, w) in enumerate(grp):
                cch = oc // 128
                sev = E("scalar", lambda e, _j=j, _c=cch: e.activation(
                    s_q[:, _c * TILE:(_c + 1) * TILE], ps[_j][:, :], AF.Copy),
                    waits=[(pe, pe_val), (ve, ve_q_free)], inc=se)
                marks[(j, 0)] = (se, sev)
            return marks

        emit_projmt(ctx, cm, wqT, D,
                    [[(o * 128, 128) for o in range(g * 4, g * 4 + 4)]
                     for g in range(2)],
                    lambda kc, tt: s_ln[:, kc * TILE:(kc + 1) * TILE],
                    (dlq, dg_ln), evac_q, s_w, wbuf_state,
                    lambda j, tt: ps[j], 1, init_free=[(se, se.n), (ve, ve.n)])
        se_q_done = se.n
        ve_rq = rope(8, t0, (se, se_q_done))

        nkc = 4 * t + 4
        ve_y_ev = {0: ve.n, 1: ve.n}
        se_exp = {0: {0: se.n, 1: se.n}, 1: {0: se.n, 1: se.n}}
        pe_av = {0: {0: pe.n, 1: pe.n}, 1: {0: pe.n, 1: pe.n}}
        pe_rb_prev = 0
        pe_yfm_g = pe_yfm_prev if wo_part else 0
        for pr in range(8):
            heads = (2 * pr, 2 * pr + 1)
            kvh = heads[0] // 4
            kchunk = kvh // 2
            kbase = (kvh % 2) * 64
            for kc in range(nkc):
                par = kc % 2
                off = t0 - kc * 128
                diag = -512 < off <= 0
                pe_sc = {}
                for hi, h in enumerate(heads):
                    stb = ps[2 * hi + par]
                    qb = (h % 2) * 64
                    ksrc = s_k if qb == kbase else s_ksw
                    v_ = E("tensor", lambda e, _kc=kc, _bk=stb, _kchunk=kchunk,
                           _qb=qb, _h=h, _ks=ksrc, _st=(not diag): e.matmul(
                        _bk[:, :],
                        _ks[_qb:_qb + 64,
                            _kchunk * T + _kc * 128:_kchunk * T + _kc * 128 + 128],
                        s_qr[_qb:_qb + 64,
                             (_h // 2) * TILE:(_h // 2) * TILE + TILE],
                        start=True, stop=_st),
                        waits=[(ve, ve_rq), (ve, ve_k_done),
                               (se, se_exp[hi][par])],
                        inc=(pe if not diag else None))
                    if diag:
                        vmask = (-off) // 128
                        v_ = E("tensor", lambda e, _bk=stb, _v=vmask: e.matmul(
                            _bk[:, :], s_t1c[:],
                            s_t2c[:, _v * TILE:(_v + 1) * TILE],
                            start=False, stop=True), inc=pe)
                    pe_sc[hi] = v_
                for hi, h in enumerate(heads):
                    stb = ps[2 * hi + par]
                    se_exp[hi][par] = E("scalar", lambda e, _bk=stb,
                                        _p=s_pt[2 * hi + par]: e.activation(
                        _p[:], _bk[:, :], AF.Exp, scale=0.125, bias=-10.0),
                        waits=[(pe, pe_sc[hi]), (pe, pe_av[hi][par])], inc=se)
                for hi, h in enumerate(heads):
                    pe_av[hi][par] = E("tensor", lambda e, _kc=kc, _kvh=kvh,
                                       _p=s_pt[2 * hi + par], _yb=ps[4 + hi],
                                       _f=(kc == 0), _l=(kc == nkc - 1): e.matmul(
                        _yb[0:65, :],
                        s_vt[:, _kc * 260 + _kvh * 65:_kc * 260 + _kvh * 65 + 65],
                        _p[:], start=_f, stop=_l),
                        waits=[(se, se_exp[hi][par]), (ve, ve_vt1),
                               (ve, ve_y_ev[hi])],
                        inc=pe)
            for hi, h in enumerate(heads):
                yb = ps[4 + hi]
                pe_yd = pe_av[hi][(nkc - 1) % 2]
                vv = E("vector", lambda e, _yb=yb: e.reciprocal(
                    s_rr[0:1, :], _yb[64:65, :]),
                    waits=[(pe, pe_yd), (pe, pe_rb_prev)], inc=ve)
                pe_rb = E("tensor", lambda e: e.matmul(
                    ps[6][0:64, :], cm.s_ones_f[0:1, 0:64], s_rr[0:1, :],
                    start=True, stop=True),
                    waits=[(ve, vv), (se, se_rb_prev)], inc=pe)
                pe_rb_prev = pe_rb
                se_rb = E("scalar", lambda e: e.activation(
                    s_rb[:, :], ps[6][0:64, :], AF.Copy),
                    waits=[(pe, pe_rb), (ve, ve_rb_war)], inc=se)
                se_rb_prev = se_rb
                ve_y_ev[hi] = E("vector", lambda e, _h=h, _yb=yb: e.tensor_tensor(
                    s_y[:, _h * TILE:(_h + 1) * TILE], _yb[0:64, :], s_rb[:, :],
                    ALU.mult), waits=[(se, se_rb), (pe, pe_yfm_g)], inc=ve)
                ve_rb_war = ve_y_ev[hi]
        pe_q_users = pe.n
        if not wo_part:
            _unused = 0
        if not wo_part:
            E("sync", lambda e, _t0=t0: e.dma_start(
                out=y_out.rearrange("(h p) T -> p h T", p=64)[:, :, _t0:_t0 + TILE],
                in_=s_y[:].rearrange("p (h T) -> p h T", h=16)),
                waits=[(ve, ve.n)], inc=so, inc_n=16)
            continue
        ve_yall = ve.n
        se_fm = {0: se.n, 1: se.n}
        for c in range(8):
            bk = ps[4 + c % 2]
            E("tensor", lambda e, _c=c, _bk=bk: e.matmul(
                _bk[:, :], s_idb[0:64, :], s_y[:, (2 * _c) * TILE:
                                               (2 * _c + 1) * TILE],
                start=True, stop=False),
                waits=[(ve, ve_yall), (se, se_fm[c % 2])], inc=None)
            pe_f = E("tensor", lambda e, _c=c, _bk=bk: e.matmul(
                _bk[:, :], s_pr64[0:64, :], s_y[:, (2 * _c + 1) * TILE:
                                                (2 * _c + 2) * TILE],
                start=False, stop=True), inc=pe)
            se_fm[c % 2] = E("scalar", lambda e, _c=c, _bk=bk: e.activation(
                s_yfm[:, _c * TILE:(_c + 1) * TILE], _bk[:, :], AF.Copy),
                waits=[(pe, pe_f)], inc=se)
        se_fm_done = max(se_fm.values())

        def ev_wo(grp, gi, pe_val, _t=t):
            marks = {}
            for j, (oc, w) in enumerate(grp):
                sev = E("scalar", lambda e, _j=j, _oc=oc // 128: e.activation(
                    s_wosb[:, _oc * TILE:(_oc + 1) * TILE], ps[_j][:, :],
                    AF.Copy), waits=[(pe, pe_val), (so, so_wo_prev)], inc=se)
                marks[(j, 0)] = (se, sev)
            return marks

        emit_projmt(ctx, cm, woT, 1024,
                    [[(o * 128, 128) for o in range(g * 4, g * 4 + 4)]
                     for g in range(4)],
                    lambda kc, tt: s_yfm[:, kc * TILE:(kc + 1) * TILE],
                    (se, se_fm_done), ev_wo, s_w, wbuf_state,
                    lambda j, tt: ps[j], 1, init_free=[(se, se.n), (ve, ve.n)])
        pe_yfm_prev = pe.n
        pe_q_users = pe.n
        E("sync", lambda e, _t=t: e.dma_start(
            out=wopart.rearrange("(s o p) T -> p s o T", p=128, o=16)[
                :, _t // 4, :, (_t % 4) * TILE:(_t % 4 + 1) * TILE],
            in_=s_wosb[:].rearrange("p (o T) -> p o T", o=16)),
            waits=[(se, se.n)], inc=so, inc_n=16)
        so_wo_prev = so.n
    ctx.wait("sync", [(so, so.n)])


def host_prep_a1(inputs, core, resid2_full):
    b, hf = core // 2, core % 2
    c_idf, _, c_m1, _ = host_consts()
    perm = np.concatenate([np.arange(0, 64, 2), np.arange(1, 64, 2)])
    wq = np.asarray(inputs["L1_attn_wq"], np.float32)
    wk = np.asarray(inputs["L1_attn_wk"], np.float32)
    wv = np.asarray(inputs["L1_attn_wv"], np.float32)
    qrows = np.concatenate([(hf * 16 + h) * 64 + perm for h in range(16)])
    krows = np.concatenate([(hf * 4 + h) * 64 + perm for h in range(4)])
    vrows = np.concatenate([np.arange((hf * 4 + h) * 64, (hf * 4 + h) * 64 + 64)
                            for h in range(4)])
    wqT = tile_w(wq[qrows].T, [[(0, 512)], [(512, 512)]])
    wkvT = tile_w(np.concatenate([wk[krows].T, wv[vrows].T], axis=1),
                  [[(0, 512)]])
    theta = 1.0 / 10000.0 ** (np.arange(0, 64, 2, dtype=np.float32) / 64)
    tpos = np.arange(T, dtype=np.float32)
    fr = tpos[None, :] * theta[:, None]
    cosr = np.cos(fr).astype(np.float32)
    sinr = np.sin(fr).astype(np.float32)
    cc = np.empty((128, T), np.float32)
    ss = np.empty((128, T), np.float32)
    for r in range(128):
        i = r % 64
        if i < 32:
            cc[r] = cosr[i]
            ss[r] = -sinr[i]
        else:
            cc[r] = cosr[i - 32]
            ss[r] = sinr[i - 32]
    c_pr = np.zeros((128, 128), np.float32)
    for a in range(128):
        sw = a + 32 if (a % 64) < 32 else a - 32
        c_pr[sw, a] = 1.0
    c_pr64 = np.zeros((128, 128), np.float32)
    for a in range(128):
        c_pr64[a ^ 64, a] = 1.0
    c_t2c = np.zeros((128, 4 * TILE), np.float32)
    for v in range(4):
        for q_ in range(TILE):
            pp = q_ - 128 * v + 1
            pp = max(0, pp)
            if pp <= 127:
                c_t2c[pp, v * TILE + q_] = BIGNEG
    ret = {
        "wkvT": wkvT, "wqT": wqT,
        "c_cc": cc, "c_ss": ss, "c_pr": c_pr, "c_pr64": c_pr64, "c_idf": c_idf,
        "c_t1c": c_m1, "c_t2c": c_t2c, "cpool": make_cpool_host(),
    }
    if resid2_full is not None:
        ret["xr"] = np.ascontiguousarray(resid2_full[b])
    else:
        wo = np.asarray(inputs["L1_attn_wo"], np.float32)
        ret["woT"] = tile_w(wo.T[hf * 1024:(hf + 1) * 1024],
                            [[(g * 512, 512)] for g in range(4)])
    return ret



F1_INPUTS = ["woT", "gw1T", "wgu", "wdn", "swgu", "swdT", "sgwT", "c_idf",
             "c_oh8"]
F1_SHAPES = {
    "y_in": ([D, 2048], BF16, "ExternalInput"),
    "xr": ([D, 2048], BF16, "ExternalInput"),
    "woT": ([128, 16 * 2048], F32, "ExternalInput"),
    "gw1T": ([128, 16 * 8], F32, "ExternalInput"),
    "wgu": ([8, 128, 16 * 2816], F32, "ExternalInput"),
    "wdn": ([8, 128, 11 * 2048], F32, "ExternalInput"),
    "swgu": ([128, 16 * 11264], F32, "ExternalInput"),
    "swdT": ([128, 44 * 2048], F32, "ExternalInput"),
    "sgwT": ([128, 16], F32, "ExternalInput"),
    "c_idf": ([128, 128], F32, "ExternalInput"),
    "c_oh8": ([8, 8 * 128], F32, "ExternalInput"),
    "r_out": ([D, 2048], F32, "ExternalOutput"),
}


def build_phase_a2f1(stage="full"):
    nc = bass.Bass("TRN2", target_bir_lowering=False, detect_race_conditions=False)
    es = contextlib.ExitStack()
    ctx = Ctx(nc, es)
    with es:
        io = dram_io(nc, F1_INPUTS + ["y_in", "xr", "r_out"], F1_SHAPES)
        cpool = nc.dram_tensor("cpool", [128, max(4, len(CONSTS))], F32,
                               kind="ExternalInput").ap()
        cm = Common(ctx, cpool)
        ps = [ctx.psum(f"ps{i}", [128, 512], F32) for i in range(7)]
        _emit_phase_a2f1(ctx, cm, ps, io, stage)
        finish_build(nc, ctx)
    return nc


def _emit_phase_a2f1(ctx, cm, ps, io, stage="full", rs_mode=False):
    nc = ctx.nc
    NT = 2048
    NE = 8
    y_in = io.get("y_in")
    xr = io["xr"]
    woT = io.get("woT")
    gw1T = io["gw1T"]
    wgu = io["wgu"]
    wdn = io["wdn"]
    swgu = io["swgu"]
    swdT = io["swdT"]
    sgwT = io["sgwT"]
    c_idf = io["c_idf"]
    c_oh8 = io["c_oh8"]
    r_out = io["r_out"]
    sbf = ctx.sbuf
    s_rt = sbf("s_rt", [128, 16 * NT], BF16)
    s_lnf = sbf("s_lnf", [128, 16 * NT], BF16)
    s_hmid = sbf("s_hmid", [128, 11 * 1024], BF16)
    s_sh = s_hmid
    s_w = [sbf(f"s_w{i}", [128, 16 * 256], BF16) for i in range(2)]
    s_yb = [sbf(f"s_yb{i}", [128, NT], BF16) for i in range(2)]
    s_idf = sbf("s_idf", [128, 128], F32)
    s_oh8 = sbf("s_oh8", [NE, NE * 128], BF16)
    s_gw1 = sbf("s_gw1", [128, 16 * NE], BF16)
    s_sgw = sbf("s_sgw", [128, 16], BF16)
    s_rows = sbf("s_rows", [1, TILE * 5], F32)
    s_sq2 = [sbf(f"s_sq{i}", [128, TILE], F32) for i in range(2)]
    s_abc = sbf("s_abc", [128, TILE], F32)
    s_bbc = sbf("s_bbc", [128, TILE], F32)
    s_ev = [sbf(f"s_ev{i}", [128, TILE], F32) for i in range(2)]
    s_sm = sbf("s_sm", [128, 8], F32)
    s_e16 = sbf("s_e16", [128, NE], F32)
    s_e16b = sbf("s_e16b", [128, NE], F32)
    s_cmb = sbf("s_cmb", [128, NE], F32)
    s_CT = sbf("s_CT", [NE, NT], BF16)
    s_cbc = s_yb[0]
    s_sgbc = s_yb[1]

    dsr = [ctx.sem("dr0"), ctx.sem("dr1")]
    dg, ds, pe, ve, se, so = cm.dg, cm.ds, cm.pe, cm.ve, cm.se, cm.so
    E = ctx.emit

    E("sync", lambda e: e.dma_start(out=s_idf[:], in_=c_idf), inc=ds, inc_n=16)
    for dst, srcd in ((s_gw1, gw1T), (s_sgw, sgwT), (s_oh8, c_oh8)):
        E("gpsimd", lambda e, _d=dst, _s=srcd: e.dma_start(out=_d[:], in_=_s),
          inc=dg, inc_n=16)
    ds0, dg0 = ds.n, dg.n
    dg_rt = E("sync", lambda e: e.dma_start(
        out=s_rt[:].rearrange("p (kc T) -> p kc T", kc=16),
        in_=xr.rearrange("(kc p) T -> p kc T", p=128)), inc=cm.dx, inc_n=16)

    wbuf_state = {0: (pe, 0), 1: (pe, 0)}
    ve_wo = 0
    if rs_mode:
        wop2_rs = io["wop2_rs"]
        w2src = wop2_rs.rearrange("(kc p) T -> p kc T", p=128)
        dwo = {0: 0, 1: 0}
        for kc in range(16):
            par = kc % 2
            dmw = E("sync", lambda e, _kc=kc, _p=par: e.dma_start(
                out=s_yb[_p][:], in_=w2src[:, _kc, :]),
                waits=[(ve, dwo[par])], inc=dsr[par], inc_n=16)
            ve_wo = E("vector", lambda e, _kc=kc, _p=par: e.tensor_tensor(
                s_rt[:, _kc * NT:(_kc + 1) * NT], s_rt[:, _kc * NT:(_kc + 1) * NT],
                s_yb[_p][:], ALU.add),
                waits=[(dsr[par], dmw), (cm.dx, dg_rt)], inc=ve)
            dwo[par] = ve_wo
    else:
        ysrc = y_in.rearrange("(kc p) T -> p kc T", p=128)
        ystream = RhsStream(ctx, cm, "y", lambda kc: ysrc[:, kc, :], s_yb, dsr,
                            False)

        def wo_evac(grp, gi, pe_val):
            nonlocal ve_wo
            marks = {}
            oc = grp[0][0]
            for tt in range(4):
                dst = s_rt[:, (oc // 128) * NT + tt * TILE:
                           (oc // 128) * NT + (tt + 1) * TILE]
                ve_wo = E("vector", lambda e, _d=dst, _tt=tt: e.tensor_tensor(
                    _d, _d, ps[_tt][:, :], ALU.add),
                    waits=[(pe, pe_val), (cm.dx, dg_rt)], inc=ve)
                marks[(0, tt)] = (ve, ve_wo)
            return marks

        emit_projmt(ctx, cm, woT, D, [[(o, 128)] for o in range(0, D, 128)],
                    None, None, wo_evac, s_w, wbuf_state,
                    lambda j, tt: ps[tt], 4, init_free=[(ve, 0)], stream=ystream)

    if stage == "wo":
        E("gpsimd", lambda e: e.dma_start(
            out=r_out.rearrange("(kc p) T -> p kc T", p=128),
            in_=s_rt[:].rearrange("p (kc T) -> p kc T", kc=16)),
            waits=[(ve, ve_wo)], inc=so, inc_n=16)
        ctx.wait("gpsimd", [(so, so.n)])
        return

    ve_ln = 0
    for tt in range(4):
        ve_ln, _ = emit_ln(
            ctx, cm, lambda kc, _tt=tt: s_rt[:, kc * NT + _tt * TILE:
                                             kc * NT + (_tt + 1) * TILE], 16,
            lambda kc, _tt=tt: s_lnf[:, kc * NT + _tt * TILE:
                                     kc * NT + (_tt + 1) * TILE],
            ps[5], ps[6], s_sq2, s_rows, s_abc, s_bbc,
            (ve, ve_wo), [(se, se.n), (ve, ve.n), (pe, pe.n)])

    se_ct = 0
    ve_p4_users = 0
    se_p4_users = 0
    for tt in range(4):
        for tsub in range(4):
            pe_rl = 0
            for kc in range(16):
                off = kc * NT + tt * TILE + tsub * 128
                pe_rl = E("tensor", lambda e, _o=off, _kc=kc, _f=(kc == 0),
                          _l=(kc == 15): e.matmul(
                    ps[4][:, 0:NE], s_lnf[:, _o:_o + 128],
                    s_gw1[:, _kc * NE:(_kc + 1) * NE], start=_f, stop=_l),
                    waits=[(ve, ve_ln), (dg, dg0),
                           (ve, ve_p4_users), (se, se_p4_users)], inc=pe)

            def VB(fn, waits=()):
                vv = E("vector", fn, waits=waits, inc=ve)
                ctx.wait("vector", [(ve, vv)])
                return vv
            VB(lambda e: e.reduce_max(s_sm[:, 0:1], ps[4][:, 0:NE], axis=AX.X),
               waits=[(pe, pe_rl)])
            ve_neg = VB(lambda e: e.tensor_scalar_mul(s_sm[:, 1:2], s_sm[:, 0:1],
                                                      -1.0))
            se_e = E("scalar", lambda e: e.activation(
                s_e16[:], ps[4][:, 0:NE], AF.Exp, bias=s_sm[:, 1:2]),
                waits=[(ve, ve_neg)], inc=se)
            VB(lambda e: e.tensor_scalar(s_cmb[:], ps[4][:, 0:NE], s_sm[:, 0:1],
                                         None, ALU.subtract))
            VB(lambda e: e.tensor_scalar(s_cmb[:], s_cmb[:],
                                         cm.s_cpool[:, 4:5], cm.s_cpool[:, 0:1],
                                         ALU.mult, ALU.min))
            VB(lambda e: e.tensor_scalar(s_cmb[:], s_cmb[:], cm.s_cpool[:, 7:8],
                                         None, ALU.add))
            VB(lambda e: e.scalar_tensor_tensor(s_e16b[:], s_cmb[:],
                                                cm.s_cpool[:, 6:7], ps[4][:, 0:NE],
                                                ALU.mult, ALU.add))
            VB(lambda e: e.reduce_max(s_sm[:, 2:3], s_e16b[:], axis=AX.X))
            VB(lambda e: e.tensor_scalar(s_cmb[:], ps[4][:, 0:NE], s_sm[:, 2:3],
                                         None, ALU.subtract))
            VB(lambda e: e.tensor_scalar(s_cmb[:], s_cmb[:],
                                         cm.s_cpool[:, 4:5], cm.s_cpool[:, 5:6],
                                         ALU.mult, ALU.max))
            VB(lambda e: e.tensor_scalar(s_cmb[:], s_cmb[:],
                                         cm.s_cpool[:, 0:1], cm.s_cpool[:, 7:8],
                                         ALU.min, ALU.add))
            VB(lambda e: e.tensor_scalar(s_cmb[:], s_cmb[:], cm.s_cpool[:, 7:8],
                                         None, ALU.mult))
            VB(lambda e: e.tensor_tensor(s_cmb[:], s_cmb[:], s_e16[:], ALU.mult),
               waits=[(se, se_e)])
            VB(lambda e: e.tensor_reduce(s_sm[:, 3:4], s_cmb[:], axis=AX.X,
                                         op=ALU.add))
            VB(lambda e: e.reciprocal(s_sm[:, 4:5], s_sm[:, 3:4]))
            ve_c = VB(lambda e: e.tensor_scalar(s_cmb[:], s_cmb[:], s_sm[:, 4:5],
                                                None, ALU.mult))
            ve_p4_users = ve_c
            se_p4_users = se_e
            pe_tc = E("tensor", lambda e: e.transpose(
                ps[5][0:NE, 0:128], s_cmb[:, 0:NE].rearrange(
                    "p c -> p c") if False else s_cmb[:], s_idf[:]),
                waits=[(ve, ve_c), (se, se_ct), (ds, ds0)], inc=pe)
            se_ct = E("scalar", lambda e, _o=tt * TILE + tsub * 128: e.activation(
                s_CT[0:NE, _o:_o + 128], ps[5][0:NE, 0:128], AF.Copy),
                waits=[(pe, pe_tc)], inc=se)

    ve_hmid_last = 0
    se_cbc_users = 0
    for ei in range(NE):
        se_cbc = se_cbc_users
        pe_cb = 0
        for tt in range(4):
            pe_cb = E("tensor", lambda e, _e=ei, _tt=tt: e.matmul(
                ps[6][:, :], s_oh8[:, _e * 128:(_e + 1) * 128],
                s_CT[0:NE, _tt * TILE:(_tt + 1) * TILE], start=True, stop=True),
                waits=[(se, se_ct), (ve, ve_hmid_last), (se, se_cbc)], inc=pe)
            se_cbc = E("scalar", lambda e, _tt=tt: e.activation(
                s_cbc[:, _tt * TILE:(_tt + 1) * TILE], ps[6][:, :], AF.Copy),
                waits=[(pe, pe_cb)], inc=se)
        se_cbc_users = se_cbc

        wgug = [[(j * 256, 128), (j * 256 + 128, 128)] for j in range(11)]
        for tth in range(2):
            def evg(grp, gi, pe_val, _tth=tth, _e=ei):
                nonlocal ve_hmid_last
                marks = {}
                j = grp[0][0] // 256
                for tl in range(2):
                    tt = _tth * 2 + tl
                    sil = s_ev[tl]
                    sev = E("scalar", lambda e, _s=sil, _tl=tl: e.activation(
                        _s[:], ps[_tl * 2][:, :], AF.Silu),
                        waits=[(pe, pe_val)], inc=se)
                    E("vector", lambda e, _s=sil, _tl=tl: e.tensor_tensor(
                        _s[:], _s[:], ps[_tl * 2 + 1][:, :], ALU.mult),
                        waits=[(se, sev)], inc=ve)
                    dst = s_hmid[:, j * 1024 + tl * TILE: j * 1024 + (tl + 1) * TILE]
                    vh = E("vector", lambda e, _d=dst, _s=sil, _tt=tt: e.tensor_tensor(
                        _d, _s[:], s_cbc[:, _tt * TILE:(_tt + 1) * TILE], ALU.mult),
                        waits=[(se, se_cbc)], inc=ve)
                    marks[(0, tl)] = (ve, vh)
                    marks[(1, tl)] = (ve, vh)
                    ve_hmid_last = vh
                return marks

            emit_projmt(ctx, cm, wgu[ei], D, wgug,
                        lambda kc, tl, _tth=tth: s_lnf[:, kc * NT + (_tth * 2 + tl)
                                                       * TILE: kc * NT +
                                                       (_tth * 2 + tl + 1) * TILE],
                        (ve, ve_ln), evg, s_w, wbuf_state,
                        lambda j, tl: ps[tl * 2 + j], 2,
                        init_free=[(ve, ve.n), (se, se.n)])

            def evd(grp, gi, pe_val, _e=ei, _tth=tth):
                marks = {}
                oc = grp[0][0]
                for tl in range(2):
                    tt = _tth * 2 + tl
                    dst = s_rt[:, (oc // 128) * NT + tt * TILE:
                               (oc // 128) * NT + (tt + 1) * TILE]
                    vv = E("vector", lambda e, _d=dst, _tl=tl: e.tensor_tensor(
                        _d, _d, ps[_tl][:, :], ALU.add),
                        waits=[(pe, pe_val), (ve, ve_ln)], inc=ve)
                    marks[(0, tl)] = (ve, vv)
                return marks

            emit_projmt(ctx, cm, wdn[ei], 1408,
                        [[(o, 128)] for o in range(0, D, 128)],
                        lambda kc, tl: s_hmid[:, kc * 1024 + tl * TILE:
                                              kc * 1024 + (tl + 1) * TILE],
                        (ve, ve_hmid_last), evd, s_w, wbuf_state,
                        lambda j, tl: ps[tl], 2, init_free=[(ve, ve.n), (se, se.n)])

    if stage == "moe":
        E("gpsimd", lambda e: e.dma_start(
            out=r_out.rearrange("(kc p) T -> p kc T", p=128),
            in_=s_rt[:].rearrange("p (kc T) -> p kc T", kc=16)),
            waits=[(ve, ve.n)], inc=so, inc_n=16)
        ctx.wait("gpsimd", [(so, so.n)])
        return

    se_sg = se.n
    ve_sg_guard = ve.n
    for tt in range(4):
        pe_sg = 0
        for kc in range(16):
            pe_sg = E("tensor", lambda e, _kc=kc, _tt=tt, _f=(kc == 0),
                      _l=(kc == 15): e.matmul(
                ps[4][0:1, :], s_sgw[:, _kc:_kc + 1],
                s_lnf[:, _kc * NT + _tt * TILE:_kc * NT + (_tt + 1) * TILE],
                start=_f, stop=_l),
                waits=[(ve, ve_ln), (dg, dg0), (se, se_sg), (ve, ve_sg_guard)],
                inc=pe)
        r = s_rows[0:1, 0:TILE]
        E("scalar", lambda e, _r=r: e.activation(_r, ps[4][0:1, :], AF.Sigmoid),
          waits=[(pe, pe_sg)], inc=se)
        pe_sgb = E("tensor", lambda e, _r=r: e.matmul(
            ps[5][:, :], cm.s_ones_f[:], _r, start=True, stop=True),
            waits=[(se, se.n)], inc=pe)
        se_sg = E("scalar", lambda e, _tt=tt: e.activation(
            s_sgbc[:, _tt * TILE:(_tt + 1) * TILE], ps[5][:, :], AF.Copy),
            waits=[(pe, pe_sgb)], inc=se)

    for blk in range(22):
        sgug = [[(blk * 512 + j * 256, 128), (blk * 512 + j * 256 + 128, 128)]
                for j in range(2)]
        for tth in range(2):
            def evsh(grp, gi, pe_val, _tth=tth, _blk=blk):
                marks = {}
                j = (grp[0][0] - _blk * 512) // 256
                for tl in range(2):
                    tt = _tth * 2 + tl
                    sil = s_ev[tl]
                    sev = E("scalar", lambda e, _s=sil, _tl=tl: e.activation(
                        _s[:], ps[_tl * 2][:, :], AF.Silu),
                        waits=[(pe, pe_val)], inc=se)
                    dst = s_sh[:, j * (4 * TILE) + tt * TILE:
                               j * (4 * TILE) + (tt + 1) * TILE]
                    vh = E("vector", lambda e, _d=dst, _s=sil, _tl=tl: e.tensor_tensor(
                        _d, _s[:], ps[_tl * 2 + 1][:, :], ALU.mult),
                        waits=[(se, sev)], inc=ve)
                    marks[(0, tl)] = (ve, vh)
                    marks[(1, tl)] = (ve, vh)
                return marks

            emit_projmt(ctx, cm, swgu, D, sgug,
                        lambda kc, tl, _tth=tth: s_lnf[:, kc * NT + (_tth * 2 + tl)
                                                       * TILE: kc * NT +
                                                       (_tth * 2 + tl + 1) * TILE],
                        (ve, ve_ln), evsh, s_w, wbuf_state,
                        lambda j, tl: ps[tl * 2 + j], 2,
                        init_free=[(ve, ve.n), (se, se.n)])
        ve_sh = ve.n

        def evsd(grp, gi, pe_val, _blk=blk):
            marks = {}
            oc = grp[0][0]
            for tt in range(4):
                E("vector", lambda e, _tt=tt: e.tensor_tensor(
                    s_ev[_tt % 2][:], ps[_tt][:, :],
                    s_sgbc[:, _tt * TILE:(_tt + 1) * TILE], ALU.mult),
                    waits=[(pe, pe_val), (se, se_sg)], inc=ve)
                dst = s_rt[:, (oc // 128) * NT + tt * TILE:
                           (oc // 128) * NT + (tt + 1) * TILE]
                vv = E("vector", lambda e, _d=dst, _tt=tt: e.tensor_tensor(
                    _d, _d, s_ev[_tt % 2][:], ALU.add), inc=ve)
                marks[(0, tt)] = (ve, vv)
            return marks

        emit_projmt(ctx, cm, swdT[:, blk * 4096:(blk + 1) * 4096], 256,
                    [[(o, 128)] for o in range(0, D, 128)],
                    lambda kc, tt, _blk=blk: s_sh[:, kc * (4 * TILE) + tt * TILE:
                                                  kc * (4 * TILE) + (tt + 1) * TILE],
                    (ve, ve_sh), evsd, s_w, wbuf_state,
                    lambda j, tt: ps[tt], 4, init_free=[(ve, ve.n), (se, se.n)])

    E("gpsimd", lambda e: e.dma_start(
        out=r_out.rearrange("(kc p) T -> p kc T", p=128),
        in_=s_rt[:].rearrange("p (kc T) -> p kc T", kc=16)),
        waits=[(ve, ve.n)], inc=so, inc_n=16)
    ctx.wait("gpsimd", [(so, so.n)])


def host_prep_a2f1(inputs, core, y_full, resid2_full):
    b, hf = core // 2, core % 2
    c_idf, _, _, _ = host_consts()
    c_oh8 = np.zeros((8, 8 * 128), np.float32)
    for h in range(8):
        c_oh8[h, h * 128:(h + 1) * 128] = 1.0
    own_tok = np.concatenate([np.arange((2 * s + hf) * TILE, (2 * s + hf + 1) * TILE)
                              for s in range(4)])
    gw1 = np.asarray(inputs["L1_ffn_gate_w"], np.float32)
    gw1T = np.ascontiguousarray(
        gw1.T.reshape(16, 128, 8).transpose(1, 0, 2).reshape(128, 128))
    wg = np.asarray(inputs["L1_ffn_w_gate"], np.float32)
    wu = np.asarray(inputs["L1_ffn_w_up"], np.float32)
    wgu = np.empty((8, 2048, 2816), np.float32)
    for j in range(11):
        wgu[:, :, j * 256:j * 256 + 128] = wg[:, :, j * 128:(j + 1) * 128]
        wgu[:, :, j * 256 + 128:j * 256 + 256] = wu[:, :, j * 128:(j + 1) * 128]
    wgug = [[(j * 256, 128), (j * 256 + 128, 128)] for j in range(11)]
    oc16 = [[(o, 128)] for o in range(0, 2048, 128)]
    wgu = np.stack([tile_w(wgu[e], wgug) for e in range(8)])
    wdn_ = np.asarray(inputs["L1_ffn_w_down"], np.float32)
    wdn_ = np.stack([tile_w(wdn_[e], oc16) for e in range(8)])
    swg = np.asarray(inputs["L1_ffn_sw_gate"], np.float32)
    swu = np.asarray(inputs["L1_ffn_sw_up"], np.float32)
    swgu = np.empty((2048, 11264), np.float32)
    for j in range(44):
        swgu[:, j * 256:j * 256 + 128] = swg[:, j * 128:(j + 1) * 128]
        swgu[:, j * 256 + 128:j * 256 + 256] = swu[:, j * 128:(j + 1) * 128]
    sgw2 = np.asarray(inputs["L1_ffn_shared_gate_w"], np.float32)
    sgwT = np.ascontiguousarray(sgw2.reshape(16, 128).T)
    sgug = [[(blk * 512 + j * 256, 128), (blk * 512 + j * 256 + 128, 128)]
            for blk in range(22) for j in range(2)]
    swd = np.asarray(inputs["L1_ffn_sw_down"], np.float32)
    swdT = np.concatenate([tile_w(swd[blk * 256:(blk + 1) * 256], oc16)
                           for blk in range(22)], axis=1)
    ret = {
        "gw1T": gw1T, "wgu": wgu, "wdn": wdn_,
        "swgu": tile_w(swgu, sgug), "swdT": swdT,
        "sgwT": sgwT, "c_idf": c_idf, "c_oh8": c_oh8, "cpool": make_cpool_host(),
    }
    if y_full is not None:
        ret["y_in"] = np.ascontiguousarray(y_full[b][:, own_tok])
        ret["xr"] = np.ascontiguousarray(resid2_full[b][:, own_tok])
        wo = np.asarray(inputs["L1_attn_wo"], np.float32)
        ret["woT"] = tile_w(wo.T, oc16)
    return ret



def host_prep_m(inputs, core):
    b, hf = core // 2, core % 2
    c_idf, c_oh, c_m1, c_m2s = host_consts()
    x = np.asarray(inputs["L0_attn_hidden_states"][b])
    Wx = inputs["L0_attn_Wx"][:, hf * 2048:(hf + 1) * 2048]
    WB = inputs["L0_attn_WB"][:, hf * 512:(hf + 1) * 512]
    WC = inputs["L0_attn_WC"][:, hf * 512:(hf + 1) * 512]
    Wdt = inputs["L0_attn_Wdt"][:, hf * 32:(hf + 1) * 32]
    w_proj = tile_w(np.concatenate([Wx, WB, WC, Wdt], axis=1),
                    ogrid(3104, 5))
    w_z = tile_w(np.asarray(inputs["L0_attn_Wz"][:, hf * 2048:(hf + 1) * 2048],
                            np.float32), ogrid(2048, 4))
    cw = np.asarray(inputs["L0_attn_conv_w"])
    cb = np.asarray(inputs["L0_attn_conv_b"])
    idx = np.concatenate([
        np.arange(hf * 2048, hf * 2048 + 2048),
        np.arange(4096 + hf * 512, 4096 + hf * 512 + 512),
        np.arange(5120 + hf * 512, 5120 + hf * 512 + 512)])
    convw = np.ascontiguousarray(
        cw[idx].reshape(24, 128, 4).transpose(1, 0, 2).reshape(128, 96),
        dtype=np.float32)
    convb = np.ascontiguousarray(cb[idx].reshape(24, 128).T, dtype=np.float32)
    sl = slice(hf * 32, hf * 32 + 32)
    return {
        "xr": np.ascontiguousarray(x.T, dtype=np.float32),
        "w_proj": w_proj,
        "w_z": w_z,
        "convw": convw,
        "convb": convb,
        "dtb": np.ascontiguousarray(
            np.asarray(inputs["L0_attn_dt_bias"][sl], dtype=np.float32).reshape(32, 1)),
        "alog": np.ascontiguousarray(
            np.asarray(inputs["L0_attn_A_log"][sl], dtype=np.float32).reshape(32, 1)),
        "dvec": np.ascontiguousarray(
            np.asarray(inputs["L0_attn_D"][sl], dtype=np.float32).reshape(1, 32)),
        "c_idf": c_idf, "c_oh": c_oh, "c_m1": c_m1, "c_m2s": c_m2s,
        "cpool": make_cpool_host(),
    }


def _own_tok(hf):
    return np.concatenate([np.arange((2 * s + hf) * TILE, (2 * s + hf + 1) * TILE)
                           for s in range(4)])



RG_PAIRS = [[0, 1], [2, 3], [4, 5], [6, 7]]


def build_fused():
    nc = bass.Bass("TRN2", target_bir_lowering=False, detect_race_conditions=False)
    es = contextlib.ExitStack()
    ctx = Ctx(nc, es)
    with es:
        io_m = dram_io(nc, M_INPUTS, M_SHAPES, pfx="m_")
        f0_names = [k for k in F0_INPUTS if k not in ("g_in", "w_out")]
        io_f0 = dram_io(nc, f0_names, F0_SHAPES, pfx="f0_")
        io_p0 = dram_io(nc, ["w_out", "nw"], P0_SHAPES, pfx="p0_")
        io_a1 = dram_io(nc, A1_INPUTS + ["woT"], A1_SHAPES, pfx="a1_")
        f1_names = [k for k in F1_INPUTS if k != "woT"]
        io_f1 = dram_io(nc, f1_names, F1_SHAPES, pfx="f1_")
        cpool = nc.dram_tensor("cpool", [128, max(4, len(CONSTS))], F32,
                               kind="ExternalInput").ap()
        r_out = nc.dram_tensor("r_out", [D, 2048], F32,
                               kind="ExternalOutput").ap()
        g_loc = nc.dram_tensor("g_loc", [128, NTILE * 16 * TILE], BF16,
                               kind="Internal").ap()
        ssq_part = nc.dram_tensor("ssq_part", [2, 2048], F32,
                                  kind="Internal").ap()
        ssq_rs = nc.dram_tensor("ssq_rs", [1, 2048], F32, kind="Internal").ap()
        wop = nc.dram_tensor("wop", [4096, 2048], BF16, kind="Internal").ap()
        wop_rs = nc.dram_tensor("wop_rs", [2048, 2048], BF16,
                                kind="Internal").ap()
        r2loc = nc.dram_tensor("r2loc", [D, 2048], BF16, kind="Internal").ap()
        r2_ag = nc.dram_tensor("r2_ag", [2 * D, 2048], BF16,
                               kind="Internal").ap()
        wop2 = nc.dram_tensor("wop2", [4096, 2048], BF16, kind="Internal").ap()
        wop2_rs = nc.dram_tensor("wop2_rs", [2048, 2048], BF16,
                                 kind="Internal").ap()
        lnsc = nc.dram_tensor("lnsc", [D, T], BF16, kind="Internal").ap()

        cm = Common(ctx, cpool)
        ps = [ctx.psum(f"ps{i}", [128, 512], F32) for i in range(7)]
        psb = ctx.psum("psb", [128, 1024], BF16)
        cc = ctx.sem("cc")
        E = ctx.emit

        g_locv = g_loc.rearrange("p (t kc T) -> p t kc T", t=NTILE, kc=16)

        def g_store(t):
            return g_locv[:, t, :, :]

        with ctx.scope("m_"):
            iom = dict(io_m)
            _emit_phase_m(ctx, cm, ps, psb, iom, NTILE, "full", g_store=g_store)
        ctx.barrier_all()

        with ctx.scope("p0_"):
            iop = dict(io_p0)
            iop["ssq_part"] = ssq_part
            iop["wop"] = wop
            _emit_m2_partials(ctx, cm, ps, iop,
                              g_tile=lambda i: g_loc[:, i * 8192:(i + 1) * 8192])
        ctx.barrier_all()
        E("gpsimd", lambda e: e.collective_compute(
            "ReduceScatter", ALU.add, replica_groups=RG_PAIRS,
            ins=[ssq_part.opt()], outs=[ssq_rs.opt()]), inc=cc)
        E("gpsimd", lambda e: e.collective_compute(
            "ReduceScatter", ALU.add, replica_groups=RG_PAIRS,
            ins=[wop.opt()], outs=[wop_rs.opt()]), inc=cc)
        for eng in ENGS:
            ctx.wait(eng, [(cc, cc.n)])

        with ctx.scope("f0_"):
            io2 = dict(io_f0)
            io2["ssq_rs"] = ssq_rs
            io2["wop_rs"] = wop_rs
            io2["r_out"] = r2loc
            _emit_phase_m2f0(ctx, cm, ps, io2, "full", rs_mode=True)
        ctx.barrier_all()
        E("gpsimd", lambda e: e.collective_compute(
            "AllGather", ALU.bypass, replica_groups=RG_PAIRS,
            ins=[r2loc.opt()], outs=[r2_ag.opt()]), inc=cc)
        for eng in ENGS:
            ctx.wait(eng, [(cc, cc.n)])

        def x_tile(t):
            return r2_ag.rearrange("(r kc p) T -> p r kc T", p=128, kc=16)[
                :, t // 4, :, (t % 4) * TILE:(t % 4 + 1) * TILE]

        with ctx.scope("a1_"):
            ioa = dict(io_a1)
            ioa["lnsc"] = lnsc
            ioa["wopart"] = wop2
            _emit_phase_a1(ctx, cm, ps, psb, ioa, "full", x_tile=x_tile,
                           wo_part=True)
        ctx.barrier_all()
        E("gpsimd", lambda e: e.collective_compute(
            "ReduceScatter", ALU.add, replica_groups=RG_PAIRS,
            ins=[wop2.opt()], outs=[wop2_rs.opt()]), inc=cc)
        for eng in ENGS:
            ctx.wait(eng, [(cc, cc.n)])

        with ctx.scope("f1_"):
            iof = dict(io_f1)
            iof["xr"] = r2loc
            iof["wop2_rs"] = wop2_rs
            iof["r_out"] = r_out
            _emit_phase_a2f1(ctx, cm, ps, iof, "full", rs_mode=True)
        finish_build(nc, ctx)
    return nc


def host_prep_fused(inputs, core):
    b, hf = core // 2, core % 2
    m = host_prep_m(inputs, core)
    f0 = host_prep_m2f0(inputs, core, None)
    p0 = host_prep_m2p(inputs, core)
    a1 = host_prep_a1(inputs, core, None)
    f1 = host_prep_a2f1(inputs, core, None, None)
    im = {}
    for k in M_INPUTS:
        im["m_" + k] = m[k]
    for k in F0_INPUTS:
        if k not in ("g_in", "w_out", "xr"):
            im["f0_" + k] = f0[k]
    im["p0_w_out"] = p0["w_out"]
    im["p0_nw"] = p0["nw"]
    for k in A1_INPUTS + ["woT"]:
        im["a1_" + k] = a1[k]
    for k in F1_INPUTS:
        if k != "woT":
            im["f1_" + k] = f1[k]
    x = np.asarray(inputs["L0_attn_hidden_states"][b])
    im["f0_xr"] = np.ascontiguousarray(x[hf * 2048:(hf + 1) * 2048].T,
                                       dtype=np.float32)
    im["cpool"] = make_cpool_host()
    return im


def kernel_fused(**inputs):
    inputs = {k: np.asarray(v) for k, v in inputs.items()}
    cores = list(range(8))
    nc = build_fused()
    res = run_bass_kernel_spmd(nc, [host_prep_fused(inputs, c) for c in cores],
                               cores)
    out = np.empty((4, T, D), np.float32)
    for b in range(4):
        out[b][0:2048] = np.asarray(res.results[2 * b]["r_out"]).T
        out[b][2048:4096] = np.asarray(res.results[2 * b + 1]["r_out"]).T
    return out


def kernel(**inputs):
    return kernel_fused(**inputs)


def kernel_unfused(**inputs):
    inputs = {k: np.asarray(v) for k, v in inputs.items()}
    cores = list(range(8))

    nc = build_phase_m()
    res = run_bass_kernel_spmd(nc, [host_prep_m(inputs, c) for c in cores], cores)
    g_full = [np.concatenate([np.asarray(res.results[2 * b]["g_out"]),
                              np.asarray(res.results[2 * b + 1]["g_out"])], axis=0)
              for b in range(4)]

    nc = build_phase_m2f0()
    res = run_bass_kernel_spmd(nc, [host_prep_m2f0(inputs, c, g_full)
                                    for c in cores], cores)
    r2 = []
    for b in range(4):
        r = np.empty((D, T), dtype=ml_dtypes.bfloat16)
        r[:, _own_tok(0)] = np.asarray(res.results[2 * b]["r_out"])
        r[:, _own_tok(1)] = np.asarray(res.results[2 * b + 1]["r_out"])
        r2.append(r)

    nc = build_phase_a1()
    res = run_bass_kernel_spmd(nc, [host_prep_a1(inputs, c, r2) for c in cores],
                               cores)
    y_full = [np.concatenate([np.asarray(res.results[2 * b]["y_out"]),
                              np.asarray(res.results[2 * b + 1]["y_out"])], axis=0)
              for b in range(4)]

    nc = build_phase_a2f1()
    res = run_bass_kernel_spmd(nc, [host_prep_a2f1(inputs, c, y_full, r2)
                                    for c in cores], cores)
    out = np.empty((4, T, D), np.float32)
    for b in range(4):
        out[b][_own_tok(0)] = np.asarray(res.results[2 * b]["r_out"]).T
        out[b][_own_tok(1)] = np.asarray(res.results[2 * b + 1]["r_out"]).T
    return out


# version 05 (final)
# speedup vs baseline: 1.0176x; 1.0176x over previous
import math
import contextlib
import numpy as np
import ml_dtypes
import concourse.bass as bass
import concourse.mybir as mybir
from concourse.bass_utils import run_bass_kernel_spmd

F32 = mybir.dt.float32
BF16 = mybir.dt.bfloat16
AF = mybir.ActivationFunctionType
ALU = mybir.AluOpType
AX = mybir.AxisListType

BDIM = 4
T = 4096
D = 2048
TILE = 512
NTILE = T // TILE
CH = 128
EPS = 1e-5
CONSTS = [1.0, EPS, -10.0, 0.5, -1.0e30, 0.0, 1.0e6, -1.0]
BIGNEG = -1e30
ENGS = ("tensor", "vector", "scalar", "gpsimd", "sync")
NKC = D // 128


class Sem:
    __slots__ = ("h", "n", "name")

    def __init__(self, h, name):
        self.h = h
        self.n = 0
        self.name = name


class Ctx:
    def __init__(self, nc, es):
        self.nc = nc
        self.es = es
        self.es_outer = es
        self.streams = {e: [] for e in ENGS}
        self.waited = {e: {} for e in ENGS}
        self._semi = 0
        self.prefix = ""
        self.all_sems = []

    def sem(self, name=None):
        name = f"{name or 'sem'}_{self._semi}"
        self._semi += 1
        h = self.es_outer.enter_context(self.nc.semaphore(name))
        s = Sem(h, name)
        self.all_sems.append(s)
        return s

    @contextlib.contextmanager
    def scope(self, prefix):
        old_pfx, old_es = self.prefix, self.es
        self.prefix = prefix
        es2 = contextlib.ExitStack()
        self.es = es2
        try:
            with es2:
                yield
        finally:
            self.es = old_es
            self.prefix = old_pfx

    def barrier_all(self):
        waits = [(s, s.n) for s in self.all_sems if s.n > 0]
        for eng in ENGS:
            self.wait(eng, waits)

    def sbuf(self, name, shape, dtype):
        return self.es.enter_context(
            self.nc.sbuf_tensor(self.prefix + name, list(shape), dtype))

    def psum(self, name, shape, dtype=F32):
        return self.es.enter_context(self.nc.psum_tensor(name, list(shape), dtype))

    def emit(self, eng, fn, waits=(), inc=None, inc_n=1):
        wl = []
        wc = self.waited[eng]
        for s, v in waits:
            if v > 0 and wc.get(s.name, -1) < v:
                wl.append((s.h, v))
                wc[s.name] = v
        if inc is not None:
            inc.n += inc_n
            hn = (inc.h, inc_n)
        else:
            hn = None
        wl = tuple(wl)

        def run(e, _wl=wl, _fn=fn, _hn=hn):
            for (h, v) in _wl:
                e.wait_ge(h, v)
            ins = _fn(e)
            if _hn is not None:
                ins.then_inc(_hn[0], _hn[1])
        self.streams[eng].append(run)
        return inc.n if inc is not None else None

    def wait(self, eng, waits):
        wl = []
        wc = self.waited[eng]
        for s, v in waits:
            if v > 0 and wc.get(s.name, -1) < v:
                wl.append((s.h, v))
                wc[s.name] = v
        if not wl:
            return

        def run(e, _wl=tuple(wl)):
            for (h, v) in _wl:
                e.wait_ge(h, v)
        self.streams[eng].append(run)


def finish_build(nc, ctx):
    with nc.Block() as block:
        for eng in ENGS:
            fns = ctx.streams[eng]
            if not fns:
                continue
            dec = getattr(block, eng)

            def body(e, _fns=fns):
                for f in _fns:
                    f(e)
            dec(body)
    return nc


def make_cpool_host():
    a = np.zeros((128, max(4, len(CONSTS))), np.float32)
    for i, v in enumerate(CONSTS):
        a[:, i] = v
    return a


def ogrid(O, maxg=5):
    ocs = []
    o = 0
    while o < O:
        w = min(128, O - o)
        ocs.append((o, w))
        o += w
    return [ocs[i:i + maxg] for i in range(0, len(ocs), maxg)]


def tile_w(W, grps):
    W = np.asarray(W, np.float32)
    F, O = W.shape
    nkf = F // 128
    Wr = W.reshape(nkf, 128, O)
    blocks = []
    for grp in grps:
        o0 = grp[0][0]
        ow = sum(w for _, w in grp)
        blocks.append(Wr[:, :, o0:o0 + ow].transpose(1, 0, 2).reshape(128, nkf * ow))
    return np.ascontiguousarray(np.concatenate(blocks, axis=1))


def host_consts():
    c_idf = np.eye(128, dtype=np.float32)
    c_oh = np.zeros((32, 32 * 128), np.float32)
    for h in range(32):
        c_oh[h, h * 128:(h + 1) * 128] = 1.0
    c_m1 = np.zeros((128, 128), np.float32)
    for p in range(128):
        c_m1[p, p:] = 1.0
    c_m2s = np.zeros((128, 128), np.float32)
    for i in range(127):
        c_m2s[i + 1, i] = BIGNEG
    return c_idf, c_oh, c_m1, c_m2s


class Common:

    def __init__(self, ctx, cpool_dram):
        sbf = ctx.sbuf
        self.s_cpool = sbf("s_cpool", [128, max(4, len(CONSTS))], F32)
        for i, v in enumerate(CONSTS):
            ctx.nc.const_aps.register(F32, float(v), self.s_cpool[:, i:i + 1])
        self.s_ones_f = sbf("s_ones_f", [1, 128], F32)
        self.s_ones_b = sbf("s_ones_b", [1, 128], BF16)
        self.s_onescol_f = sbf("s_onescol_f", [128, 1], F32)
        self.s_onescol_b = sbf("s_onescol_b", [128, 1], BF16)
        self.ds = ctx.sem("ds")
        self.dg = ctx.sem("dg")
        self.dx = ctx.sem("dx")
        self.dw = [ctx.sem("dw0"), ctx.sem("dw1")]
        self.pe = ctx.sem("pe")
        self.ve = ctx.sem("ve")
        self.se = ctx.sem("se")
        self.so = ctx.sem("so")
        E = ctx.emit
        E("sync", lambda e: e.dma_start(out=self.s_cpool[:], in_=cpool_dram),
          inc=self.ds, inc_n=16)
        E("vector", lambda e: e.memset(self.s_ones_f[:], 1.0), inc=self.ve)
        E("vector", lambda e: e.memset(self.s_ones_b[:], 1.0), inc=self.ve)
        E("vector", lambda e: e.memset(self.s_onescol_f[:], 1.0), inc=self.ve)
        E("vector", lambda e: e.memset(self.s_onescol_b[:], 1.0), inc=self.ve)
        self.ve_init = self.ve.n
        self.ds_cpool = self.ds.n
        ctx.wait("scalar", [(self.ds, self.ds_cpool)])


def emit_ln(ctx, cm, x_tiles, nkc, ln_dst, pa, pb, s_sq2, s_rows, s_abc, s_bbc,
            x_ready, bank_free):
    E = ctx.emit
    pe, ve, se = cm.pe, cm.ve, cm.se
    ntok = TILE
    guard = [(pe, pe.n), (ve, ve.n)]
    se_sq = [0, 0]
    pe_mm = [0, 0]
    for kc in range(nkc):
        sq = s_sq2[kc % 2]
        se_sq[kc % 2] = E("scalar", lambda e, _x=x_tiles(kc), _q=sq: e.activation(
            _q[:], _x, AF.Square),
            waits=[x_ready, (pe, pe_mm[kc % 2])] + (guard if kc < 2 else []),
            inc=se)
        E("tensor", lambda e, _x=x_tiles(kc), _f=(kc == 0), _l=(kc == nkc - 1): e.matmul(
            pa[0:1, :], cm.s_onescol_b[:], _x, start=_f, stop=_l),
            waits=[x_ready, (ve, cm.ve_init)] + list(bank_free), inc=pe)
        pe_mm[kc % 2] = E("tensor", lambda e, _q=sq, _f=(kc == 0), _l=(kc == nkc - 1): e.matmul(
            pb[0:1, :], cm.s_onescol_f[:], _q[:], start=_f, stop=_l),
            waits=[(se, se_sq[kc % 2])], inc=pe)
    pe_stats = pe.n
    r_mean = s_rows[0:1, 0:ntok]
    r_msq = s_rows[0:1, ntok:2 * ntok]
    r_var = s_rows[0:1, 2 * ntok:3 * ntok]
    r_a = s_rows[0:1, 3 * ntok:4 * ntok]
    r_b = s_rows[0:1, 4 * ntok:5 * ntok]
    inv_n = 1.0 / (nkc * 128)
    se_m = E("scalar", lambda e: e.activation(r_mean, pa[0:1, :], AF.Copy, scale=inv_n),
             waits=[(pe, pe_stats)] + guard, inc=se)
    se_mq = E("scalar", lambda e: e.activation(r_msq, r_mean, AF.Square), inc=se)
    ve_v = E("vector", lambda e: e.scalar_tensor_tensor(
        r_var, pb[0:1, :], inv_n, r_msq, ALU.mult, ALU.subtract),
        waits=[(pe, pe_stats), (se, se_mq)], inc=ve)
    E("scalar", lambda e: e.activation(r_a, r_var, AF.Ln, bias=EPS),
      waits=[(ve, ve_v)], inc=se)
    se_a = E("scalar", lambda e: e.activation(r_a, r_a, AF.Exp, scale=-0.5), inc=se)
    E("vector", lambda e: e.tensor_tensor(r_b, r_mean, r_a, ALU.mult),
      waits=[(se, se_a)], inc=ve)
    ve_b = E("vector", lambda e: e.tensor_scalar_mul(r_b, r_b, -1.0), inc=ve)
    pe_ba = E("tensor", lambda e: e.matmul(pa[:, :], cm.s_ones_f[:], r_a,
                                           start=True, stop=True),
              waits=[(se, se_a), (ve, ve_v)], inc=pe)
    pe_bb = E("tensor", lambda e: e.matmul(pb[:, :], cm.s_ones_f[:], r_b,
                                           start=True, stop=True),
              waits=[(ve, ve_b)], inc=pe)
    se_ca = E("scalar", lambda e: e.activation(s_abc[:], pa[:, :], AF.Copy),
              waits=[(pe, pe_ba)], inc=se)
    se_cb = E("scalar", lambda e: e.activation(s_bbc[:], pb[:, :], AF.Copy),
              waits=[(pe, pe_bb)], inc=se)
    ve_last = 0
    for kc in range(nkc):
        sq = s_sq2[kc % 2]
        E("vector", lambda e, _x=x_tiles(kc), _q=sq: e.tensor_tensor(
            _q[:], _x, s_abc[:], ALU.mult),
            waits=[(se, se_ca)], inc=ve)
        ve_last = E("vector", lambda e, _d=ln_dst(kc), _q=sq: e.tensor_tensor(
            _d, _q[:], s_bbc[:], ALU.add),
            waits=[(se, se_cb)], inc=ve)
    return ve_last, pe.n


def emit_proj(ctx, cm, w_dram, Ocols, F, rhs_tiles, rhs_ready, evac, s_wbufs,
              banks, wbuf_state, maxg=None, init_free=()):
    E = ctx.emit
    pe = cm.pe
    nkf = F // 128
    grps = ogrid(Ocols, maxg or len(banks))
    bank_free = {i: list(init_free) for i in range(len(banks))}
    for gi, grp in enumerate(grps):
        par = gi % 2
        buf = s_wbufs[par]
        o0 = grp[0][0]
        ow = sum(w for _, w in grp)
        win = w_dram[:, nkf * o0:nkf * (o0 + ow)]
        dwsem = cm.dw[par]
        dg_w = E("gpsimd", lambda e, _w=win, _b=buf, _ow=ow, _nk=nkf: e.dma_start(
            out=_b[:, 0:_nk * _ow], in_=_w),
            waits=[wbuf_state[par]], inc=dwsem, inc_n=16)
        for kc in range(nkf):
            for j, (oc, w) in enumerate(grp):
                joff = oc - o0
                E("tensor", lambda e, _b=buf, _kc=kc, _j=j, _joff=joff, _w=w, _ow=ow,
                  _r=rhs_tiles(kc), _f=(kc == 0), _l=(kc == nkf - 1): e.matmul(
                    banks[_j][0:_w, :], _b[:, _kc * _ow + _joff:_kc * _ow + _joff + _w],
                    _r, start=_f, stop=_l),
                    waits=[(dwsem, dg_w), rhs_ready] + (bank_free[j] if kc == 0 else []),
                    inc=(pe if (kc == nkf - 1) else None))
        pe_grp = pe.n
        for j, (oc, w) in enumerate(grp):
            free_mark = evac(j, oc, w, banks[j], pe_grp)
            bank_free[j] = [free_mark]
        wbuf_state[par] = (pe, pe_grp)
    return


def emit_projmt(ctx, cm, w_dram, F, grps, rhs, rhs_ready, evac_grp, s_wbufs,
                wbuf_state, bank_of, ntt, init_free=(), after_load=None,
                stream=None):
    E = ctx.emit
    pe = cm.pe
    nkf = F // 128
    bank_free = {}
    pe_entry = pe.n
    for gi, grp in enumerate(grps):
        par = gi % 2
        buf = s_wbufs[par]
        o0 = grp[0][0]
        ow = sum(w for _, w in grp)
        win = w_dram[:, nkf * o0:nkf * (o0 + ow)]
        dwsem = cm.dw[par]
        dg_w = E("gpsimd", lambda e, _w=win, _b=buf, _ow=ow, _nk=nkf: e.dma_start(
            out=_b[:, 0:_nk * _ow], in_=_w),
            waits=[wbuf_state[par]], inc=dwsem, inc_n=16)
        wready = (dwsem, dg_w)
        if after_load is not None:
            wready = after_load(buf, grp, gi, ow, nkf, wready)
        pe_kc = {}
        if stream is not None:
            stream.reset(war_floor=[(pe, max(wbuf_state[0][1], wbuf_state[1][1],
                                             pe_entry))])
        for kc in range(nkf):
            if stream is not None:
                rready = stream.load(kc, [(pe, pe_kc.get(kc - 2, 0))])
            for j, (oc, w) in enumerate(grp):
                joff = oc - o0
                for tt in range(ntt):
                    if stream is not None:
                        rap, rw = stream.get(kc, tt), rready
                    else:
                        rap, rw = rhs(kc, tt), rhs_ready
                    bk = bank_of(j, tt)
                    last = (tt == ntt - 1 and j == len(grp) - 1)
                    fw = []
                    if kc == 0:
                        fw = [bank_free[(j, tt)]] if (j, tt) in bank_free \
                            else list(init_free)
                    E("tensor", lambda e, _b=buf, _kc=kc, _joff=joff, _w=w,
                      _ow=ow, _r=rap, _bk=bk, _f=(kc == 0), _l=(kc == nkf - 1):
                      e.matmul(_bk[0:_w, :],
                               _b[:, _kc * _ow + _joff:_kc * _ow + _joff + _w],
                               _r, start=_f, stop=_l),
                      waits=[wready, rw] + fw,
                      inc=(pe if last else None))
            pe_kc[kc] = pe.n
        pe_grp = pe.n
        marks = evac_grp(grp, gi, pe_grp)
        bank_free.update(marks)
        wbuf_state[par] = (pe, pe_grp)
    return


class RhsStream:

    def __init__(self, ctx, cm, name, src_kc, bufs, sems, cast):
        self.ctx, self.cm = ctx, cm
        self.src_kc = src_kc
        self.bufs = bufs
        self.sems = sems
        self.cast = cast
        self.loaded = {}
        self.war_floor = []

    def load(self, kc, war_waits):
        if kc in self.loaded:
            return self.loaded[kc]
        par = kc % 2
        buf = self.bufs[par]
        sem = self.sems[par]
        eng = "gpsimd" if self.cast else "sync"
        v = self.ctx.emit(eng, lambda e, _b=buf, _s=self.src_kc(kc): e.dma_start(
            out=_b[:], in_=_s), waits=list(war_waits) + list(self.war_floor),
            inc=sem, inc_n=16)
        self.loaded[kc] = (sem, v)
        return (sem, v)

    def reset(self, war_floor=()):
        self.loaded = {}
        self.war_floor = list(war_floor)

    def get(self, kc, tt):
        return self.bufs[kc % 2][:, tt * 512:(tt + 1) * 512]



M_INPUTS = ["xr", "w_proj", "w_z", "convw", "convb", "dtb", "alog", "dvec",
            "c_idf", "c_oh", "c_m1", "c_m2s"]


def dram_io(nc, names, shapes, pfx=""):
    io = {}
    for n in names:
        shape, dt, kind = shapes[n]
        io[n] = nc.dram_tensor(pfx + n, list(shape), dt, kind=kind).ap()
    return io


M_SHAPES = {
    "xr": ([D, T], F32, "ExternalInput"),
    "w_proj": ([128, 16 * 3104], F32, "ExternalInput"),
    "w_z": ([128, 16 * 2048], F32, "ExternalInput"),
    "convw": ([128, 24 * 4], F32, "ExternalInput"),
    "convb": ([128, 24], F32, "ExternalInput"),
    "dtb": ([32, 1], F32, "ExternalInput"),
    "alog": ([32, 1], F32, "ExternalInput"),
    "dvec": ([1, 32], F32, "ExternalInput"),
    "c_idf": ([128, 128], F32, "ExternalInput"),
    "c_oh": ([32, 32 * 128], F32, "ExternalInput"),
    "c_m1": ([128, 128], F32, "ExternalInput"),
    "c_m2s": ([128, 128], F32, "ExternalInput"),
    "g_out": ([D, T], BF16, "ExternalOutput"),
}


def build_phase_m(ntiles=NTILE, stage="full"):
    nc = bass.Bass("TRN2", target_bir_lowering=False, detect_race_conditions=False)
    es = contextlib.ExitStack()
    ctx = Ctx(nc, es)
    with es:
        io = dram_io(nc, M_INPUTS + ["g_out"], M_SHAPES)
        cpool = nc.dram_tensor("cpool", [128, max(4, len(CONSTS))], F32,
                               kind="ExternalInput").ap()
        cm = Common(ctx, cpool)
        ps = [ctx.psum(f"ps{i}", [128, 512], F32) for i in range(7)]
        psb = ctx.psum("psb", [128, 1024], BF16)
        _emit_phase_m(ctx, cm, ps, psb, io, ntiles, stage)
        finish_build(nc, ctx)
    return nc


def _emit_phase_m(ctx, cm, ps, psb, io, ntiles=NTILE, stage="full",
                  g_store=None):
    nc = ctx.nc
    xr = io["xr"]
    w_proj = io["w_proj"]
    w_z = io["w_z"]
    convw = io["convw"]
    convb = io["convb"]
    dtb = io["dtb"]
    alog = io["alog"]
    dvec = io["dvec"]
    c_idf = io["c_idf"]
    c_oh = io["c_oh"]
    c_m1 = io["c_m1"]
    c_m2s = io["c_m2s"]
    g_out = io.get("g_out")
    sbf = ctx.sbuf
    s_idf = sbf("s_idf", [128, 128], F32)
    s_idb = sbf("s_idb", [128, 128], BF16)
    s_ohf = sbf("s_ohf", [32, 32 * 128], F32)
    s_ohb = sbf("s_ohb", [32, 32 * 128], BF16)
    s_m1 = sbf("s_m1", [128, 128], BF16)
    s_m2s = sbf("s_m2s", [128, 128], BF16)
    s_convw = sbf("s_convw", [128, 24 * 4], F32)
    s_convb = sbf("s_convb", [128, 24], F32)
    s_dtb = sbf("s_dtb", [32, 1], F32)
    s_acol = sbf("s_acol", [32, 1], F32)
    s_dvec = sbf("s_dvec", [1, 32], F32)
    s_dbc = sbf("s_dbc", [128, 32], F32)
    s_xt = sbf("s_xt", [128, NKC * TILE], BF16)
    s_ln = sbf("s_ln", [128, NKC * TILE], BF16)
    s_xbcp = sbf("s_xbcp", [128, 24 * (TILE + 3)], BF16)
    s_xbco = sbf("s_xbco", [128, 24 * TILE], BF16)
    s_g = sbf("s_g", [128, 16 * TILE], BF16)
    s_state = sbf("s_state", [128, 32 * 64], BF16)
    s_w = [sbf(f"s_w{i}", [128, 16 * 640], BF16) for i in range(2)]
    s_sq2 = [sbf(f"s_sq{i}", [128, TILE], F32) for i in range(2)]
    s_rows = sbf("s_rows", [1, TILE * 5], F32)
    s_abc = sbf("s_abc", [128, TILE], F32)
    s_bbc = sbf("s_bbc", [128, TILE], F32)
    s_dtraw = sbf("s_dtraw", [32, TILE], F32)
    s_dt = sbf("s_dt", [32, TILE], F32)
    s_dtA = sbf("s_dtA", [32, TILE], F32)
    s_sp1 = sbf("s_sp1", [32, TILE], F32)
    s_scan1 = sbf("s_scan1", [32, 128], F32)
    s_L = sbf("s_L", [32, 128], F32)
    s_erow = sbf("s_erow", [32, 128], BF16)
    s_LtN = sbf("s_LtN", [128, 32], F32)
    s_dttm = sbf("s_dttm", [128, 32], F32)
    s_xtm = sbf("s_xtm", [128, 16 * 128], BF16)
    s_bt = sbf("s_bt", [128, 4 * 128], BF16)
    s_T1 = [sbf(f"s_T1_{i}", [128, 128], F32) for i in range(4)]
    s_mt = [sbf(f"s_mt_{i}", [128, 128], BF16) for i in range(4)]
    s_m2 = [sbf(f"s_m2_{i}", [128, 128], BF16) for i in range(4)]
    s_btsc = [sbf(f"s_btsc_{i}", [128, 128], BF16) for i in range(4)]
    s_eLQ = sbf("s_eLQ", [128, 4], F32)

    dg, ds, pe, ve, se, so = cm.dg, cm.ds, cm.pe, cm.ve, cm.se, cm.so
    dx = cm.dx
    E = ctx.emit

    for dst, src in ((s_idf, c_idf), (s_ohf, c_oh), (s_convw, convw),
                     (s_convb, convb), (s_dtb, dtb), (s_acol, alog),
                     (s_dvec, dvec)):
        E("sync", lambda e, _d=dst, _s=src: e.dma_start(out=_d[:], in_=_s),
          inc=ds, inc_n=16)
    for dst, src in ((s_idb, c_idf), (s_ohb, c_oh), (s_m1, c_m1), (s_m2s, c_m2s)):
        E("gpsimd", lambda e, _d=dst, _s=src: e.dma_start(out=_d[:], in_=_s),
          inc=dg, inc_n=16)
    ds0 = ds.n
    dg0 = dg.n
    E("vector", lambda e: e.memset(s_scan1[:], 1.0), inc=ve)
    E("vector", lambda e: e.memset(s_state[:], 0.0), inc=ve)
    ve_state0 = ve.n
    E("vector", lambda e: e.memset(s_xbcp[:], 0.0), inc=ve)
    ve_init2 = ve.n
    E("scalar", lambda e: e.activation(s_acol[:], s_acol[:], AF.Exp),
      waits=[(ds, ds0)], inc=se)
    ve_acol = E("vector", lambda e: e.tensor_scalar_mul(s_acol[:], s_acol[:], -1.0),
                waits=[(se, se.n)], inc=ve)
    pe_d = E("tensor", lambda e: e.matmul(ps[6][:, 0:32], cm.s_ones_f[:],
                                          s_dvec[0:1, :], start=True, stop=True),
             waits=[(ds, ds0), (ve, cm.ve_init)], inc=pe)
    se_dbc = E("scalar", lambda e: e.activation(s_dbc[:], ps[6][:, 0:32], AF.Copy),
               waits=[(pe, pe_d)], inc=se)

    wbuf_state = {0: (pe, 0), 1: (pe, 0)}
    so_g = 0
    ve_tail_done = 0
    ve_state_prev = ve_state0
    ve_ln_prev = 0
    se_evac = {}

    for t in range(ntiles):
        t0 = t * TILE
        xin = xr.rearrange("(kc p) T -> p kc T", p=128)[:, :, t0:t0 + TILE]
        dg_xt = E("gpsimd", lambda e, _x=xin: e.dma_start(
            out=s_xt[:].rearrange("p (kc T) -> p kc T", kc=NKC), in_=_x),
            waits=[(ve, ve_ln_prev)], inc=cm.dx, inc_n=16)

        ve_ln, pe_lnb = emit_ln(
            ctx, cm, lambda kc: s_xt[:, kc * TILE:(kc + 1) * TILE], NKC,
            lambda kc: s_ln[:, kc * TILE:(kc + 1) * TILE],
            ps[5], ps[6], s_sq2, s_rows, s_abc, s_bbc,
            (cm.dx, dg_xt), [(se, se.n), (ve, ve.n), (pe, pe.n)])
        ve_ln_prev = ve_ln

        if stage == "ln":
            vv = 0
            for kc in range(16):
                vv = E("vector", lambda e, _kc=kc: e.tensor_copy(
                    s_g[:, _kc * TILE:(_kc + 1) * TILE],
                    s_ln[:, _kc * TILE:(_kc + 1) * TILE]),
                    waits=[(ve, ve_ln), (so, so_g)], inc=ve)
            so_g = E("sync", lambda e, _t0=t0: e.dma_start(
                out=g_out.rearrange("(kc p) T -> p kc T", p=128)[:, :, _t0:_t0 + TILE],
                in_=s_g[:].rearrange("p (kc T) -> p kc T", kc=16)),
                waits=[(ve, vv)], inc=so, inc_n=16)
            continue

        def evac_m(j, oc, w, bank, pe_val, _t=t):
            if oc < 24 * 128:
                cch = oc // 128
                dst = s_xbcp[:, cch * (TILE + 3) + 3: cch * (TILE + 3) + 3 + TILE]
                sev = E("scalar", lambda e, _d=dst, _b=bank: e.activation(
                    _d, _b[0:128, :], AF.Copy),
                    waits=[(pe, pe_val), (ve, ve_tail_done)], inc=se)
                se_evac[cch] = sev
            else:
                sev = E("scalar", lambda e, _b=bank: e.activation(
                    s_dtraw[:], _b[0:32, :], AF.Copy),
                    waits=[(pe, pe_val)], inc=se)
                se_evac['dt'] = sev
            return (se, sev)

        emit_proj(ctx, cm, w_proj, 3104, D,
                  lambda kc: s_ln[:, kc * TILE:(kc + 1) * TILE],
                  (ve, ve_ln), evac_m, s_w, [ps[0], ps[1], ps[2], ps[3], ps[4]],
                  wbuf_state, init_free=[(se, se.n), (ve, ve.n)])

        se_conv = {}
        ve_tail = 0
        for cch in range(24):
            base = cch * (TILE + 3)
            acc = s_sq2[cch % 2]
            E("vector", lambda e, _b=base, _a=acc, _c=cch: e.tensor_scalar(
                _a[:], s_xbcp[:, _b:_b + TILE], s_convw[:, _c * 4:_c * 4 + 1],
                s_convb[:, _c:_c + 1], ALU.mult, ALU.add),
                waits=[(se, se_evac[cch]), (se, se_conv.get(cch - 2, 0)),
                       (ds, ds0)], inc=ve)
            for k in (1, 2, 3):
                E("vector", lambda e, _b=base, _a=acc, _c=cch, _k=k: e.scalar_tensor_tensor(
                    _a[:], s_xbcp[:, _b + _k:_b + _k + TILE],
                    s_convw[:, _c * 4 + _k:_c * 4 + _k + 1], _a[:],
                    ALU.mult, ALU.add), inc=ve)
            ve_acc = ve.n
            se_conv[cch] = E("scalar", lambda e, _a=acc, _c=cch: e.activation(
                s_xbco[:, _c * TILE:(_c + 1) * TILE], _a[:], AF.Silu),
                waits=[(ve, ve_acc)], inc=se)
            ve_tail = E("vector", lambda e, _b=base: e.tensor_copy(
                s_xbcp[:, _b:_b + 3], s_xbcp[:, _b + TILE:_b + TILE + 3]), inc=ve)
        ve_tail_done = ve_tail

        E("scalar", lambda e: e.activation(s_sp1[:], s_dtraw[:], AF.Abs,
                                           bias=s_dtb[:]),
          waits=[(se, se_evac['dt'])], inc=se)
        E("scalar", lambda e: e.activation(s_sp1[:], s_sp1[:], AF.Exp, scale=-1.0),
          inc=se)
        se_sp = E("scalar", lambda e: e.activation(s_sp1[:], s_sp1[:], AF.Ln,
                                                   bias=1.0), inc=se)
        E("vector", lambda e: e.tensor_scalar(s_dt[:], s_dtraw[:], s_dtb[:], 0.0,
                                              ALU.add, ALU.max),
          waits=[(se, se_evac['dt'])], inc=ve)
        E("vector", lambda e: e.tensor_tensor(s_dt[:], s_dt[:], s_sp1[:], ALU.add),
          waits=[(se, se_sp)], inc=ve)
        ve_dtA = E("vector", lambda e: e.tensor_scalar(s_dtA[:], s_dt[:], s_acol[:],
                                                       None, ALU.mult),
                   waits=[(ve, ve_acol)], inc=ve)

        if stage == "conv":
            vv = 0
            for kc in range(16):
                vv = E("vector", lambda e, _kc=kc: e.tensor_copy(
                    s_g[:, _kc * TILE:(_kc + 1) * TILE],
                    s_xbco[:, _kc * TILE:(_kc + 1) * TILE]),
                    waits=[(se, se_conv[kc]), (so, so_g)], inc=ve)
            so_g = E("sync", lambda e, _t0=t0: e.dma_start(
                out=g_out.rearrange("(kc p) T -> p kc T", p=128)[:, :, _t0:_t0 + TILE],
                in_=s_g[:].rearrange("p (kc T) -> p kc T", kc=16)),
                waits=[(ve, vv)], inc=so, inc_n=16)
            continue

        ve_state_this = ve_state_prev
        for c in range(4):
            ve_state_this = _scan_chunk(
                ctx, cm, c, s_xbco, s_dt, s_dtA, s_scan1, s_L, s_erow, s_LtN,
                s_dttm, s_xtm, s_bt, s_T1, s_mt, s_m2, s_btsc, s_eLQ, s_state,
                s_idf, s_idb, s_ohf, s_ohb, s_m1, s_m2s, s_dbc, s_g,
                ps, psb, ve_dtA, se_conv, ve_state_this, se_dbc,
                (so, so_g), (ds, ds0), (dg, dg0))
        ve_state_prev = ve_state_this

        if stage == "scan":
            so_g = E("sync", lambda e, _t0=t0: e.dma_start(
                out=g_out.rearrange("(kc p) T -> p kc T", p=128)[:, :, _t0:_t0 + TILE],
                in_=s_g[:].rearrange("p (kc T) -> p kc T", kc=16)),
                waits=[(ve, ve_state_this)], inc=so, inc_n=16)
            continue

        ve_y_done = ve_state_this
        ve_gate = {}

        def evac_gate(j, oc, w, bank, pe_val, _t=t):
            cch = oc // 128
            sil = s_sq2[cch % 2]
            sev = E("scalar", lambda e, _s=sil, _b=bank: e.activation(
                _s[:], _b[0:128, :], AF.Silu),
                waits=[(pe, pe_val), (ve, ve_gate.get(cch - 2, ve_y_done))], inc=se)
            vev = E("vector", lambda e, _s=sil, _c=cch: e.tensor_tensor(
                s_g[:, _c * TILE:(_c + 1) * TILE],
                s_g[:, _c * TILE:(_c + 1) * TILE], _s[:], ALU.mult),
                waits=[(se, sev), (ve, ve_y_done)], inc=ve)
            ve_gate[cch] = vev
            return (ve, vev)

        emit_proj(ctx, cm, w_z, 2048, D,
                  lambda kc: s_ln[:, kc * TILE:(kc + 1) * TILE],
                  (ve, ve_ln), evac_gate, s_w, [ps[0], ps[1], ps[2], ps[3]],
                  wbuf_state, maxg=4, init_free=[(se, se.n), (ve, ve.n)])
        ve_g_done = max(ve_gate.values())

        gdst = (g_store(t) if g_store is not None else
                g_out.rearrange("(kc p) T -> p kc T", p=128)[:, :, t0:t0 + TILE])
        so_g = E("sync", lambda e, _gd=gdst: e.dma_start(
            out=_gd, in_=s_g[:].rearrange("p (kc T) -> p kc T", kc=16)),
            waits=[(ve, ve_g_done)], inc=so, inc_n=16)

    ctx.wait("sync", [(so, so.n)])


def _scan_chunk(ctx, cm, c, s_xbco, s_dt, s_dtA, s_scan1, s_L, s_erow, s_LtN,
                s_dttm, s_xtm, s_bt, s_T1, s_mt, s_m2, s_btsc, s_eLQ, s_state,
                s_idf, s_idb, s_ohf, s_ohb, s_m1, s_m2s, s_dbc, s_g,
                ps, psb, ve_dtA, se_conv, ve_state_prev, se_dbc,
                g_store_done, ds0w, dg0w):
    E = ctx.emit
    pe, ve, se = cm.pe, cm.ve, cm.se
    c0 = c * CH
    fence_se = se.n
    fence_ve = ve.n

    ve_L = E("vector", lambda e: e.tensor_tensor_scan(
        s_L[:], s_scan1[:], s_dtA[:, c0:c0 + CH], 0.0, ALU.mult, ALU.add),
        waits=[(ve, ve_dtA)], inc=ve)
    se_er = E("scalar", lambda e: e.activation(s_erow[:], s_L[:], AF.Exp),
              waits=[(ve, ve_L)], inc=se)

    E("tensor", lambda e: e.transpose(ps[6][:, 0:32], s_L[:], s_idf[0:32, 0:32]),
      waits=[(ve, ve_L), (se, fence_se)] + [ds0w], inc=pe)
    pe_t = E("tensor", lambda e: e.transpose(ps[6][:, 32:64],
                                             s_dt[:, c0:c0 + CH], s_idf[0:32, 0:32]),
             inc=pe)
    E("scalar", lambda e: e.activation(s_LtN[:], ps[6][:, 0:32], AF.Copy,
                                       scale=-1.0),
      waits=[(pe, pe_t)], inc=se)
    se_lt = E("scalar", lambda e: e.activation(s_dttm[:], ps[6][:, 32:64], AF.Copy),
              inc=se)

    items = []
    for i in range(16):
        items.append((s_xbco[:, i * TILE + c0:i * TILE + c0 + CH],
                      s_xtm[:, i * CH:(i + 1) * CH], se_conv[i]))
    for grp in range(4):
        items.append((s_xbco[:, (16 + grp) * TILE + c0:(16 + grp) * TILE + c0 + CH],
                      s_bt[:, grp * CH:(grp + 1) * CH], se_conv[16 + grp]))
    ve_round = fence_ve
    ve_xtm = fence_ve
    for r0 in range(0, 20, 8):
        batch = items[r0:r0 + 8]
        pe_tr = 0
        for sl, (src, dst, sev) in enumerate(batch):
            pe_tr = E("tensor", lambda e, _s=src, _sl=sl: e.transpose(
                psb[:, _sl * CH:(_sl + 1) * CH], _s, s_idb[:]),
                waits=[(cm.se, sev), (ve, ve_round)] + [dg0w], inc=pe)
        for sl, (src, dst, sev) in enumerate(batch):
            ve_xtm = E("vector", lambda e, _d=dst, _sl=sl: e.tensor_copy(
                _d, psb[:, _sl * CH:(_sl + 1) * CH]),
                waits=[(pe, pe_tr)], inc=ve)
        ve_round = ve_xtm

    ve_state_last = ve_state_prev
    ve_upd_prev = fence_ve
    pe_G = 0
    ve_G_users = fence_ve
    seb_done = {0: fence_se, 1: fence_se}
    ve_e_done = fence_ve
    se_q_prev = fence_se
    ve_q_users = fence_ve
    ve_t1_users = [fence_ve] * 4
    for hh in range(8):
        grp = hh // 2
        bbank = ps[1] if hh % 2 == 0 else ps[6]
        if hh % 2 == 0:
            pe_G = E("tensor", lambda e, _g=grp, _c0=c0: e.matmul(
                ps[0][:, 0:128],
                s_xbco[:, (16 + _g) * TILE + _c0:(16 + _g) * TILE + _c0 + CH],
                s_xbco[:, (20 + _g) * TILE + _c0:(20 + _g) * TILE + _c0 + CH],
                start=True, stop=True),
                waits=[(cm.se, se_conv[16 + grp]), (cm.se, se_conv[20 + grp]),
                       (ve, ve_G_users)], inc=pe)
        bwait = seb_done[hh % 2] if hh % 2 == 0 else max(seb_done[1], se_lt)
        pe_b = 0
        for hl in range(4):
            h = hh * 4 + hl
            E("tensor", lambda e, _h=h, _hl=hl, _bk=bbank: e.matmul(
                _bk[:, _hl * 128:(_hl + 1) * 128], s_ohf[:, _h * 128:(_h + 1) * 128],
                s_L[:], start=True, stop=False),
                waits=[(ve, ve_L), (se, bwait)], inc=None)
            pe_b = E("tensor", lambda e, _hl=hl, _bk=bbank: e.matmul(
                _bk[:, _hl * 128:(_hl + 1) * 128], s_m1[:], s_m2s[:],
                start=False, stop=True), inc=pe)
        pe_e = 0
        for hl in range(4):
            h = hh * 4 + hl
            pe_e = E("tensor", lambda e, _h=h, _hl=hl: e.matmul(
                ps[2][:, _hl * 128:(_hl + 1) * 128],
                s_ohb[:, _h * 128:(_h + 1) * 128], s_erow[:],
                start=True, stop=True),
                waits=[(se, se_er), (se, se_q_prev), (ve, ve_e_done)], inc=pe)
        se_q = E("scalar", lambda e: e.activation(
            s_eLQ[:, 0:4].rearrange("p (h i) -> p h i", i=1),
            ps[2][:].rearrange("p (h i) -> p h i", h=4)[:, :, 127:128],
            AF.Copy), waits=[(pe, pe_e), (ve, ve_q_users)], inc=se)
        se_q_prev = se_q
        se_T1 = 0
        for hl in range(4):
            h = hh * 4 + hl
            se_T1 = E("scalar", lambda e, _hl=hl, _h=h, _bk=bbank: e.activation(
                s_T1[_hl][:], _bk[:, _hl * 128:(_hl + 1) * 128], AF.Exp,
                bias=s_LtN[:, _h:_h + 1]),
                waits=[(pe, pe_b), (se, se_lt), (ve, ve_t1_users[hl])], inc=se)
        seb_done[hh % 2] = se_T1
        ve_prep = 0
        for hl in range(4):
            h = hh * 4 + hl
            E("vector", lambda e, _hl=hl, _h=h: e.scalar_tensor_tensor(
                s_mt[_hl][:], s_T1[_hl][:], s_dttm[:, _h:_h + 1], ps[0][:, 0:128],
                ALU.mult, ALU.mult),
                waits=[(se, se_T1), (pe, pe_G), (ve, ve_xtm)], inc=ve)
            E("vector", lambda e, _hl=hl, _g=grp, _c0=c0: e.tensor_tensor(
                s_m2[_hl][:],
                s_xbco[:, (20 + _g) * TILE + _c0:(20 + _g) * TILE + _c0 + CH],
                ps[2][:, _hl * 128:(_hl + 1) * 128], ALU.mult),
                waits=[(pe, pe_e)], inc=ve)
            ve_prep = E("vector", lambda e, _hl=hl, _g=grp, _h=h: e.tensor_scalar(
                s_btsc[_hl][:], s_bt[:, _g * 128:(_g + 1) * 128],
                s_T1[_hl][:, 127:128], s_dttm[:, _h:_h + 1], ALU.mult, ALU.mult),
                inc=ve)
        ve_G_users = ve_prep
        ve_e_done = ve_prep
        ybank = ps[3 + (hh % 2)]
        pe_y = 0
        for hl in range(4):
            h = hh * 4 + hl
            ybase = (h % 2) * 64
            tp = (0, ybase)
            E("tensor", lambda e, _bk=ybank, _yb=ybase, _hl=hl,
              _x=(h // 2) * 128 + (h % 2) * 64, _tp=tp: e.matmul(
                _bk[_yb:_yb + 64, _hl * 128:(_hl + 1) * 128],
                s_xtm[:, _x:_x + 64], s_mt[_hl][:], start=True, stop=False,
                tile_position=_tp),
                waits=[(ve, ve_prep), (ve, ve_state_prev)], inc=None)
            pe_y = E("tensor", lambda e, _bk=ybank, _yb=ybase, _hl=hl, _h=h,
                     _tp=tp: e.matmul(
                _bk[_yb:_yb + 64, _hl * 128:(_hl + 1) * 128],
                s_state[:, _h * 64:(_h + 1) * 64], s_m2[_hl][:],
                start=False, stop=True, tile_position=_tp), inc=pe)
        for hl in range(4):
            h = hh * 4 + hl
            ybase = (h % 2) * 64
            gdst = s_g[ybase:ybase + 64,
                       (h // 2) * TILE + c0:(h // 2) * TILE + c0 + CH]
            xfm = s_xbco[ybase:ybase + 64,
                         (h // 2) * TILE + c0:(h // 2) * TILE + c0 + CH]
            E("vector", lambda e, _gd=gdst, _xf=xfm, _h=h, _bk=ybank, _yb=ybase,
              _hl=hl: e.scalar_tensor_tensor(
                _gd, _xf, s_dbc[_yb:_yb + 64, _h:_h + 1],
                _bk[_yb:_yb + 64, _hl * 128:(_hl + 1) * 128], ALU.mult, ALU.add),
                waits=[(pe, pe_y), (se, se_dbc)] + [g_store_done], inc=ve)
        pe_S = 0
        for hl in range(4):
            pe_S = E("tensor", lambda e, _hl=hl,
                     _x=((hh * 4 + hl) // 2) * 128 + ((hh * 4 + hl) % 2) * 64: e.matmul(
                ps[5][:, _hl * 64:(_hl + 1) * 64], s_btsc[_hl][:],
                s_xtm[:, _x:_x + 64], start=True, stop=True),
                waits=[(ve, ve_prep), (ve, ve_upd_prev)], inc=pe)
        for hl in range(4):
            h = hh * 4 + hl
            ve_state_last = E("vector", lambda e, _h=h, _hl=hl: e.scalar_tensor_tensor(
                s_state[:, _h * 64:(_h + 1) * 64], s_state[:, _h * 64:(_h + 1) * 64],
                s_eLQ[:, _hl:_hl + 1], ps[5][:, _hl * 64:(_hl + 1) * 64],
                ALU.mult, ALU.add),
                waits=[(pe, pe_S), (pe, pe_y), (se, se_q)], inc=ve)
        ve_upd_prev = ve_state_last
        ve_q_users = ve_state_last
        for hl in range(4):
            ve_t1_users[hl] = ve_prep
        ve_t1_users = [ve_prep] * 4
        seb_pe_guard = pe_y
    return ve_state_last



F0_INPUTS = ["g_in", "xr", "nw", "w_out", "gw0T", "w13", "w2T", "wsgu",
             "wsdT", "sgwT", "c_idf", "c_oh16"]
F0_SHAPES = {
    "g_in": ([4096, 2048], BF16, "ExternalInput"),
    "xr": ([D, 2048], F32, "ExternalInput"),
    "nw": ([128, 32], F32, "ExternalInput"),
    "w_out": ([128, 32 * 2048], F32, "ExternalInput"),
    "gw0T": ([128, 16 * 16], F32, "ExternalInput"),
    "w13": ([16, 128, 16 * 1024], F32, "ExternalInput"),
    "w2T": ([16, 128, 4 * 2048], F32, "ExternalInput"),
    "wsgu": ([128, 16 * 4096], F32, "ExternalInput"),
    "wsdT": ([128, 16 * 2048], F32, "ExternalInput"),
    "sgwT": ([128, 16], F32, "ExternalInput"),
    "c_idf": ([128, 128], F32, "ExternalInput"),
    "c_oh16": ([16, 16 * 128], F32, "ExternalInput"),
    "r_out": ([D, 2048], BF16, "ExternalOutput"),
}


def build_phase_m2f0(stage="full"):
    nc = bass.Bass("TRN2", target_bir_lowering=False, detect_race_conditions=False)
    es = contextlib.ExitStack()
    ctx = Ctx(nc, es)
    with es:
        io = dram_io(nc, F0_INPUTS + ["r_out"], F0_SHAPES)
        cpool = nc.dram_tensor("cpool", [128, max(4, len(CONSTS))], F32,
                               kind="ExternalInput").ap()
        cm = Common(ctx, cpool)
        ps = [ctx.psum(f"ps{i}", [128, 512], F32) for i in range(7)]
        _emit_phase_m2f0(ctx, cm, ps, io, stage)
        finish_build(nc, ctx)
    return nc


def _emit_phase_m2f0(ctx, cm, ps, io, stage="full", xr_bf16=False,
                     rs_mode=False, r_tiled=False):
    nc = ctx.nc
    NT = 2048
    g_in = io.get("g_in")
    xr = io["xr"]
    nw = io["nw"]
    w_out = io.get("w_out")
    gw0T = io["gw0T"]
    w13 = io["w13"]
    w2T = io["w2T"]
    wsgu = io["wsgu"]
    wsdT = io["wsdT"]
    sgwT = io["sgwT"]
    c_idf = io["c_idf"]
    c_oh16 = io["c_oh16"]
    r_out = io["r_out"]
    sbf = ctx.sbuf
    s_rt = sbf("s_rt", [128, 16 * NT], BF16)
    s_lnf = sbf("s_lnf", [128, 16 * NT], BF16)
    s_hmid = sbf("s_hmid", [128, 4 * NT], BF16)
    s_sh = s_hmid
    s_w = [sbf(f"s_w{i}", [128, 32 * 128], BF16) for i in range(2)]
    s_gb = [sbf(f"s_gb{i}", [128, NT], BF16) for i in range(2)]
    s_idf = sbf("s_idf", [128, 128], F32)
    s_oh16 = sbf("s_oh16", [16, 16 * 128], BF16)
    s_gw0 = sbf("s_gw0", [128, 16 * 16], BF16)
    s_sgw = sbf("s_sgw", [128, 16], BF16)
    s_nw = sbf("s_nw", [128, 32], F32)
    s_rows = sbf("s_rows", [1, TILE * 5], F32)
    s_sq2 = [sbf(f"s_sq{i}", [128, TILE], F32) for i in range(2)]
    s_abc = sbf("s_abc", [128, TILE], F32)
    s_bbc = sbf("s_bbc", [128, TILE], F32)
    s_rstd = sbf("s_rstd", [128, 4 * TILE], BF16)
    s_ev = [sbf(f"s_ev{i}", [128, TILE], F32) for i in range(2)]
    s_sm = sbf("s_sm", [128, 8], F32)
    s_e16 = sbf("s_e16", [128, 16], F32)
    s_e16b = sbf("s_e16b", [128, 16], F32)
    s_cmb = sbf("s_cmb", [128, 16], F32)
    s_CT = sbf("s_CT", [16, NT], BF16)
    s_cbc = s_gb[0]
    s_sgbc = s_gb[1]

    dsr = [ctx.sem("dr0"), ctx.sem("dr1")]
    dg, ds, pe, ve, se, so = cm.dg, cm.ds, cm.pe, cm.ve, cm.se, cm.so
    E = ctx.emit

    for dst, src in ((s_idf, c_idf), (s_nw, nw)):
        E("sync", lambda e, _d=dst, _s=src: e.dma_start(out=_d[:], in_=_s),
          inc=ds, inc_n=16)
    for dst, src in ((s_gw0, gw0T), (s_sgw, sgwT), (s_oh16, c_oh16)):
        E("gpsimd", lambda e, _d=dst, _s=src: e.dma_start(out=_d[:], in_=_s),
          inc=dg, inc_n=16)
    ds0, dg0 = ds.n, dg.n
    _rteng = "sync" if xr_bf16 else "gpsimd"
    dg_rt = E(_rteng, lambda e: e.dma_start(
        out=s_rt[:].rearrange("p (kc T) -> p kc T", kc=16),
        in_=xr.rearrange("(kc p) T -> p kc T", p=128)), inc=cm.dx, inc_n=16)

    wbuf_state = {0: (pe, 0), 1: (pe, 0)}
    if rs_mode:
        ssq_rs = io["ssq_rs"]
        wop_rs = io["wop_rs"]
        drs = ctx.sem("drs")
        d1 = E("sync", lambda e: e.dma_start(out=s_rows[0:1, 0:2048],
                                             in_=ssq_rs), inc=drs, inc_n=16)
        se_r2 = 0
        for tt in range(4):
            r = s_rows[0:1, tt * TILE:(tt + 1) * TILE]
            E("scalar", lambda e, _r=r: e.activation(
                _r, _r, AF.Ln, scale=1.0 / 4096.0, bias=EPS),
                waits=[(drs, d1)], inc=se)
            se_r = E("scalar", lambda e, _r=r: e.activation(
                _r, _r, AF.Exp, scale=-0.5), inc=se)
            pe_bc = E("tensor", lambda e, _r=r: e.matmul(
                ps[4][:, :], cm.s_ones_f[:], _r, start=True, stop=True),
                waits=[(se, se_r), (se, se_r2), (ve, cm.ve_init)], inc=pe)
            se_r2 = E("scalar", lambda e, _tt=tt: e.activation(
                s_rstd[:, _tt * TILE:(_tt + 1) * TILE], ps[4][:, :], AF.Copy),
                waits=[(pe, pe_bc)], inc=se)
        se_rstd = se_r2
        wsrc = wop_rs.rearrange("(kc p) T -> p kc T", p=128)
        ve_wout = 0
        dwo = {0: 0, 1: 0}
        for kc in range(16):
            par = kc % 2
            dmw = E("sync", lambda e, _kc=kc, _p=par: e.dma_start(
                out=s_gb[_p][:], in_=wsrc[:, _kc, :]),
                waits=[(ve, dwo[par])], inc=dsr[par], inc_n=16)
            E("vector", lambda e, _p=par: e.tensor_tensor(
                s_gb[_p][:], s_gb[_p][:], s_rstd[:, :], ALU.mult),
                waits=[(dsr[par], dmw), (se, se_rstd)], inc=ve)
            ve_wout = E("vector", lambda e, _kc=kc, _p=par: e.tensor_tensor(
                s_rt[:, _kc * NT:(_kc + 1) * NT], s_rt[:, _kc * NT:(_kc + 1) * NT],
                s_gb[_p][:], ALU.add),
                waits=[(cm.dx, dg_rt)], inc=ve)
            dwo[par] = ve_wout
    else:
        gsrc = g_in.rearrange("(kc p) T -> p kc T", p=128)
        gstream = RhsStream(ctx, cm, "g", lambda kc: gsrc[:, kc, :], s_gb, dsr, False)

        se_sq = {0: 0, 1: 0}
        pe_mm = {0: 0, 1: 0}
        pe_sqmm = {0: 0, 1: 0}
        for kc in range(32):
            par = kc % 2
            rr = gstream.load(kc, [(pe, pe_mm[par])])
            for tt in range(4):
                sq = s_sq2[tt % 2]
                ssq = E("scalar", lambda e, _p=par, _tt=tt, _q=sq: e.activation(
                    _q[:], s_gb[_p][:, _tt * TILE:(_tt + 1) * TILE], AF.Square),
                    waits=[rr, (pe, pe_sqmm[tt % 2])], inc=se)
                pe_mm[par] = E("tensor", lambda e, _q=sq, _tt=tt, _f=(kc == 0),
                               _l=(kc == 31): e.matmul(
                    ps[_tt][0:1, :], cm.s_onescol_f[:], _q[:], start=_f, stop=_l),
                    waits=[(se, ssq), (ve, cm.ve_init)], inc=pe)
                pe_sqmm[tt % 2] = pe_mm[par]
        pe_ss = pe.n
        se_r = 0
        for tt in range(4):
            r = s_rows[0:1, tt * TILE:(tt + 1) * TILE]
            E("scalar", lambda e, _r=r, _tt=tt: e.activation(
                _r, ps[_tt][0:1, :], AF.Ln, scale=1.0 / 4096.0, bias=EPS),
                waits=[(pe, pe_ss)], inc=se)
            se_r = E("scalar", lambda e, _r=r: e.activation(_r, _r, AF.Exp, scale=-0.5),
                     inc=se)
        pe_bc = 0
        se_r2 = 0
        for tt in range(4):
            r = s_rows[0:1, tt * TILE:(tt + 1) * TILE]
            pe_bc = E("tensor", lambda e, _r=r: e.matmul(
                ps[4][:, :], cm.s_ones_f[:], _r, start=True, stop=True),
                waits=[(se, se_r), (se, se_r2)], inc=pe)
            se_r2 = E("scalar", lambda e, _tt=tt: e.activation(
                s_rstd[:, _tt * TILE:(_tt + 1) * TILE], ps[4][:, :], AF.Copy),
                waits=[(pe, pe_bc)], inc=se)
        se_rstd = se_r2
        gstream.reset()

        def wout_after_load(buf, grp, gi, ow, nkf, wready):
            vv = 0
            for kc in range(nkf):
                vv = E("vector", lambda e, _b=buf, _kc=kc, _ow=ow: e.tensor_scalar(
                    _b[:, _kc * _ow:(_kc + 1) * _ow], _b[:, _kc * _ow:(_kc + 1) * _ow],
                    s_nw[:, _kc:_kc + 1], None, ALU.mult),
                    waits=[wready, (ds, ds0)], inc=ve)
            return (ve, vv)

        ve_wout = 0

        def wout_evac(grp, gi, pe_val):
            nonlocal ve_wout
            marks = {}
            oc = grp[0][0]
            for tt in range(4):
                E("vector", lambda e, _tt=tt: e.tensor_tensor(
                    s_ev[_tt % 2][:], ps[_tt][:, :],
                    s_rstd[:, _tt * TILE:(_tt + 1) * TILE], ALU.mult),
                    waits=[(pe, pe_val), (se, se_rstd)], inc=ve)
                dst = s_rt[:, (oc // 128) * NT + tt * TILE:
                           (oc // 128) * NT + (tt + 1) * TILE]
                ve_wout = E("vector", lambda e, _d=dst, _tt=tt: e.tensor_tensor(
                    _d, _d, s_ev[_tt % 2][:], ALU.add),
                    waits=[(cm.dx, dg_rt)], inc=ve)
                marks[(0, tt)] = (ve, ve_wout)
            return marks

        emit_projmt(ctx, cm, w_out, 4096, [[(o, 128)] for o in range(0, 2048, 128)],
                    None, None, wout_evac, s_w, wbuf_state,
                    lambda j, tt: ps[tt], 4, init_free=[(se, se_rstd)],
                    after_load=wout_after_load, stream=gstream)

        if stage == "wout":
            so_v = E("sync", lambda e: e.dma_start(
                out=r_out.rearrange("(kc p) T -> p kc T", p=128),
                in_=s_rt[:].rearrange("p (kc T) -> p kc T", kc=16)),
                waits=[(ve, ve_wout)], inc=so, inc_n=16)
            ctx.wait("sync", [(so, so.n)])
            return

    ve_ln = 0
    for tt in range(4):
        ve_ln, _ = emit_ln(
            ctx, cm, lambda kc, _tt=tt: s_rt[:, kc * NT + _tt * TILE:
                                             kc * NT + (_tt + 1) * TILE], 16,
            lambda kc, _tt=tt: s_lnf[:, kc * NT + _tt * TILE:
                                     kc * NT + (_tt + 1) * TILE],
            ps[5], ps[6], s_sq2, s_rows, s_abc, s_bbc,
            (ve, ve_wout), [(se, se.n), (ve, ve.n), (pe, pe.n)])

    if stage == "ln":
        so_v = E("sync", lambda e: e.dma_start(
            out=r_out.rearrange("(kc p) T -> p kc T", p=128),
            in_=s_lnf[:].rearrange("p (kc T) -> p kc T", kc=16)),
            waits=[(ve, ve_ln)], inc=so, inc_n=16)
        ctx.wait("sync", [(so, so.n)])
        return

    se_ct = 0
    ve_p4_users = 0
    se_p4_users = 0
    for tt in range(4):
        for tsub in range(4):
            pe_rl = 0
            for kc in range(16):
                off = kc * NT + tt * TILE + tsub * 128
                pe_rl = E("tensor", lambda e, _o=off, _kc=kc, _f=(kc == 0),
                          _l=(kc == 15): e.matmul(
                    ps[4][:, 0:16], s_lnf[:, _o:_o + 128],
                    s_gw0[:, _kc * 16:(_kc + 1) * 16], start=_f, stop=_l),
                    waits=[(ve, ve_ln), (dg, dg0),
                           (ve, ve_p4_users), (se, se_p4_users)], inc=pe)
            def VB(fn, waits=()):
                vv = E("vector", fn, waits=waits, inc=ve)
                ctx.wait("vector", [(ve, vv)])
                return vv
            VB(lambda e: e.reduce_max(s_sm[:, 0:1], ps[4][:, 0:16], axis=AX.X),
               waits=[(pe, pe_rl)])
            ve_neg = VB(lambda e: e.tensor_scalar_mul(s_sm[:, 1:2], s_sm[:, 0:1],
                                                      -1.0))
            se_e = E("scalar", lambda e: e.activation(
                s_e16[:], ps[4][:, 0:16], AF.Exp, bias=s_sm[:, 1:2]),
                waits=[(ve, ve_neg)], inc=se)
            VB(lambda e: e.tensor_scalar(s_cmb[:], ps[4][:, 0:16], s_sm[:, 0:1],
                                         None, ALU.subtract))
            VB(lambda e: e.tensor_scalar(s_cmb[:], s_cmb[:],
                                         cm.s_cpool[:, 4:5], cm.s_cpool[:, 0:1],
                                         ALU.mult, ALU.min))
            VB(lambda e: e.tensor_scalar(s_cmb[:], s_cmb[:], cm.s_cpool[:, 7:8],
                                         None, ALU.add))
            VB(lambda e: e.scalar_tensor_tensor(s_e16b[:], s_cmb[:],
                                                cm.s_cpool[:, 6:7], ps[4][:, 0:16],
                                                ALU.mult, ALU.add))
            VB(lambda e: e.reduce_max(s_sm[:, 2:3], s_e16b[:], axis=AX.X))
            VB(lambda e: e.tensor_scalar(s_cmb[:], ps[4][:, 0:16], s_sm[:, 2:3],
                                         None, ALU.subtract))
            VB(lambda e: e.tensor_scalar(s_cmb[:], s_cmb[:],
                                         cm.s_cpool[:, 4:5], cm.s_cpool[:, 5:6],
                                         ALU.mult, ALU.max))
            VB(lambda e: e.tensor_scalar(s_cmb[:], s_cmb[:],
                                         cm.s_cpool[:, 0:1], cm.s_cpool[:, 7:8],
                                         ALU.min, ALU.add))
            VB(lambda e: e.tensor_scalar(s_cmb[:], s_cmb[:], cm.s_cpool[:, 7:8],
                                         None, ALU.mult))
            VB(lambda e: e.tensor_tensor(s_cmb[:], s_cmb[:], s_e16[:], ALU.mult),
               waits=[(se, se_e)])
            VB(lambda e: e.tensor_reduce(s_sm[:, 3:4], s_cmb[:], axis=AX.X,
                                         op=ALU.add))
            VB(lambda e: e.reciprocal(s_sm[:, 4:5], s_sm[:, 3:4]))
            ve_c = VB(lambda e: e.tensor_scalar(s_cmb[:], s_cmb[:], s_sm[:, 4:5],
                                                None, ALU.mult))
            ve_p4_users = ve_c
            se_p4_users = se_e
            pe_tc = E("tensor", lambda e: e.transpose(
                ps[5][0:16, 0:128], s_cmb[:], s_idf[:]),
                waits=[(ve, ve_c), (se, se_ct), (ds, ds0)], inc=pe)
            se_ct = E("scalar", lambda e, _o=tt * TILE + tsub * 128: e.activation(
                s_CT[0:16, _o:_o + 128], ps[5][0:16, 0:128], AF.Copy),
                waits=[(pe, pe_tc)], inc=se)

    if stage == "router":
        so_v = E("sync", lambda e: e.dma_start(
            out=r_out[0:16, :], in_=s_CT[0:16, :]),
            waits=[(se, se_ct)], inc=so, inc_n=16)
        ctx.wait("sync", [(so, so.n)])
        return

    ve_hmid_last = 0
    se_cbc_users = 0
    for ei in range(16):
        se_cbc = se_cbc_users
        pe_cb = 0
        for tt in range(4):
            pe_cb = E("tensor", lambda e, _e=ei, _tt=tt: e.matmul(
                ps[6][:, :], s_oh16[:, _e * 128:(_e + 1) * 128],
                s_CT[0:16, _tt * TILE:(_tt + 1) * TILE], start=True, stop=True),
                waits=[(se, se_ct), (ve, ve_hmid_last), (se, se_cbc)],
                inc=pe)
            se_cbc = E("scalar", lambda e, _tt=tt: e.activation(
                s_cbc[:, _tt * TILE:(_tt + 1) * TILE], ps[6][:, :], AF.Copy),
                waits=[(pe, pe_cb)], inc=se)
        se_cbc_users = se_cbc

        w13g = [[(j * 256, 128), (j * 256 + 128, 128)] for j in range(4)]
        for tth in range(2):
            def ev13(grp, gi, pe_val, _tth=tth, _e=ei):
                nonlocal ve_hmid_last
                marks = {}
                j = grp[0][0] // 256
                for tl in range(2):
                    tt = _tth * 2 + tl
                    sil = s_ev[tl]
                    sev = E("scalar", lambda e, _s=sil, _tl=tl: e.activation(
                        _s[:], ps[_tl * 2][:, :], AF.Silu),
                        waits=[(pe, pe_val)], inc=se)
                    E("vector", lambda e, _s=sil, _tl=tl: e.tensor_tensor(
                        _s[:], _s[:], ps[_tl * 2 + 1][:, :], ALU.mult),
                        waits=[(se, sev)], inc=ve)
                    dst = s_hmid[:, j * NT + tt * TILE: j * NT + (tt + 1) * TILE]
                    vh = E("vector", lambda e, _d=dst, _s=sil, _tt=tt: e.tensor_tensor(
                        _d, _s[:], s_cbc[:, _tt * TILE:(_tt + 1) * TILE], ALU.mult),
                        waits=[(se, se_cbc)], inc=ve)
                    marks[(0, tl)] = (ve, vh)
                    marks[(1, tl)] = (ve, vh)
                    ve_hmid_last = vh
                return marks

            emit_projmt(ctx, cm, w13[ei], D, w13g,
                        lambda kc, tl, _tth=tth: s_lnf[:, kc * NT + (_tth * 2 + tl)
                                                       * TILE: kc * NT +
                                                       (_tth * 2 + tl + 1) * TILE],
                        (ve, ve_ln), ev13, s_w, wbuf_state,
                        lambda j, tl: ps[tl * 2 + j], 2,
                        init_free=[(ve, ve.n), (se, se.n)])

        def ev2(grp, gi, pe_val, _e=ei):
            marks = {}
            oc = grp[0][0]
            for tt in range(4):
                dst = s_rt[:, (oc // 128) * NT + tt * TILE:
                           (oc // 128) * NT + (tt + 1) * TILE]
                vv = E("vector", lambda e, _d=dst, _tt=tt: e.tensor_tensor(
                    _d, _d, ps[_tt][:, :], ALU.add),
                    waits=[(pe, pe_val), (ve, ve_ln)], inc=ve)
                marks[(0, tt)] = (ve, vv)
            return marks

        emit_projmt(ctx, cm, w2T[ei], 512, [[(o, 128)] for o in range(0, 2048, 128)],
                    lambda kc, tt: s_hmid[:, kc * NT + tt * TILE:
                                          kc * NT + (tt + 1) * TILE],
                    (ve, ve_hmid_last), ev2, s_w, wbuf_state,
                    lambda j, tt: ps[tt], 4, init_free=[(ve, ve.n), (se, se.n)])

    if stage == "moe":
        so_v = E("sync", lambda e: e.dma_start(
            out=r_out.rearrange("(kc p) T -> p kc T", p=128),
            in_=s_rt[:].rearrange("p (kc T) -> p kc T", kc=16)),
            waits=[(ve, ve.n)], inc=so, inc_n=16)
        ctx.wait("sync", [(so, so.n)])
        return

    se_sg = se.n
    ve_sg_guard = ve.n
    for tt in range(4):
        pe_sg = 0
        for kc in range(16):
            pe_sg = E("tensor", lambda e, _kc=kc, _tt=tt, _f=(kc == 0),
                      _l=(kc == 15): e.matmul(
                ps[4][0:1, :], s_sgw[:, _kc:_kc + 1],
                s_lnf[:, _kc * NT + _tt * TILE:_kc * NT + (_tt + 1) * TILE],
                start=_f, stop=_l),
                waits=[(ve, ve_ln), (dg, dg0), (se, se_sg), (ve, ve_sg_guard)],
                inc=pe)
        r = s_rows[0:1, 0:TILE]
        E("scalar", lambda e, _r=r: e.activation(_r, ps[4][0:1, :], AF.Sigmoid),
          waits=[(pe, pe_sg)], inc=se)
        pe_sgb = E("tensor", lambda e, _r=r: e.matmul(
            ps[5][:, :], cm.s_ones_f[:], _r, start=True, stop=True),
            waits=[(se, se.n)], inc=pe)
        se_sg = E("scalar", lambda e, _tt=tt: e.activation(
            s_sgbc[:, _tt * TILE:(_tt + 1) * TILE], ps[5][:, :], AF.Copy),
            waits=[(pe, pe_sgb)], inc=se)

    for blk in range(8):
        sgug = [[(blk * 512 + j * 256, 128), (blk * 512 + j * 256 + 128, 128)]
                for j in range(2)]
        for tth in range(2):
            def evsh(grp, gi, pe_val, _tth=tth, _blk=blk):
                marks = {}
                j = (grp[0][0] - _blk * 512) // 256
                for tl in range(2):
                    tt = _tth * 2 + tl
                    sil = s_ev[tl]
                    sev = E("scalar", lambda e, _s=sil, _tl=tl: e.activation(
                        _s[:], ps[_tl * 2][:, :], AF.Silu),
                        waits=[(pe, pe_val)], inc=se)
                    dst = s_sh[:, j * (4 * TILE) + tt * TILE:
                               j * (4 * TILE) + (tt + 1) * TILE]
                    vh = E("vector", lambda e, _d=dst, _s=sil, _tl=tl: e.tensor_tensor(
                        _d, _s[:], ps[_tl * 2 + 1][:, :], ALU.mult),
                        waits=[(se, sev)], inc=ve)
                    marks[(0, tl)] = (ve, vh)
                    marks[(1, tl)] = (ve, vh)
                return marks

            emit_projmt(ctx, cm, wsgu, D, sgug,
                        lambda kc, tl, _tth=tth: s_lnf[:, kc * NT + (_tth * 2 + tl)
                                                       * TILE: kc * NT +
                                                       (_tth * 2 + tl + 1) * TILE],
                        (ve, ve_ln), evsh, s_w, wbuf_state,
                        lambda j, tl: ps[tl * 2 + j], 2,
                        init_free=[(ve, ve.n), (se, se.n)])

        ve_sh = ve.n

        def evsd(grp, gi, pe_val, _blk=blk):
            marks = {}
            oc = grp[0][0]
            for tt in range(4):
                E("vector", lambda e, _tt=tt: e.tensor_tensor(
                    s_ev[_tt % 2][:], ps[_tt][:, :],
                    s_sgbc[:, _tt * TILE:(_tt + 1) * TILE], ALU.mult),
                    waits=[(pe, pe_val), (se, se_sg)], inc=ve)
                dst = s_rt[:, (oc // 128) * NT + tt * TILE:
                           (oc // 128) * NT + (tt + 1) * TILE]
                vv = E("vector", lambda e, _d=dst, _tt=tt: e.tensor_tensor(
                    _d, _d, s_ev[_tt % 2][:], ALU.add), inc=ve)
                marks[(0, tt)] = (ve, vv)
            return marks

        emit_projmt(ctx, cm, wsdT[:, blk * 4096:(blk + 1) * 4096], 256,
                    [[(o, 128)] for o in range(0, 2048, 128)],
                    lambda kc, tt, _blk=blk: s_sh[:, kc * (4 * TILE) + tt * TILE:
                                                  kc * (4 * TILE) + (tt + 1) * TILE],
                    (ve, ve_sh), evsd, s_w, wbuf_state,
                    lambda j, tt: ps[tt], 4, init_free=[(ve, ve.n), (se, se.n)])

    ve_fin = ve.n
    if r_tiled:
        for tt in range(4):
            so_v = E("sync", lambda e, _tt=tt: e.dma_start(
                out=r_out[:, _tt * 8192:(_tt + 1) * 8192].rearrange(
                    "p (kc T) -> p kc T", kc=16),
                in_=s_rt[:].rearrange("p (kc tt T) -> p tt kc T",
                                      tt=4, T=TILE)[:, _tt, :, :]),
                waits=[(ve, ve_fin)], inc=so, inc_n=16)
    else:
        so_v = E("sync", lambda e: e.dma_start(
            out=r_out.rearrange("(kc p) T -> p kc T", p=128),
            in_=s_rt[:].rearrange("p (kc T) -> p kc T", kc=16)),
            waits=[(ve, ve_fin)], inc=so, inc_n=16)
    ctx.wait("sync", [(so, so.n)])


def host_prep_m2f0(inputs, core, g_full):
    b, hf = core // 2, core % 2
    c_idf, _, _, _ = host_consts()
    c_oh16 = np.zeros((16, 16 * 128), np.float32)
    for h in range(16):
        c_oh16[h, h * 128:(h + 1) * 128] = 1.0
    own_tok = np.concatenate([np.arange((2 * s + hf) * TILE, (2 * s + hf + 1) * TILE)
                              for s in range(4)])
    nw_ = np.ascontiguousarray(
        np.asarray(inputs["L0_attn_norm_w"], np.float32).reshape(32, 128).T)
    w1 = np.asarray(inputs["L0_ffn_w1"], np.float32)
    w3 = np.asarray(inputs["L0_ffn_w3"], np.float32)
    w2 = np.asarray(inputs["L0_ffn_w2"], np.float32)
    w13 = np.empty((16, 2048, 1024), np.float32)
    for j in range(4):
        w13[:, :, j * 256:j * 256 + 128] = w1[:, j * 128:(j + 1) * 128, :].transpose(0, 2, 1)
        w13[:, :, j * 256 + 128:j * 256 + 256] = w3[:, j * 128:(j + 1) * 128, :].transpose(0, 2, 1)
    oc16 = [[(o, 128)] for o in range(0, 2048, 128)]
    w13g = [[(j * 256, 128), (j * 256 + 128, 128)] for j in range(4)]
    w13 = np.stack([tile_w(w13[e], w13g) for e in range(16)])
    w2T = np.stack([tile_w(w2.transpose(0, 2, 1)[e], oc16) for e in range(16)])
    wsg = np.asarray(inputs["L0_ffn_ws_gate"], np.float32)
    wsu = np.asarray(inputs["L0_ffn_ws_up"], np.float32)
    wsgu = np.empty((2048, 4096), np.float32)
    for j in range(16):
        wsgu[:, j * 256:j * 256 + 128] = wsg[j * 128:(j + 1) * 128, :].T
        wsgu[:, j * 256 + 128:j * 256 + 256] = wsu[j * 128:(j + 1) * 128, :].T
    sgug = [[(blk * 512 + j * 256, 128), (blk * 512 + j * 256 + 128, 128)]
            for blk in range(8) for j in range(2)]
    wsgu = tile_w(wsgu, sgug)
    wsd = np.asarray(inputs["L0_ffn_ws_down"], np.float32).T
    wsdT = np.concatenate([tile_w(wsd[blk * 256:(blk + 1) * 256], oc16)
                           for blk in range(8)], axis=1)
    gw0 = np.asarray(inputs["L0_ffn_gate_w"], np.float32)
    gw0T = np.ascontiguousarray(
        gw0.T.reshape(16, 128, 16).transpose(1, 0, 2).reshape(128, 256))
    sgw = np.asarray(inputs["L0_ffn_shared_gate_w"], np.float32)
    sgwT = np.ascontiguousarray(sgw.reshape(16, 128).T)
    ret = {
        "nw": nw_,
        "gw0T": gw0T, "w13": w13, "w2T": w2T,
        "wsgu": wsgu, "wsdT": wsdT, "sgwT": sgwT,
        "c_idf": c_idf, "c_oh16": c_oh16, "cpool": make_cpool_host(),
    }
    if g_full is not None:
        x = np.asarray(inputs["L0_attn_hidden_states"][b])
        ret["xr"] = np.ascontiguousarray(x[own_tok].T, dtype=np.float32)
        ret["g_in"] = np.ascontiguousarray(g_full[b][:, own_tok])
        ret["w_out"] = tile_w(np.asarray(inputs["L0_attn_Wout"], np.float32),
                              oc16)
    return ret



P0_SHAPES = {
    "g_in": ([D, T], BF16, "ExternalInput"),
    "w_out": ([128, 16 * 2048], F32, "ExternalInput"),
    "nw": ([128, 16], F32, "ExternalInput"),
    "ssq_part": ([2, 2048], F32, "ExternalOutput"),
    "wop": ([4096, 2048], BF16, "ExternalOutput"),
}


def build_phase_m2p():
    nc = bass.Bass("TRN2", target_bir_lowering=False, detect_race_conditions=False)
    es = contextlib.ExitStack()
    ctx = Ctx(nc, es)
    with es:
        io = dram_io(nc, ["g_in", "w_out", "nw", "ssq_part", "wop"], P0_SHAPES)
        cpool = nc.dram_tensor("cpool", [128, max(4, len(CONSTS))], F32,
                               kind="ExternalInput").ap()
        cm = Common(ctx, cpool)
        ps = [ctx.psum(f"ps{i}", [128, 512], F32) for i in range(7)]
        _emit_m2_partials(ctx, cm, ps, io)
        finish_build(nc, ctx)
    return nc


def _emit_m2_partials(ctx, cm, ps, io, g_tile=None):
    g_in = io.get("g_in")
    w_out = io["w_out"]
    nw = io["nw"]
    ssq_part = io["ssq_part"]
    wop = io["wop"]
    sbf = ctx.sbuf
    s_wout = sbf("s_wout", [128, 16 * 2048], BF16)
    s_gt = sbf("s_gt", [128, 16 * TILE], BF16)
    s_wsb = sbf("s_wsb", [128, 16 * TILE], BF16)
    s_sq = [sbf(f"s_sq{i}", [128, TILE], F32) for i in range(2)]
    s_row = sbf("s_row", [1, TILE], F32)
    s_nw = sbf("s_nw", [128, 16], F32)
    dg, ds, pe, ve, se, so = cm.dg, cm.ds, cm.pe, cm.ve, cm.se, cm.so
    E = ctx.emit

    ds_nw = E("sync", lambda e: e.dma_start(out=s_nw[:], in_=nw),
              inc=ds, inc_n=16)
    dg_w = 0
    for half in range(2):
        dg_w = E("gpsimd", lambda e, _h=half: e.dma_start(
            out=s_wout[:, _h * 8 * 2048:(_h + 1) * 8 * 2048],
            in_=w_out[:, _h * 8 * 2048:(_h + 1) * 8 * 2048]), inc=dg, inc_n=16)
    ve_wf = 0
    for kc in range(16):
        ve_wf = E("vector", lambda e, _kc=kc: e.tensor_scalar(
            s_wout[:, _kc * 2048:(_kc + 1) * 2048],
            s_wout[:, _kc * 2048:(_kc + 1) * 2048],
            s_nw[:, _kc:_kc + 1], None, ALU.mult),
            waits=[(dg, dg_w), (ds, ds_nw)], inc=ve)

    gsrc = None if g_tile is not None else \
        g_in.rearrange("(kc p) T -> p kc T", p=128)
    dgt = ctx.sem("dgt")
    pe_gdone = 0
    se_ev_prev = 0
    se_r_prev = 0
    so_prev = 0
    for h in range(2):
        for tt in range(4):
            tg0 = h * 2048 + tt * TILE
            gin = (g_tile(h * 4 + tt) if g_tile is not None else
                   gsrc[:, :, tg0:tg0 + TILE])
            dm = E("sync", lambda e, _g=gin, _fl=(g_tile is not None): e.dma_start(
                out=(s_gt[:] if _fl else
                     s_gt[:].rearrange("p (kc T) -> p kc T", kc=16)),
                in_=_g), waits=[(pe, pe_gdone)], inc=dgt, inc_n=16)
            se_sq = {0: 0, 1: 0}
            pe_ss = 0
            for kc in range(16):
                par = kc % 2
                ssq = E("scalar", lambda e, _kc=kc, _p=par: e.activation(
                    s_sq[_p][:], s_gt[:, _kc * TILE:(_kc + 1) * TILE], AF.Square),
                    waits=[(dgt, dm), (pe, se_sq[par])], inc=se)
                pe_ss = E("tensor", lambda e, _p=par, _f=(kc == 0),
                          _l=(kc == 15): e.matmul(
                    ps[6][0:1, :], cm.s_onescol_f[:], s_sq[_p][:],
                    start=_f, stop=_l),
                    waits=[(se, ssq), (ve, cm.ve_init)] +
                          ([(se, se_r_prev)] if kc == 0 else []), inc=pe)
                se_sq[par] = pe_ss
            se_r = E("scalar", lambda e: e.activation(
                s_row[0:1, :], ps[6][0:1, :], AF.Copy),
                waits=[(pe, pe_ss), (so, so_prev)], inc=se)
            se_r_prev = se_r
            E("sync", lambda e, _h=h, _tt=tt: e.dma_start(
                out=ssq_part[_h:_h + 1, _tt * TILE:(_tt + 1) * TILE],
                in_=s_row[0:1, :]), waits=[(se, se_r)], inc=so, inc_n=16)
            se_ev = {0: se_ev_prev, 1: se_ev_prev}
            for oc in range(16):
                ob = oc % 2
                pe_o = 0
                for kc in range(16):
                    pe_o = E("tensor", lambda e, _kc=kc, _oc=oc, _ob=ob,
                             _f=(kc == 0), _l=(kc == 15): e.matmul(
                        ps[_ob][:, :],
                        s_wout[:, _kc * 2048 + _oc * 128:
                               _kc * 2048 + _oc * 128 + 128],
                        s_gt[:, _kc * TILE:(_kc + 1) * TILE],
                        start=_f, stop=_l),
                        waits=[(ve, ve_wf), (dgt, dm)] +
                              ([(se, se_ev[ob])] if kc == 0 else []),
                        inc=(pe if kc == 15 else None))
                se_ev[ob] = E("scalar", lambda e, _oc=oc, _ob=ob: e.activation(
                    s_wsb[:, _oc * TILE:(_oc + 1) * TILE], ps[_ob][:, :], AF.Copy),
                    waits=[(pe, pe_o), (so, so_prev)], inc=se)
            se_ev_prev = max(se_ev.values())
            pe_gdone = pe.n
            E("sync", lambda e, _h=h, _tt=tt: e.dma_start(
                out=wop.rearrange("(s o p) T -> p s o T", p=128, o=16)[
                    :, _h, :, _tt * TILE:(_tt + 1) * TILE],
                in_=s_wsb[:].rearrange("p (o T) -> p o T", o=16)),
                waits=[(se, se_ev_prev)], inc=so, inc_n=16)
            so_prev = so.n
    ctx.wait("sync", [(so, so.n)])


def host_prep_m2p(inputs, core):
    b, hf = core // 2, core % 2
    w_out = np.asarray(inputs["L0_attn_Wout"], np.float32)
    nw_ = np.asarray(inputs["L0_attn_norm_w"], np.float32).reshape(32, 128).T
    return {
        "w_out": tile_w(w_out[hf * 2048:(hf + 1) * 2048], [[(0, 2048)]]),
        "nw": np.ascontiguousarray(nw_[:, hf * 16:(hf + 1) * 16]),
    }



A1_INPUTS = ["wkvT", "wqT", "c_cc", "c_ss", "c_pr", "c_pr64", "c_idf",
             "c_t1c", "c_t2c"]
A1_SHAPES = {
    "xr": ([D, T], BF16, "ExternalInput"),
    "wkvT": ([128, 16 * 512], F32, "ExternalInput"),
    "wqT": ([128, 16 * 1024], F32, "ExternalInput"),
    "woT": ([128, 8 * 2048], F32, "ExternalInput"),
    "c_cc": ([128, T], F32, "ExternalInput"),
    "c_ss": ([128, T], F32, "ExternalInput"),
    "c_pr": ([128, 128], F32, "ExternalInput"),
    "c_pr64": ([128, 128], F32, "ExternalInput"),
    "c_idf": ([128, 128], F32, "ExternalInput"),
    "c_t1c": ([128, 128], F32, "ExternalInput"),
    "c_t2c": ([128, 4 * TILE], F32, "ExternalInput"),
    "y_out": ([1024, T], BF16, "ExternalOutput"),
}


def build_phase_a1(stage="full"):
    nc = bass.Bass("TRN2", target_bir_lowering=False, detect_race_conditions=False)
    es = contextlib.ExitStack()
    ctx = Ctx(nc, es)
    with es:
        io = dram_io(nc, A1_INPUTS + ["xr", "y_out"], A1_SHAPES)
        cpool = nc.dram_tensor("cpool", [128, max(4, len(CONSTS))], F32,
                               kind="ExternalInput").ap()
        cm = Common(ctx, cpool)
        ps = [ctx.psum(f"ps{i}", [128, 512], F32) for i in range(7)]
        psb = ctx.psum("psb", [128, 1024], BF16)
        io["lnsc"] = nc.dram_tensor("lnsc", [128, NTILE * 16 * TILE], BF16,
                                    kind="Internal").ap()
        _emit_phase_a1(ctx, cm, ps, psb, io, stage)
        finish_build(nc, ctx)
    return nc


def _emit_phase_a1(ctx, cm, ps, psb, io, stage="full", x_tile=None,
                   wo_part=False):
    nc = ctx.nc
    xr = io.get("xr")
    wkvT = io["wkvT"]
    wqT = io["wqT"]
    c_cc = io["c_cc"]
    c_ss = io["c_ss"]
    c_pr = io["c_pr"]
    c_pr64 = io["c_pr64"]
    c_idf = io["c_idf"]
    c_t1c = io["c_t1c"]
    c_t2c = io["c_t2c"]
    y_out = io.get("y_out")
    lnsc = io["lnsc"]
    sbf = ctx.sbuf
    s_xt = sbf("s_xt", [128, 16 * TILE], BF16)
    s_ln = sbf("s_ln", [128, 16 * TILE], BF16)
    s_k = sbf("s_k", [128, 2 * T], BF16)
    s_ksw = sbf("s_ksw", [128, 2 * T], BF16)
    s_vt = sbf("s_vt", [128, 32 * 260], BF16)
    s_q = sbf("s_q", [128, 8 * TILE], BF16)
    s_qr = sbf("s_qr", [128, 8 * TILE], BF16)
    s_cc = sbf("s_cc", [128, T], BF16)
    s_ss = sbf("s_ss", [128, T], BF16)
    s_pr = sbf("s_pr", [128, 128], BF16)
    s_pr64 = sbf("s_pr64", [128, 128], BF16)
    s_idb = sbf("s_idb", [128, 128], BF16)
    s_t1c = sbf("s_t1c", [128, 128], BF16)
    s_t2c = sbf("s_t2c", [128, 4 * TILE], BF16)
    s_pt = [sbf(f"s_pt{i}", [128, TILE], BF16) for i in range(4)]
    s_y = sbf("s_y", [64, 16 * TILE], BF16)
    s_w = [sbf(f"s_w{i}", [128, 16 * 512], BF16) for i in range(2)]
    s_sq2 = [sbf(f"s_sq{i}", [128, TILE], F32) for i in range(2)]
    s_rows = sbf("s_rows", [1, TILE * 5], F32)
    s_abc = sbf("s_abc", [128, TILE], F32)
    s_bbc = sbf("s_bbc", [128, TILE], F32)
    s_rr = sbf("s_rr", [1, TILE], F32)
    s_rb = sbf("s_rb", [64, TILE], F32)
    s_rop = [sbf(f"s_rop{i}", [128, TILE], F32) for i in range(2)]

    dg, ds, pe, ve, se, so = cm.dg, cm.ds, cm.pe, cm.ve, cm.se, cm.so
    E = ctx.emit

    for dst, srcd in ((s_cc, c_cc), (s_ss, c_ss), (s_pr, c_pr), (s_t1c, c_t1c),
                      (s_t2c, c_t2c), (s_idb, c_idf), (s_pr64, c_pr64)):
        E("gpsimd", lambda e, _d=dst, _s=srcd: e.dma_start(out=_d[:], in_=_s),
          inc=dg, inc_n=16)
    dg0 = dg.n
    ve_vt1 = 0
    for kc in range(32):
        for h in range(4):
            ve_vt1 = E("vector", lambda e, _o=kc * 260 + h * 65 + 64: e.tensor_copy(
                s_vt[:, _o:_o + 1], cm.s_cpool[:, 0:1]),
                waits=[(ds, cm.ds_cpool)], inc=ve)
    wbuf_state = {0: (pe, 0), 1: (pe, 0)}
    ve_ln_prev = 0

    def rope(nch, t0, src_ready):
        ve_last = 0
        war = {0: 0, 1: 0}
        for c in range(nch):
            bk = ps[c % 2]
            pe_s = E("tensor", lambda e, _c=c, _bk=bk: e.matmul(
                _bk[:, :], s_pr[:], s_q[:, _c * TILE:(_c + 1) * TILE],
                start=True, stop=True),
                waits=[src_ready, (dg, dg0), (ve, war[c % 2])], inc=pe)
            E("vector", lambda e, _bk=bk, _p=s_rop[c % 2], _t0=t0: e.tensor_tensor(
                _p[:], _bk[:, :], s_ss[:, _t0:_t0 + TILE], ALU.mult),
                waits=[(pe, pe_s)], inc=ve)
            E("vector", lambda e, _c=c, _p=s_sq2[c % 2], _t0=t0: e.tensor_tensor(
                _p[:], s_q[:, _c * TILE:(_c + 1) * TILE],
                s_cc[:, _t0:_t0 + TILE], ALU.mult), inc=ve)
            ve_last = E("vector", lambda e, _c=c, _p=s_rop[c % 2],
                        _q=s_sq2[c % 2]: e.tensor_tensor(
                s_qr[:, _c * TILE:(_c + 1) * TILE], _p[:], _q[:], ALU.add),
                inc=ve)
            war[c % 2] = ve_last
        return ve_last

    for t in range(NTILE):
        t0 = t * TILE
        xin = (x_tile(t) if x_tile is not None else
               xr.rearrange("(kc p) T -> p kc T", p=128)[:, :, t0:t0 + TILE])
        dg_xt = E("sync", lambda e, _x=xin: e.dma_start(
            out=s_xt[:].rearrange("p (kc T) -> p kc T", kc=16), in_=_x),
            waits=[(ve, ve_ln_prev)], inc=cm.dx, inc_n=16)
        ve_ln, _ = emit_ln(
            ctx, cm, lambda kc: s_xt[:, kc * TILE:(kc + 1) * TILE], 16,
            lambda kc: s_ln[:, kc * TILE:(kc + 1) * TILE],
            ps[5], ps[6], s_sq2, s_rows, s_abc, s_bbc,
            (cm.dx, dg_xt), [(se, se.n), (ve, ve.n), (pe, pe.n)])
        ve_ln_prev = ve_ln
        E("sync", lambda e, _t=t: e.dma_start(
            out=lnsc[:, _t * 8192:(_t + 1) * 8192], in_=s_ln[:]),
            waits=[(ve, ve_ln)], inc=so, inc_n=16)

        kvev = {}

        def evac_kv(grp, gi, pe_val, _t=t):
            marks = {}
            for j, (oc, w) in enumerate(grp):
                sev = E("scalar", lambda e, _j=j: e.activation(
                    s_q[:, _j * TILE:(_j + 1) * TILE], ps[_j][:, :], AF.Copy),
                    waits=[(pe, pe_val), (ve, ve.n)], inc=se)
                marks[(j, 0)] = (se, sev)
                kvev[j] = sev
            return marks

        emit_projmt(ctx, cm, wkvT, D,
                    [[(0, 128), (128, 128), (256, 128), (384, 128)]],
                    lambda kc, tt: s_ln[:, kc * TILE:(kc + 1) * TILE],
                    (ve, ve_ln), evac_kv, s_w, wbuf_state,
                    lambda j, tt: ps[j], 1, init_free=[(se, se.n), (ve, ve.n)])

        ve_rk = rope(2, t0, (se, max(kvev[0], kvev[1])))
        for c in range(2):
            E("vector", lambda e, _c=c, _t0=t0: e.tensor_copy(
                s_k[:, _c * T + _t0:_c * T + _t0 + TILE],
                s_qr[:, _c * TILE:(_c + 1) * TILE]),
                waits=[(ve, ve_rk)], inc=ve)
            pe_sw = E("tensor", lambda e, _c=c: e.matmul(
                ps[4 + _c][:, :], s_pr64[:], s_qr[:, _c * TILE:(_c + 1) * TILE],
                start=True, stop=True), waits=[(ve, ve_rk)], inc=pe)
            E("vector", lambda e, _c=c, _t0=t0: e.tensor_copy(
                s_ksw[:, _c * T + _t0:_c * T + _t0 + TILE], ps[4 + _c][:, :]),
                waits=[(pe, pe_sw)], inc=ve)
        ve_k_done = ve.n
        ve_tv = 0
        for c in range(2):
            for sub in range(4):
                kc = t * 4 + sub
                slot = (c * 4 + sub) % 8
                pe_tv = E("tensor", lambda e, _c=c, _sub=sub, _sl=slot: e.transpose(
                    psb[:, _sl * 128:(_sl + 1) * 128],
                    s_q[:, (2 + _c) * TILE + _sub * 128:
                        (2 + _c) * TILE + (_sub + 1) * 128],
                    s_idb[:]),
                    waits=[(se, kvev[2 + c]), (ve, ve_tv), (dg, dg0)], inc=pe)
                for hh in range(2):
                    h = 2 * c + hh
                    ve_tv = E("vector", lambda e, _kc=kc, _h=h, _sl=slot,
                              _hh=hh: e.tensor_copy(
                        s_vt[:, _kc * 260 + _h * 65:_kc * 260 + _h * 65 + 64],
                        psb[:, _sl * 128 + _hh * 64:_sl * 128 + _hh * 64 + 64]),
                        waits=[(pe, pe_tv)], inc=ve)
    ve_kv_done = ve.n
    so_p1 = so.n

    if stage == "kv":
        E("sync", lambda e: e.dma_start(
            out=y_out.rearrange("(kc p) T -> p kc T", p=128)[:, 0:2, :],
            in_=s_k[:].rearrange("p (kc T) -> p kc T", kc=2)),
            waits=[(ve, ve_kv_done)], inc=so, inc_n=16)
        ctx.wait("sync", [(so, so.n)])
        return

    if wo_part:
        woT = io["woT"]
        wopart = io["wopart"]
        s_yfm = s_q
        s_wosb = s_xt
    dlq = ctx.sem("dlq")
    pe_q_users = pe.n
    pe_yfm_prev = 0
    so_wo_prev = 0
    se_rb_prev = 0
    ve_rb_war = 0
    for t in range(NTILE):
        t0 = t * TILE
        dg_ln = E("sync", lambda e, _t=t: e.dma_start(
            out=s_ln[:], in_=lnsc[:, _t * 8192:(_t + 1) * 8192]),
            waits=[(so, so_p1), (pe, pe_q_users)], inc=dlq, inc_n=16)

        ve_q_free = ve.n

        def evac_q(grp, gi, pe_val, _t=t):
            marks = {}
            for j, (oc, w) in enumerate(grp):
                cch = oc // 128
                sev = E("scalar", lambda e, _j=j, _c=cch: e.activation(
                    s_q[:, _c * TILE:(_c + 1) * TILE], ps[_j][:, :], AF.Copy),
                    waits=[(pe, pe_val), (ve, ve_q_free)], inc=se)
                marks[(j, 0)] = (se, sev)
            return marks

        emit_projmt(ctx, cm, wqT, D,
                    [[(o * 128, 128) for o in range(g * 4, g * 4 + 4)]
                     for g in range(2)],
                    lambda kc, tt: s_ln[:, kc * TILE:(kc + 1) * TILE],
                    (dlq, dg_ln), evac_q, s_w, wbuf_state,
                    lambda j, tt: ps[j], 1, init_free=[(se, se.n), (ve, ve.n)])
        se_q_done = se.n
        ve_rq = rope(8, t0, (se, se_q_done))

        nkc = 4 * t + 4
        ve_y_ev = {0: ve.n, 1: ve.n}
        se_exp = {0: {0: se.n, 1: se.n}, 1: {0: se.n, 1: se.n}}
        pe_av = {0: {0: pe.n, 1: pe.n}, 1: {0: pe.n, 1: pe.n}}
        pe_rb_prev = 0
        pe_yfm_g = pe_yfm_prev if wo_part else 0
        for pr in range(8):
            heads = (2 * pr, 2 * pr + 1)
            kvh = heads[0] // 4
            kchunk = kvh // 2
            kbase = (kvh % 2) * 64
            for kc in range(nkc):
                par = kc % 2
                off = t0 - kc * 128
                diag = -512 < off <= 0
                pe_sc = {}
                for hi, h in enumerate(heads):
                    stb = ps[2 * hi + par]
                    qb = (h % 2) * 64
                    ksrc = s_k if qb == kbase else s_ksw
                    v_ = E("tensor", lambda e, _kc=kc, _bk=stb, _kchunk=kchunk,
                           _qb=qb, _h=h, _ks=ksrc, _st=(not diag): e.matmul(
                        _bk[:, :],
                        _ks[_qb:_qb + 64,
                            _kchunk * T + _kc * 128:_kchunk * T + _kc * 128 + 128],
                        s_qr[_qb:_qb + 64,
                             (_h // 2) * TILE:(_h // 2) * TILE + TILE],
                        start=True, stop=_st),
                        waits=[(ve, ve_rq), (ve, ve_k_done),
                               (se, se_exp[hi][par])],
                        inc=(pe if not diag else None))
                    if diag:
                        vmask = (-off) // 128
                        v_ = E("tensor", lambda e, _bk=stb, _v=vmask: e.matmul(
                            _bk[:, :], s_t1c[:],
                            s_t2c[:, _v * TILE:(_v + 1) * TILE],
                            start=False, stop=True), inc=pe)
                    pe_sc[hi] = v_
                for hi, h in enumerate(heads):
                    stb = ps[2 * hi + par]
                    se_exp[hi][par] = E("scalar", lambda e, _bk=stb,
                                        _p=s_pt[2 * hi + par]: e.activation(
                        _p[:], _bk[:, :], AF.Exp, scale=0.125, bias=-10.0),
                        waits=[(pe, pe_sc[hi]), (pe, pe_av[hi][par])], inc=se)
                for hi, h in enumerate(heads):
                    pe_av[hi][par] = E("tensor", lambda e, _kc=kc, _kvh=kvh,
                                       _p=s_pt[2 * hi + par], _yb=ps[4 + hi],
                                       _f=(kc == 0), _l=(kc == nkc - 1): e.matmul(
                        _yb[0:65, :],
                        s_vt[:, _kc * 260 + _kvh * 65:_kc * 260 + _kvh * 65 + 65],
                        _p[:], start=_f, stop=_l),
                        waits=[(se, se_exp[hi][par]), (ve, ve_vt1),
                               (ve, ve_y_ev[hi])],
                        inc=pe)
            for hi, h in enumerate(heads):
                yb = ps[4 + hi]
                pe_yd = pe_av[hi][(nkc - 1) % 2]
                vv = E("vector", lambda e, _yb=yb: e.reciprocal(
                    s_rr[0:1, :], _yb[64:65, :]),
                    waits=[(pe, pe_yd), (pe, pe_rb_prev)], inc=ve)
                pe_rb = E("tensor", lambda e: e.matmul(
                    ps[6][0:64, :], cm.s_ones_f[0:1, 0:64], s_rr[0:1, :],
                    start=True, stop=True),
                    waits=[(ve, vv), (se, se_rb_prev)], inc=pe)
                pe_rb_prev = pe_rb
                se_rb = E("scalar", lambda e: e.activation(
                    s_rb[:, :], ps[6][0:64, :], AF.Copy),
                    waits=[(pe, pe_rb), (ve, ve_rb_war)], inc=se)
                se_rb_prev = se_rb
                ve_y_ev[hi] = E("vector", lambda e, _h=h, _yb=yb: e.tensor_tensor(
                    s_y[:, _h * TILE:(_h + 1) * TILE], _yb[0:64, :], s_rb[:, :],
                    ALU.mult), waits=[(se, se_rb), (pe, pe_yfm_g)], inc=ve)
                ve_rb_war = ve_y_ev[hi]
        pe_q_users = pe.n
        if not wo_part:
            _unused = 0
        if not wo_part:
            E("sync", lambda e, _t0=t0: e.dma_start(
                out=y_out.rearrange("(h p) T -> p h T", p=64)[:, :, _t0:_t0 + TILE],
                in_=s_y[:].rearrange("p (h T) -> p h T", h=16)),
                waits=[(ve, ve.n)], inc=so, inc_n=16)
            continue
        ve_yall = ve.n
        se_fm = {0: se.n, 1: se.n}
        for c in range(8):
            bk = ps[4 + c % 2]
            E("tensor", lambda e, _c=c, _bk=bk: e.matmul(
                _bk[:, :], s_idb[0:64, :], s_y[:, (2 * _c) * TILE:
                                               (2 * _c + 1) * TILE],
                start=True, stop=False),
                waits=[(ve, ve_yall), (se, se_fm[c % 2])], inc=None)
            pe_f = E("tensor", lambda e, _c=c, _bk=bk: e.matmul(
                _bk[:, :], s_pr64[0:64, :], s_y[:, (2 * _c + 1) * TILE:
                                                (2 * _c + 2) * TILE],
                start=False, stop=True), inc=pe)
            se_fm[c % 2] = E("scalar", lambda e, _c=c, _bk=bk: e.activation(
                s_yfm[:, _c * TILE:(_c + 1) * TILE], _bk[:, :], AF.Copy),
                waits=[(pe, pe_f)], inc=se)
        se_fm_done = max(se_fm.values())

        def ev_wo(grp, gi, pe_val, _t=t):
            marks = {}
            for j, (oc, w) in enumerate(grp):
                sev = E("scalar", lambda e, _j=j, _oc=oc // 128: e.activation(
                    s_wosb[:, _oc * TILE:(_oc + 1) * TILE], ps[_j][:, :],
                    AF.Copy), waits=[(pe, pe_val), (so, so_wo_prev)], inc=se)
                marks[(j, 0)] = (se, sev)
            return marks

        emit_projmt(ctx, cm, woT, 1024,
                    [[(o * 128, 128) for o in range(g * 4, g * 4 + 4)]
                     for g in range(4)],
                    lambda kc, tt: s_yfm[:, kc * TILE:(kc + 1) * TILE],
                    (se, se_fm_done), ev_wo, s_w, wbuf_state,
                    lambda j, tt: ps[j], 1, init_free=[(se, se.n), (ve, ve.n)])
        pe_yfm_prev = pe.n
        pe_q_users = pe.n
        E("sync", lambda e, _t=t: e.dma_start(
            out=wopart.rearrange("(s o p) T -> p s o T", p=128, o=16)[
                :, _t // 4, :, (_t % 4) * TILE:(_t % 4 + 1) * TILE],
            in_=s_wosb[:].rearrange("p (o T) -> p o T", o=16)),
            waits=[(se, se.n)], inc=so, inc_n=16)
        so_wo_prev = so.n
    ctx.wait("sync", [(so, so.n)])


def host_prep_a1(inputs, core, resid2_full):
    b, hf = core // 2, core % 2
    c_idf, _, c_m1, _ = host_consts()
    perm = np.concatenate([np.arange(0, 64, 2), np.arange(1, 64, 2)])
    wq = np.asarray(inputs["L1_attn_wq"], np.float32)
    wk = np.asarray(inputs["L1_attn_wk"], np.float32)
    wv = np.asarray(inputs["L1_attn_wv"], np.float32)
    qrows = np.concatenate([(hf * 16 + h) * 64 + perm for h in range(16)])
    krows = np.concatenate([(hf * 4 + h) * 64 + perm for h in range(4)])
    vrows = np.concatenate([np.arange((hf * 4 + h) * 64, (hf * 4 + h) * 64 + 64)
                            for h in range(4)])
    wqT = tile_w(wq[qrows].T, [[(0, 512)], [(512, 512)]])
    wkvT = tile_w(np.concatenate([wk[krows].T, wv[vrows].T], axis=1),
                  [[(0, 512)]])
    theta = 1.0 / 10000.0 ** (np.arange(0, 64, 2, dtype=np.float32) / 64)
    tpos = np.arange(T, dtype=np.float32)
    fr = tpos[None, :] * theta[:, None]
    cosr = np.cos(fr).astype(np.float32)
    sinr = np.sin(fr).astype(np.float32)
    cc = np.empty((128, T), np.float32)
    ss = np.empty((128, T), np.float32)
    for r in range(128):
        i = r % 64
        if i < 32:
            cc[r] = cosr[i]
            ss[r] = -sinr[i]
        else:
            cc[r] = cosr[i - 32]
            ss[r] = sinr[i - 32]
    c_pr = np.zeros((128, 128), np.float32)
    for a in range(128):
        sw = a + 32 if (a % 64) < 32 else a - 32
        c_pr[sw, a] = 1.0
    c_pr64 = np.zeros((128, 128), np.float32)
    for a in range(128):
        c_pr64[a ^ 64, a] = 1.0
    c_t2c = np.zeros((128, 4 * TILE), np.float32)
    for v in range(4):
        for q_ in range(TILE):
            pp = q_ - 128 * v + 1
            pp = max(0, pp)
            if pp <= 127:
                c_t2c[pp, v * TILE + q_] = BIGNEG
    ret = {
        "wkvT": wkvT, "wqT": wqT,
        "c_cc": cc, "c_ss": ss, "c_pr": c_pr, "c_pr64": c_pr64, "c_idf": c_idf,
        "c_t1c": c_m1, "c_t2c": c_t2c, "cpool": make_cpool_host(),
    }
    if resid2_full is not None:
        ret["xr"] = np.ascontiguousarray(resid2_full[b])
    else:
        wo = np.asarray(inputs["L1_attn_wo"], np.float32)
        ret["woT"] = tile_w(wo.T[hf * 1024:(hf + 1) * 1024],
                            [[(g * 512, 512)] for g in range(4)])
    return ret



F1_INPUTS = ["woT", "gw1T", "wgu", "wdn", "swgu", "swdT", "sgwT", "c_idf",
             "c_oh8"]
F1_SHAPES = {
    "y_in": ([D, 2048], BF16, "ExternalInput"),
    "xr": ([D, 2048], BF16, "ExternalInput"),
    "woT": ([128, 16 * 2048], F32, "ExternalInput"),
    "gw1T": ([128, 16 * 8], F32, "ExternalInput"),
    "wgu": ([8, 128, 16 * 2816], F32, "ExternalInput"),
    "wdn": ([8, 128, 11 * 2048], F32, "ExternalInput"),
    "swgu": ([128, 16 * 11264], F32, "ExternalInput"),
    "swdT": ([128, 44 * 2048], F32, "ExternalInput"),
    "sgwT": ([128, 16], F32, "ExternalInput"),
    "c_idf": ([128, 128], F32, "ExternalInput"),
    "c_oh8": ([8, 8 * 128], F32, "ExternalInput"),
    "r_out": ([D, 2048], F32, "ExternalOutput"),
}


def build_phase_a2f1(stage="full"):
    nc = bass.Bass("TRN2", target_bir_lowering=False, detect_race_conditions=False)
    es = contextlib.ExitStack()
    ctx = Ctx(nc, es)
    with es:
        io = dram_io(nc, F1_INPUTS + ["y_in", "xr", "r_out"], F1_SHAPES)
        cpool = nc.dram_tensor("cpool", [128, max(4, len(CONSTS))], F32,
                               kind="ExternalInput").ap()
        cm = Common(ctx, cpool)
        ps = [ctx.psum(f"ps{i}", [128, 512], F32) for i in range(7)]
        _emit_phase_a2f1(ctx, cm, ps, io, stage)
        finish_build(nc, ctx)
    return nc


def _emit_phase_a2f1(ctx, cm, ps, io, stage="full", rs_mode=False,
                     xr_tiled=False):
    nc = ctx.nc
    NT = 2048
    NE = 8
    y_in = io.get("y_in")
    xr = io["xr"]
    woT = io.get("woT")
    gw1T = io["gw1T"]
    wgu = io["wgu"]
    wdn = io["wdn"]
    swgu = io["swgu"]
    swdT = io["swdT"]
    sgwT = io["sgwT"]
    c_idf = io["c_idf"]
    c_oh8 = io["c_oh8"]
    r_out = io["r_out"]
    sbf = ctx.sbuf
    s_rt = sbf("s_rt", [128, 16 * NT], BF16)
    s_lnf = sbf("s_lnf", [128, 16 * NT], BF16)
    s_hmid = sbf("s_hmid", [128, 11 * 1024], BF16)
    s_sh = s_hmid
    s_w = [sbf(f"s_w{i}", [128, 16 * 256], BF16) for i in range(2)]
    s_yb = [sbf(f"s_yb{i}", [128, NT], BF16) for i in range(2)]
    s_idf = sbf("s_idf", [128, 128], F32)
    s_oh8 = sbf("s_oh8", [NE, NE * 128], BF16)
    s_gw1 = sbf("s_gw1", [128, 16 * NE], BF16)
    s_sgw = sbf("s_sgw", [128, 16], BF16)
    s_rows = sbf("s_rows", [1, TILE * 5], F32)
    s_sq2 = [sbf(f"s_sq{i}", [128, TILE], F32) for i in range(2)]
    s_abc = sbf("s_abc", [128, TILE], F32)
    s_bbc = sbf("s_bbc", [128, TILE], F32)
    s_ev = [sbf(f"s_ev{i}", [128, TILE], F32) for i in range(2)]
    s_sm = sbf("s_sm", [128, 8], F32)
    s_e16 = sbf("s_e16", [128, NE], F32)
    s_e16b = sbf("s_e16b", [128, NE], F32)
    s_cmb = sbf("s_cmb", [128, NE], F32)
    s_CT = sbf("s_CT", [NE, NT], BF16)
    s_cbc = s_yb[0]
    s_sgbc = s_yb[1]

    dsr = [ctx.sem("dr0"), ctx.sem("dr1")]
    dg, ds, pe, ve, se, so = cm.dg, cm.ds, cm.pe, cm.ve, cm.se, cm.so
    E = ctx.emit

    E("sync", lambda e: e.dma_start(out=s_idf[:], in_=c_idf), inc=ds, inc_n=16)
    for dst, srcd in ((s_gw1, gw1T), (s_sgw, sgwT), (s_oh8, c_oh8)):
        E("gpsimd", lambda e, _d=dst, _s=srcd: e.dma_start(out=_d[:], in_=_s),
          inc=dg, inc_n=16)
    ds0, dg0 = ds.n, dg.n
    if xr_tiled:
        for tt in range(4):
            dg_rt = E("sync", lambda e, _tt=tt: e.dma_start(
                out=s_rt[:].rearrange("p (kc tt T) -> p tt kc T",
                                      tt=4, T=TILE)[:, _tt, :, :],
                in_=xr[:, _tt * 8192:(_tt + 1) * 8192].rearrange(
                    "p (kc T) -> p kc T", kc=16)), inc=cm.dx, inc_n=16)
    else:
        dg_rt = E("sync", lambda e: e.dma_start(
            out=s_rt[:].rearrange("p (kc T) -> p kc T", kc=16),
            in_=xr.rearrange("(kc p) T -> p kc T", p=128)), inc=cm.dx, inc_n=16)

    wbuf_state = {0: (pe, 0), 1: (pe, 0)}
    ve_wo = 0
    if rs_mode:
        wop2_rs = io["wop2_rs"]
        w2src = wop2_rs.rearrange("(kc p) T -> p kc T", p=128)
        dwo = {0: 0, 1: 0}
        for kc in range(16):
            par = kc % 2
            dmw = E("sync", lambda e, _kc=kc, _p=par: e.dma_start(
                out=s_yb[_p][:], in_=w2src[:, _kc, :]),
                waits=[(ve, dwo[par])], inc=dsr[par], inc_n=16)
            ve_wo = E("vector", lambda e, _kc=kc, _p=par: e.tensor_tensor(
                s_rt[:, _kc * NT:(_kc + 1) * NT], s_rt[:, _kc * NT:(_kc + 1) * NT],
                s_yb[_p][:], ALU.add),
                waits=[(dsr[par], dmw), (cm.dx, dg_rt)], inc=ve)
            dwo[par] = ve_wo
    else:
        ysrc = y_in.rearrange("(kc p) T -> p kc T", p=128)
        ystream = RhsStream(ctx, cm, "y", lambda kc: ysrc[:, kc, :], s_yb, dsr,
                            False)

        def wo_evac(grp, gi, pe_val):
            nonlocal ve_wo
            marks = {}
            oc = grp[0][0]
            for tt in range(4):
                dst = s_rt[:, (oc // 128) * NT + tt * TILE:
                           (oc // 128) * NT + (tt + 1) * TILE]
                ve_wo = E("vector", lambda e, _d=dst, _tt=tt: e.tensor_tensor(
                    _d, _d, ps[_tt][:, :], ALU.add),
                    waits=[(pe, pe_val), (cm.dx, dg_rt)], inc=ve)
                marks[(0, tt)] = (ve, ve_wo)
            return marks

        emit_projmt(ctx, cm, woT, D, [[(o, 128)] for o in range(0, D, 128)],
                    None, None, wo_evac, s_w, wbuf_state,
                    lambda j, tt: ps[tt], 4, init_free=[(ve, 0)], stream=ystream)

    if stage == "wo":
        E("gpsimd", lambda e: e.dma_start(
            out=r_out.rearrange("(kc p) T -> p kc T", p=128),
            in_=s_rt[:].rearrange("p (kc T) -> p kc T", kc=16)),
            waits=[(ve, ve_wo)], inc=so, inc_n=16)
        ctx.wait("gpsimd", [(so, so.n)])
        return

    ve_ln = 0
    for tt in range(4):
        ve_ln, _ = emit_ln(
            ctx, cm, lambda kc, _tt=tt: s_rt[:, kc * NT + _tt * TILE:
                                             kc * NT + (_tt + 1) * TILE], 16,
            lambda kc, _tt=tt: s_lnf[:, kc * NT + _tt * TILE:
                                     kc * NT + (_tt + 1) * TILE],
            ps[5], ps[6], s_sq2, s_rows, s_abc, s_bbc,
            (ve, ve_wo), [(se, se.n), (ve, ve.n), (pe, pe.n)])

    se_ct = 0
    ve_p4_users = 0
    se_p4_users = 0
    for tt in range(4):
        for tsub in range(4):
            pe_rl = 0
            for kc in range(16):
                off = kc * NT + tt * TILE + tsub * 128
                pe_rl = E("tensor", lambda e, _o=off, _kc=kc, _f=(kc == 0),
                          _l=(kc == 15): e.matmul(
                    ps[4][:, 0:NE], s_lnf[:, _o:_o + 128],
                    s_gw1[:, _kc * NE:(_kc + 1) * NE], start=_f, stop=_l),
                    waits=[(ve, ve_ln), (dg, dg0),
                           (ve, ve_p4_users), (se, se_p4_users)], inc=pe)

            def VB(fn, waits=()):
                vv = E("vector", fn, waits=waits, inc=ve)
                ctx.wait("vector", [(ve, vv)])
                return vv
            VB(lambda e: e.reduce_max(s_sm[:, 0:1], ps[4][:, 0:NE], axis=AX.X),
               waits=[(pe, pe_rl)])
            ve_neg = VB(lambda e: e.tensor_scalar_mul(s_sm[:, 1:2], s_sm[:, 0:1],
                                                      -1.0))
            se_e = E("scalar", lambda e: e.activation(
                s_e16[:], ps[4][:, 0:NE], AF.Exp, bias=s_sm[:, 1:2]),
                waits=[(ve, ve_neg)], inc=se)
            VB(lambda e: e.tensor_scalar(s_cmb[:], ps[4][:, 0:NE], s_sm[:, 0:1],
                                         None, ALU.subtract))
            VB(lambda e: e.tensor_scalar(s_cmb[:], s_cmb[:],
                                         cm.s_cpool[:, 4:5], cm.s_cpool[:, 0:1],
                                         ALU.mult, ALU.min))
            VB(lambda e: e.tensor_scalar(s_cmb[:], s_cmb[:], cm.s_cpool[:, 7:8],
                                         None, ALU.add))
            VB(lambda e: e.scalar_tensor_tensor(s_e16b[:], s_cmb[:],
                                                cm.s_cpool[:, 6:7], ps[4][:, 0:NE],
                                                ALU.mult, ALU.add))
            VB(lambda e: e.reduce_max(s_sm[:, 2:3], s_e16b[:], axis=AX.X))
            VB(lambda e: e.tensor_scalar(s_cmb[:], ps[4][:, 0:NE], s_sm[:, 2:3],
                                         None, ALU.subtract))
            VB(lambda e: e.tensor_scalar(s_cmb[:], s_cmb[:],
                                         cm.s_cpool[:, 4:5], cm.s_cpool[:, 5:6],
                                         ALU.mult, ALU.max))
            VB(lambda e: e.tensor_scalar(s_cmb[:], s_cmb[:],
                                         cm.s_cpool[:, 0:1], cm.s_cpool[:, 7:8],
                                         ALU.min, ALU.add))
            VB(lambda e: e.tensor_scalar(s_cmb[:], s_cmb[:], cm.s_cpool[:, 7:8],
                                         None, ALU.mult))
            VB(lambda e: e.tensor_tensor(s_cmb[:], s_cmb[:], s_e16[:], ALU.mult),
               waits=[(se, se_e)])
            VB(lambda e: e.tensor_reduce(s_sm[:, 3:4], s_cmb[:], axis=AX.X,
                                         op=ALU.add))
            VB(lambda e: e.reciprocal(s_sm[:, 4:5], s_sm[:, 3:4]))
            ve_c = VB(lambda e: e.tensor_scalar(s_cmb[:], s_cmb[:], s_sm[:, 4:5],
                                                None, ALU.mult))
            ve_p4_users = ve_c
            se_p4_users = se_e
            pe_tc = E("tensor", lambda e: e.transpose(
                ps[5][0:NE, 0:128], s_cmb[:, 0:NE].rearrange(
                    "p c -> p c") if False else s_cmb[:], s_idf[:]),
                waits=[(ve, ve_c), (se, se_ct), (ds, ds0)], inc=pe)
            se_ct = E("scalar", lambda e, _o=tt * TILE + tsub * 128: e.activation(
                s_CT[0:NE, _o:_o + 128], ps[5][0:NE, 0:128], AF.Copy),
                waits=[(pe, pe_tc)], inc=se)

    ve_hmid_last = 0
    se_cbc_users = 0
    for ei in range(NE):
        se_cbc = se_cbc_users
        pe_cb = 0
        for tt in range(4):
            pe_cb = E("tensor", lambda e, _e=ei, _tt=tt: e.matmul(
                ps[6][:, :], s_oh8[:, _e * 128:(_e + 1) * 128],
                s_CT[0:NE, _tt * TILE:(_tt + 1) * TILE], start=True, stop=True),
                waits=[(se, se_ct), (ve, ve_hmid_last), (se, se_cbc)], inc=pe)
            se_cbc = E("scalar", lambda e, _tt=tt: e.activation(
                s_cbc[:, _tt * TILE:(_tt + 1) * TILE], ps[6][:, :], AF.Copy),
                waits=[(pe, pe_cb)], inc=se)
        se_cbc_users = se_cbc

        wgug = [[(j * 256, 128), (j * 256 + 128, 128)] for j in range(11)]
        for tth in range(2):
            def evg(grp, gi, pe_val, _tth=tth, _e=ei):
                nonlocal ve_hmid_last
                marks = {}
                j = grp[0][0] // 256
                for tl in range(2):
                    tt = _tth * 2 + tl
                    sil = s_ev[tl]
                    sev = E("scalar", lambda e, _s=sil, _tl=tl: e.activation(
                        _s[:], ps[_tl * 2][:, :], AF.Silu),
                        waits=[(pe, pe_val)], inc=se)
                    E("vector", lambda e, _s=sil, _tl=tl: e.tensor_tensor(
                        _s[:], _s[:], ps[_tl * 2 + 1][:, :], ALU.mult),
                        waits=[(se, sev)], inc=ve)
                    dst = s_hmid[:, j * 1024 + tl * TILE: j * 1024 + (tl + 1) * TILE]
                    vh = E("vector", lambda e, _d=dst, _s=sil, _tt=tt: e.tensor_tensor(
                        _d, _s[:], s_cbc[:, _tt * TILE:(_tt + 1) * TILE], ALU.mult),
                        waits=[(se, se_cbc)], inc=ve)
                    marks[(0, tl)] = (ve, vh)
                    marks[(1, tl)] = (ve, vh)
                    ve_hmid_last = vh
                return marks

            emit_projmt(ctx, cm, wgu[ei], D, wgug,
                        lambda kc, tl, _tth=tth: s_lnf[:, kc * NT + (_tth * 2 + tl)
                                                       * TILE: kc * NT +
                                                       (_tth * 2 + tl + 1) * TILE],
                        (ve, ve_ln), evg, s_w, wbuf_state,
                        lambda j, tl: ps[tl * 2 + j], 2,
                        init_free=[(ve, ve.n), (se, se.n)])

            def evd(grp, gi, pe_val, _e=ei, _tth=tth):
                marks = {}
                oc = grp[0][0]
                for tl in range(2):
                    tt = _tth * 2 + tl
                    dst = s_rt[:, (oc // 128) * NT + tt * TILE:
                               (oc // 128) * NT + (tt + 1) * TILE]
                    vv = E("vector", lambda e, _d=dst, _tl=tl: e.tensor_tensor(
                        _d, _d, ps[_tl][:, :], ALU.add),
                        waits=[(pe, pe_val), (ve, ve_ln)], inc=ve)
                    marks[(0, tl)] = (ve, vv)
                return marks

            emit_projmt(ctx, cm, wdn[ei], 1408,
                        [[(o, 128)] for o in range(0, D, 128)],
                        lambda kc, tl: s_hmid[:, kc * 1024 + tl * TILE:
                                              kc * 1024 + (tl + 1) * TILE],
                        (ve, ve_hmid_last), evd, s_w, wbuf_state,
                        lambda j, tl: ps[tl], 2, init_free=[(ve, ve.n), (se, se.n)])

    if stage == "moe":
        E("gpsimd", lambda e: e.dma_start(
            out=r_out.rearrange("(kc p) T -> p kc T", p=128),
            in_=s_rt[:].rearrange("p (kc T) -> p kc T", kc=16)),
            waits=[(ve, ve.n)], inc=so, inc_n=16)
        ctx.wait("gpsimd", [(so, so.n)])
        return

    se_sg = se.n
    ve_sg_guard = ve.n
    for tt in range(4):
        pe_sg = 0
        for kc in range(16):
            pe_sg = E("tensor", lambda e, _kc=kc, _tt=tt, _f=(kc == 0),
                      _l=(kc == 15): e.matmul(
                ps[4][0:1, :], s_sgw[:, _kc:_kc + 1],
                s_lnf[:, _kc * NT + _tt * TILE:_kc * NT + (_tt + 1) * TILE],
                start=_f, stop=_l),
                waits=[(ve, ve_ln), (dg, dg0), (se, se_sg), (ve, ve_sg_guard)],
                inc=pe)
        r = s_rows[0:1, 0:TILE]
        E("scalar", lambda e, _r=r: e.activation(_r, ps[4][0:1, :], AF.Sigmoid),
          waits=[(pe, pe_sg)], inc=se)
        pe_sgb = E("tensor", lambda e, _r=r: e.matmul(
            ps[5][:, :], cm.s_ones_f[:], _r, start=True, stop=True),
            waits=[(se, se.n)], inc=pe)
        se_sg = E("scalar", lambda e, _tt=tt: e.activation(
            s_sgbc[:, _tt * TILE:(_tt + 1) * TILE], ps[5][:, :], AF.Copy),
            waits=[(pe, pe_sgb)], inc=se)

    for blk in range(22):
        sgug = [[(blk * 512 + j * 256, 128), (blk * 512 + j * 256 + 128, 128)]
                for j in range(2)]
        for tth in range(2):
            def evsh(grp, gi, pe_val, _tth=tth, _blk=blk):
                marks = {}
                j = (grp[0][0] - _blk * 512) // 256
                for tl in range(2):
                    tt = _tth * 2 + tl
                    sil = s_ev[tl]
                    sev = E("scalar", lambda e, _s=sil, _tl=tl: e.activation(
                        _s[:], ps[_tl * 2][:, :], AF.Silu),
                        waits=[(pe, pe_val)], inc=se)
                    dst = s_sh[:, j * (4 * TILE) + tt * TILE:
                               j * (4 * TILE) + (tt + 1) * TILE]
                    vh = E("vector", lambda e, _d=dst, _s=sil, _tl=tl: e.tensor_tensor(
                        _d, _s[:], ps[_tl * 2 + 1][:, :], ALU.mult),
                        waits=[(se, sev)], inc=ve)
                    marks[(0, tl)] = (ve, vh)
                    marks[(1, tl)] = (ve, vh)
                return marks

            emit_projmt(ctx, cm, swgu, D, sgug,
                        lambda kc, tl, _tth=tth: s_lnf[:, kc * NT + (_tth * 2 + tl)
                                                       * TILE: kc * NT +
                                                       (_tth * 2 + tl + 1) * TILE],
                        (ve, ve_ln), evsh, s_w, wbuf_state,
                        lambda j, tl: ps[tl * 2 + j], 2,
                        init_free=[(ve, ve.n), (se, se.n)])
        ve_sh = ve.n

        def evsd(grp, gi, pe_val, _blk=blk):
            marks = {}
            oc = grp[0][0]
            for tt in range(4):
                E("vector", lambda e, _tt=tt: e.tensor_tensor(
                    s_ev[_tt % 2][:], ps[_tt][:, :],
                    s_sgbc[:, _tt * TILE:(_tt + 1) * TILE], ALU.mult),
                    waits=[(pe, pe_val), (se, se_sg)], inc=ve)
                dst = s_rt[:, (oc // 128) * NT + tt * TILE:
                           (oc // 128) * NT + (tt + 1) * TILE]
                vv = E("vector", lambda e, _d=dst, _tt=tt: e.tensor_tensor(
                    _d, _d, s_ev[_tt % 2][:], ALU.add), inc=ve)
                marks[(0, tt)] = (ve, vv)
            return marks

        emit_projmt(ctx, cm, swdT[:, blk * 4096:(blk + 1) * 4096], 256,
                    [[(o, 128)] for o in range(0, D, 128)],
                    lambda kc, tt, _blk=blk: s_sh[:, kc * (4 * TILE) + tt * TILE:
                                                  kc * (4 * TILE) + (tt + 1) * TILE],
                    (ve, ve_sh), evsd, s_w, wbuf_state,
                    lambda j, tt: ps[tt], 4, init_free=[(ve, ve.n), (se, se.n)])

    E("gpsimd", lambda e: e.dma_start(
        out=r_out.rearrange("(kc p) T -> p kc T", p=128),
        in_=s_rt[:].rearrange("p (kc T) -> p kc T", kc=16)),
        waits=[(ve, ve.n)], inc=so, inc_n=16)
    ctx.wait("gpsimd", [(so, so.n)])


def host_prep_a2f1(inputs, core, y_full, resid2_full):
    b, hf = core // 2, core % 2
    c_idf, _, _, _ = host_consts()
    c_oh8 = np.zeros((8, 8 * 128), np.float32)
    for h in range(8):
        c_oh8[h, h * 128:(h + 1) * 128] = 1.0
    own_tok = np.concatenate([np.arange((2 * s + hf) * TILE, (2 * s + hf + 1) * TILE)
                              for s in range(4)])
    gw1 = np.asarray(inputs["L1_ffn_gate_w"], np.float32)
    gw1T = np.ascontiguousarray(
        gw1.T.reshape(16, 128, 8).transpose(1, 0, 2).reshape(128, 128))
    wg = np.asarray(inputs["L1_ffn_w_gate"], np.float32)
    wu = np.asarray(inputs["L1_ffn_w_up"], np.float32)
    wgu = np.empty((8, 2048, 2816), np.float32)
    for j in range(11):
        wgu[:, :, j * 256:j * 256 + 128] = wg[:, :, j * 128:(j + 1) * 128]
        wgu[:, :, j * 256 + 128:j * 256 + 256] = wu[:, :, j * 128:(j + 1) * 128]
    wgug = [[(j * 256, 128), (j * 256 + 128, 128)] for j in range(11)]
    oc16 = [[(o, 128)] for o in range(0, 2048, 128)]
    wgu = np.stack([tile_w(wgu[e], wgug) for e in range(8)])
    wdn_ = np.asarray(inputs["L1_ffn_w_down"], np.float32)
    wdn_ = np.stack([tile_w(wdn_[e], oc16) for e in range(8)])
    swg = np.asarray(inputs["L1_ffn_sw_gate"], np.float32)
    swu = np.asarray(inputs["L1_ffn_sw_up"], np.float32)
    swgu = np.empty((2048, 11264), np.float32)
    for j in range(44):
        swgu[:, j * 256:j * 256 + 128] = swg[:, j * 128:(j + 1) * 128]
        swgu[:, j * 256 + 128:j * 256 + 256] = swu[:, j * 128:(j + 1) * 128]
    sgw2 = np.asarray(inputs["L1_ffn_shared_gate_w"], np.float32)
    sgwT = np.ascontiguousarray(sgw2.reshape(16, 128).T)
    sgug = [[(blk * 512 + j * 256, 128), (blk * 512 + j * 256 + 128, 128)]
            for blk in range(22) for j in range(2)]
    swd = np.asarray(inputs["L1_ffn_sw_down"], np.float32)
    swdT = np.concatenate([tile_w(swd[blk * 256:(blk + 1) * 256], oc16)
                           for blk in range(22)], axis=1)
    ret = {
        "gw1T": gw1T, "wgu": wgu, "wdn": wdn_,
        "swgu": tile_w(swgu, sgug), "swdT": swdT,
        "sgwT": sgwT, "c_idf": c_idf, "c_oh8": c_oh8, "cpool": make_cpool_host(),
    }
    if y_full is not None:
        ret["y_in"] = np.ascontiguousarray(y_full[b][:, own_tok])
        ret["xr"] = np.ascontiguousarray(resid2_full[b][:, own_tok])
        wo = np.asarray(inputs["L1_attn_wo"], np.float32)
        ret["woT"] = tile_w(wo.T, oc16)
    return ret



def host_prep_m(inputs, core):
    b, hf = core // 2, core % 2
    c_idf, c_oh, c_m1, c_m2s = host_consts()
    x = np.asarray(inputs["L0_attn_hidden_states"][b])
    Wx = inputs["L0_attn_Wx"][:, hf * 2048:(hf + 1) * 2048]
    WB = inputs["L0_attn_WB"][:, hf * 512:(hf + 1) * 512]
    WC = inputs["L0_attn_WC"][:, hf * 512:(hf + 1) * 512]
    Wdt = inputs["L0_attn_Wdt"][:, hf * 32:(hf + 1) * 32]
    w_proj = tile_w(np.concatenate([Wx, WB, WC, Wdt], axis=1),
                    ogrid(3104, 5))
    w_z = tile_w(np.asarray(inputs["L0_attn_Wz"][:, hf * 2048:(hf + 1) * 2048],
                            np.float32), ogrid(2048, 4))
    cw = np.asarray(inputs["L0_attn_conv_w"])
    cb = np.asarray(inputs["L0_attn_conv_b"])
    idx = np.concatenate([
        np.arange(hf * 2048, hf * 2048 + 2048),
        np.arange(4096 + hf * 512, 4096 + hf * 512 + 512),
        np.arange(5120 + hf * 512, 5120 + hf * 512 + 512)])
    convw = np.ascontiguousarray(
        cw[idx].reshape(24, 128, 4).transpose(1, 0, 2).reshape(128, 96),
        dtype=np.float32)
    convb = np.ascontiguousarray(cb[idx].reshape(24, 128).T, dtype=np.float32)
    sl = slice(hf * 32, hf * 32 + 32)
    return {
        "xr": np.ascontiguousarray(x.T, dtype=np.float32),
        "w_proj": w_proj,
        "w_z": w_z,
        "convw": convw,
        "convb": convb,
        "dtb": np.ascontiguousarray(
            np.asarray(inputs["L0_attn_dt_bias"][sl], dtype=np.float32).reshape(32, 1)),
        "alog": np.ascontiguousarray(
            np.asarray(inputs["L0_attn_A_log"][sl], dtype=np.float32).reshape(32, 1)),
        "dvec": np.ascontiguousarray(
            np.asarray(inputs["L0_attn_D"][sl], dtype=np.float32).reshape(1, 32)),
        "c_idf": c_idf, "c_oh": c_oh, "c_m1": c_m1, "c_m2s": c_m2s,
        "cpool": make_cpool_host(),
    }


def _own_tok(hf):
    return np.concatenate([np.arange((2 * s + hf) * TILE, (2 * s + hf + 1) * TILE)
                           for s in range(4)])



RG_PAIRS = [[0, 1], [2, 3], [4, 5], [6, 7]]


def build_fused():
    nc = bass.Bass("TRN2", target_bir_lowering=False, detect_race_conditions=False)
    es = contextlib.ExitStack()
    ctx = Ctx(nc, es)
    with es:
        io_m = dram_io(nc, M_INPUTS, M_SHAPES, pfx="m_")
        f0_names = [k for k in F0_INPUTS if k not in ("g_in", "w_out")]
        io_f0 = dram_io(nc, f0_names, F0_SHAPES, pfx="f0_")
        io_p0 = dram_io(nc, ["w_out", "nw"], P0_SHAPES, pfx="p0_")
        io_a1 = dram_io(nc, A1_INPUTS + ["woT"], A1_SHAPES, pfx="a1_")
        f1_names = [k for k in F1_INPUTS if k != "woT"]
        io_f1 = dram_io(nc, f1_names, F1_SHAPES, pfx="f1_")
        cpool = nc.dram_tensor("cpool", [128, max(4, len(CONSTS))], F32,
                               kind="ExternalInput").ap()
        r_out = nc.dram_tensor("r_out", [D, 2048], F32,
                               kind="ExternalOutput").ap()
        g_loc = nc.dram_tensor("g_loc", [128, NTILE * 16 * TILE], BF16,
                               kind="Internal").ap()
        ssq_part = nc.dram_tensor("ssq_part", [2, 2048], F32,
                                  kind="Internal").ap()
        ssq_rs = nc.dram_tensor("ssq_rs", [1, 2048], F32, kind="Internal").ap()
        wop = nc.dram_tensor("wop", [4096, 2048], BF16, kind="Internal").ap()
        wop_rs = nc.dram_tensor("wop_rs", [2048, 2048], BF16,
                                kind="Internal").ap()
        r2loc = nc.dram_tensor("r2loc", [128, 4 * 16 * TILE], BF16,
                               kind="Internal").ap()
        r2_ag = nc.dram_tensor("r2_ag", [2 * 128, 4 * 16 * TILE], BF16,
                               kind="Internal").ap()
        wop2 = nc.dram_tensor("wop2", [4096, 2048], BF16, kind="Internal").ap()
        wop2_rs = nc.dram_tensor("wop2_rs", [2048, 2048], BF16,
                                 kind="Internal").ap()
        lnsc = nc.dram_tensor("lnsc", [128, NTILE * 16 * TILE], BF16,
                               kind="Internal").ap()

        cm = Common(ctx, cpool)
        ps = [ctx.psum(f"ps{i}", [128, 512], F32) for i in range(7)]
        psb = ctx.psum("psb", [128, 1024], BF16)
        cc = ctx.sem("cc")
        E = ctx.emit

        g_locv = g_loc.rearrange("p (t kc T) -> p t kc T", t=NTILE, kc=16)

        def g_store(t):
            return g_locv[:, t, :, :]

        with ctx.scope("m_"):
            iom = dict(io_m)
            _emit_phase_m(ctx, cm, ps, psb, iom, NTILE, "full", g_store=g_store)
        ctx.barrier_all()

        with ctx.scope("p0_"):
            iop = dict(io_p0)
            iop["ssq_part"] = ssq_part
            iop["wop"] = wop
            _emit_m2_partials(ctx, cm, ps, iop,
                              g_tile=lambda i: g_loc[:, i * 8192:(i + 1) * 8192])
        ctx.barrier_all()
        E("gpsimd", lambda e: e.collective_compute(
            "ReduceScatter", ALU.add, replica_groups=RG_PAIRS,
            ins=[ssq_part.opt()], outs=[ssq_rs.opt()]), inc=cc)
        E("gpsimd", lambda e: e.collective_compute(
            "ReduceScatter", ALU.add, replica_groups=RG_PAIRS,
            ins=[wop.opt()], outs=[wop_rs.opt()]), inc=cc)
        for eng in ENGS:
            ctx.wait(eng, [(cc, cc.n)])

        with ctx.scope("f0_"):
            io2 = dict(io_f0)
            io2["ssq_rs"] = ssq_rs
            io2["wop_rs"] = wop_rs
            io2["r_out"] = r2loc
            _emit_phase_m2f0(ctx, cm, ps, io2, "full", rs_mode=True,
                             r_tiled=True)
        ctx.barrier_all()
        E("gpsimd", lambda e: e.collective_compute(
            "AllGather", ALU.bypass, replica_groups=RG_PAIRS,
            ins=[r2loc.opt()], outs=[r2_ag.opt()]), inc=cc)
        for eng in ENGS:
            ctx.wait(eng, [(cc, cc.n)])

        r2agv = r2_ag.rearrange("(r p) (tt kc T) -> p r tt kc T",
                                p=128, tt=4, kc=16)

        def x_tile(t):
            return r2agv[:, t // 4, t % 4, :, :]

        with ctx.scope("a1_"):
            ioa = dict(io_a1)
            ioa["lnsc"] = lnsc
            ioa["wopart"] = wop2
            _emit_phase_a1(ctx, cm, ps, psb, ioa, "full", x_tile=x_tile,
                           wo_part=True)
        ctx.barrier_all()
        E("gpsimd", lambda e: e.collective_compute(
            "ReduceScatter", ALU.add, replica_groups=RG_PAIRS,
            ins=[wop2.opt()], outs=[wop2_rs.opt()]), inc=cc)
        for eng in ENGS:
            ctx.wait(eng, [(cc, cc.n)])

        with ctx.scope("f1_"):
            iof = dict(io_f1)
            iof["xr"] = r2loc
            iof["wop2_rs"] = wop2_rs
            iof["r_out"] = r_out
            _emit_phase_a2f1(ctx, cm, ps, iof, "full", rs_mode=True,
                             xr_tiled=True)
        finish_build(nc, ctx)
    return nc


def host_prep_fused(inputs, core):
    b, hf = core // 2, core % 2
    m = host_prep_m(inputs, core)
    f0 = host_prep_m2f0(inputs, core, None)
    p0 = host_prep_m2p(inputs, core)
    a1 = host_prep_a1(inputs, core, None)
    f1 = host_prep_a2f1(inputs, core, None, None)
    im = {}
    for k in M_INPUTS:
        im["m_" + k] = m[k]
    for k in F0_INPUTS:
        if k not in ("g_in", "w_out", "xr"):
            im["f0_" + k] = f0[k]
    im["p0_w_out"] = p0["w_out"]
    im["p0_nw"] = p0["nw"]
    for k in A1_INPUTS + ["woT"]:
        im["a1_" + k] = a1[k]
    for k in F1_INPUTS:
        if k != "woT":
            im["f1_" + k] = f1[k]
    x = np.asarray(inputs["L0_attn_hidden_states"][b])
    im["f0_xr"] = np.ascontiguousarray(x[hf * 2048:(hf + 1) * 2048].T,
                                       dtype=np.float32)
    im["cpool"] = make_cpool_host()
    return im


def kernel_fused(**inputs):
    inputs = {k: np.asarray(v) for k, v in inputs.items()}
    cores = list(range(8))
    nc = build_fused()
    res = run_bass_kernel_spmd(nc, [host_prep_fused(inputs, c) for c in cores],
                               cores)
    out = np.empty((4, T, D), np.float32)
    for b in range(4):
        out[b][0:2048] = np.asarray(res.results[2 * b]["r_out"]).T
        out[b][2048:4096] = np.asarray(res.results[2 * b + 1]["r_out"]).T
    return out


def kernel(**inputs):
    return kernel_fused(**inputs)


def kernel_unfused(**inputs):
    inputs = {k: np.asarray(v) for k, v in inputs.items()}
    cores = list(range(8))

    nc = build_phase_m()
    res = run_bass_kernel_spmd(nc, [host_prep_m(inputs, c) for c in cores], cores)
    g_full = [np.concatenate([np.asarray(res.results[2 * b]["g_out"]),
                              np.asarray(res.results[2 * b + 1]["g_out"])], axis=0)
              for b in range(4)]

    nc = build_phase_m2f0()
    res = run_bass_kernel_spmd(nc, [host_prep_m2f0(inputs, c, g_full)
                                    for c in cores], cores)
    r2 = []
    for b in range(4):
        r = np.empty((D, T), dtype=ml_dtypes.bfloat16)
        r[:, _own_tok(0)] = np.asarray(res.results[2 * b]["r_out"])
        r[:, _own_tok(1)] = np.asarray(res.results[2 * b + 1]["r_out"])
        r2.append(r)

    nc = build_phase_a1()
    res = run_bass_kernel_spmd(nc, [host_prep_a1(inputs, c, r2) for c in cores],
                               cores)
    y_full = [np.concatenate([np.asarray(res.results[2 * b]["y_out"]),
                              np.asarray(res.results[2 * b + 1]["y_out"])], axis=0)
              for b in range(4)]

    nc = build_phase_a2f1()
    res = run_bass_kernel_spmd(nc, [host_prep_a2f1(inputs, c, y_full, r2)
                                    for c in cores], cores)
    out = np.empty((4, T, D), np.float32)
    for b in range(4):
        out[b][_own_tok(0)] = np.asarray(res.results[2 * b]["r_out"]).T
        out[b][_own_tok(1)] = np.asarray(res.results[2 * b + 1]["r_out"]).T
    return out
